# version 01
# baseline (speedup 1.0000x reference)
import numpy as np
import ml_dtypes
import concourse.bass as bass
import concourse.mybir as mybir
from concourse.bass_utils import run_bass_kernel_spmd

F32 = mybir.dt.float32
BF16 = mybir.dt.bfloat16
I32 = mybir.dt.int32
AF = mybir.ActivationFunctionType
ALU = mybir.AluOpType
AX = mybir.AxisListType

NC = 8
T = 2048
D = 1024
BF = ml_dtypes.bfloat16


class Prog:
    ENGS = ("sync", "gpsimd", "tensor", "vector", "scalar")

    def __init__(self):
        import contextlib
        self.nc = bass.Bass("TRN2", target_bir_lowering=False)
        self.stack = contextlib.ExitStack()
        self.q = {e: [] for e in self.ENGS}
        self.sems = {}
        self.cnt = {}

    def din(self, name, shape, dtype):
        return self.nc.dram_tensor(name, list(shape), dtype, kind="ExternalInput").ap()

    def dout(self, name, shape, dtype):
        return self.nc.dram_tensor(name, list(shape), dtype, kind="ExternalOutput").ap()

    def dtmp(self, name, shape, dtype):
        return self.nc.dram_tensor(name, list(shape), dtype).ap()

    def sbuf(self, name, shape, dtype):
        return self.stack.enter_context(self.nc.sbuf_tensor(name, list(shape), dtype))

    def psum(self, name, shape, dtype=F32):
        return self.stack.enter_context(self.nc.psum_tensor(name, list(shape), dtype))

    def sem(self, name):
        if name not in self.sems:
            self.sems[name] = self.stack.enter_context(self.nc.semaphore(name))
            self.cnt[name] = 0
        return self.sems[name]

    def on(self, eng, fn, sem=None, inc=1):
        if sem is not None:
            s = self.sem(sem)
            self.cnt[sem] += inc
            self.q[eng].append(lambda e, fn=fn, s=s, inc=inc: fn(e).then_inc(s, inc))
            return self.cnt[sem]
        self.q[eng].append(fn)
        return None

    def dma(self, eng, out, in_, sem):
        return self.on(eng, lambda e, o=out, i=in_: e.dma_start(out=o, in_=i), sem, 16)

    def wait(self, eng, sem, val):
        s = self.sem(sem)
        self.q[eng].append(lambda e, s=s, v=val: e.wait_ge(s, v))

    def build(self):
        nc = self.nc
        qs = self.q
        with nc.Block() as block:
            @block.sync
            def _(e):
                for f in qs["sync"]:
                    f(e)
            @block.gpsimd
            def _(e):
                for f in qs["gpsimd"]:
                    f(e)
            @block.tensor
            def _(e):
                for f in qs["tensor"]:
                    f(e)
            @block.vector
            def _(e):
                for f in qs["vector"]:
                    f(e)
            @block.scalar
            def _(e):
                for f in qs["scalar"]:
                    f(e)
        self.stack.close()
        return nc



def fm(a):
    return np.ascontiguousarray(np.asarray(a).T)

def tile_vec(v, dtype=np.float32):
    return np.ascontiguousarray(np.asarray(v).reshape(-1, 128).T.astype(dtype))

def shard_tokens(x):
    return [np.asarray(x[c // 2, (c % 2) * T:(c % 2 + 1) * T, :]) for c in range(NC)]

_prog_cache = {}

def cached(key, builder, *args, **kw):
    if key not in _prog_cache:
        _prog_cache[key] = builder(*args, **kw)
    return _prog_cache[key]

def run(nc, in_maps, **kw):
    return run_bass_kernel_spmd(nc, in_maps, core_ids=list(range(NC)), **kw).results



def load_fm(p, eng, sb, dram, nchunks, ncols, sem, dcol0=0, dt_cols=None):
    for c in range(nchunks):
        p.dma(eng, sb[:, c * ncols:(c + 1) * ncols],
              dram[c * 128:(c + 1) * 128, dcol0:dcol0 + ncols], sem)
    return p.cnt[sem]

def chunk(sb, c, ncols):
    return sb[:, c * ncols:(c + 1) * ncols]


def emit_proj(p, *, x_sb, kchunks, mchunks, Tn, w_dram, out_cb,
              wq_eng="gpsimd", tag="", nbuf=2, kc0=0):
    NT = max(1, Tn // 512)
    nt = min(Tn, 512)
    wld, mm, ev = "wl" + tag, "mm" + tag, "ev" + tag
    wsb = [p.sbuf(f"w{tag}_{i}", [128, kchunks * 128], BF16) for i in range(nbuf)]
    ps = [p.psum(f"pp{tag}_{i}", [128, Tn], F32) for i in range(2)]
    for m in range(mchunks):
        buf = wsb[m % nbuf]
        if m >= nbuf:
            p.wait(wq_eng, mm, m - nbuf + 1)
        src = w_dram.rearrange("(kc p) m -> p kc m", p=128)[:, :, m * 128:(m + 1) * 128]
        p.dma(wq_eng, buf[:].rearrange("p (kc m) -> p kc m", m=128), src, wld)
        p.wait("tensor", wld, 16 * (m + 1))
        pt = ps[m % 2]
        if m >= 2:
            p.wait("tensor", ev, m - 1)
        for k in range(kchunks):
            for n in range(NT):
                sl = slice(n * nt, (n + 1) * nt)
                p.on("tensor",
                     lambda e, pt=pt, b=buf, k=k, sl=sl, st=(k == 0), sp=(k == kchunks - 1):
                     e.matmul(pt[:, sl], b[:, k * 128:(k + 1) * 128],
                              chunk(x_sb, kc0 + k, Tn)[:, sl], start=st, stop=sp),
                     sem=mm if (k == kchunks - 1 and n == NT - 1) else None)
        out_cb(m, pt, p.cnt[mm])
    return ev


def simple_evac(p, youts_dram, o_sb, Tn, act=None, b_sb=None, s_sb=None, tag="",
                out_rows=128):
    def out_cb(m, pt, mmv):
        ev, sto = "ev" + tag, "sto" + tag
        p.wait("scalar", "mm" + tag, mmv)
        if m >= 1:
            p.wait("scalar", sto, 16 * m)
        kw = {}
        if b_sb is not None:
            kw["bias"] = b_sb[:, m:m + 1]
        if s_sb is not None:
            kw["scale"] = s_sb[:, m:m + 1]
        p.on("scalar", lambda e, pt=pt, kw=kw: e.activation(
            o_sb[0:out_rows, :], pt[0:out_rows, :], act if act else AF.Copy, **kw),
            sem=ev)
        p.wait("sync", ev, p.cnt[ev])
        p.dma("sync", youts_dram[m * 128:(m + 1) * 128, :], o_sb[0:out_rows, :], sto)
    return out_cb



def build_ln(want_f32=False):
    p = Prog()
    x = p.din("x", [D, T], F32)
    xn_bf = p.dout("xn_bf", [D, T], BF16)
    xn_f = p.dout("xn_f", [D, T], F32) if want_f32 else None
    dbg = p.dout("dbg", [1, 2 * T], F32)
    KC = D // 128
    cst = p.din("cst", [128, 1], F32)
    on2d = p.din("on2d", [1, 128], F32)
    x_sb = p.sbuf("x_sb", [128, KC * T], F32)
    sq = p.sbuf("sq", [128, T], F32)
    ob = p.sbuf("ob", [128, T], BF16)
    of = p.sbuf("of", [128, T], F32)
    ones = p.sbuf("ones", [128, 1], F32)
    stat = p.sbuf("stat", [1, 2 * T], F32)
    t1 = p.sbuf("t1", [1, 2 * T], F32)
    tmp = p.sbuf("tmp", [128, T], F32)
    on2 = p.sbuf("on2", [1, 128], F32)
    ps_a = p.psum("ps_a", [128, T], F32)
    ps_b = p.psum("ps_b", [128, T], F32)
    ps_s = ps_a[0:1, :]
    ps_q = ps_b[0:1, :]
    ps_m = ps_a
    ps_r = ps_b

    p.dma("gpsimd", ones[:], cst, "ms")
    p.dma("gpsimd", on2[:], on2d, "ms")
    load_fm(p, "sync", x_sb, x, KC, T, "ld")
    p.wait("tensor", "ms", 32)
    p.wait("scalar", "ld", p.cnt["ld"])
    p.wait("tensor", "ld", p.cnt["ld"])
    for c in range(KC):
        if c > 0:
            p.wait("scalar", "sqm", c)
        p.on("scalar", lambda e, c=c: e.activation(sq[:], chunk(x_sb, c, T), AF.Square),
             sem="sqd")
        p.wait("tensor", "sqd", c + 1)
        for n in range(T // 512):
            sl = slice(n * 512, (n + 1) * 512)
            p.on("tensor", lambda e, c=c, sl=sl, st=(c == 0), sp=(c == KC - 1):
                 e.matmul(ps_s[:, sl], ones[:], chunk(x_sb, c, T)[:, sl], start=st, stop=sp),
                 sem="mms" if c == KC - 1 else None)
            p.on("tensor", lambda e, sl=sl, st=(c == 0), sp=(c == KC - 1):
                 e.matmul(ps_q[:, sl], ones[:], sq[:, sl], start=st, stop=sp),
                 sem="mms" if c == KC - 1 else ("sqm" if n == T // 512 - 1 else None))
    mean = stat[0:1, 0:T]
    rstd = stat[0:1, T:2 * T]
    va = t1[0:1, 0:T]
    vb = t1[0:1, T:2 * T]
    p.wait("vector", "mms", p.cnt["mms"])
    p.on("vector", lambda e: e.tensor_scalar(mean, ps_s, 1.0 / D, None, ALU.mult))
    p.on("vector", lambda e: e.tensor_scalar(va, ps_q, 1.0 / D, None, ALU.mult))
    p.on("vector", lambda e: e.tensor_tensor(vb, mean, mean, ALU.mult))
    p.on("vector", lambda e: e.tensor_tensor(va, va, vb, ALU.subtract))
    p.on("vector", lambda e: e.tensor_scalar(va, va, 1e-5, None, ALU.add))
    p.on("vector", lambda e: e.reciprocal(vb, va), sem="st1")
    p.wait("scalar", "st1", p.cnt["st1"])
    p.on("scalar", lambda e: e.activation(rstd, vb, AF.Sqrt), sem="st2")
    p.wait("tensor", "st2", 1)
    for n in range(T // 512):
        sl = slice(n * 512, (n + 1) * 512)
        p.on("tensor", lambda e, sl=sl: e.matmul(ps_m[:, sl], on2[:], mean[:, sl],
                                                 start=True, stop=True))
        p.on("tensor", lambda e, sl=sl: e.matmul(ps_r[:, sl], on2[:], rstd[:, sl],
                                                 start=True, stop=True), sem="bc")
    p.wait("vector", "bc", p.cnt["bc"])
    for c in range(KC):
        if c > 0:
            p.wait("vector", "sto", 16 * (2 * c if want_f32 else c))
        p.on("vector", lambda e, c=c: e.tensor_tensor(tmp[:], chunk(x_sb, c, T), ps_m[:], ALU.subtract))
        p.on("vector", lambda e: e.tensor_tensor(of[:], tmp[:], ps_r[:], ALU.mult))
        p.on("vector", lambda e: e.tensor_copy(ob[:], of[:]), sem="xn")
        p.wait("sync", "xn", p.cnt["xn"])
        p.dma("sync", xn_bf[c * 128:(c + 1) * 128, :], ob[:], "sto")
        if want_f32:
            p.dma("sync", xn_f[c * 128:(c + 1) * 128, :], of[:], "sto")
    p.wait("sync", "xn", 1)
    p.dma("sync", dbg, stat[0:1, :], "sto")
    p.wait("sync", "sto", p.cnt["sto"])
    return p.build()


ONES_COL = np.ones((128, 1), dtype=np.float32)
ONES_ROW = np.ones((1, 128), dtype=np.float32)

def run_ln(x_fm_shards, want_f32=False):
    nc = cached(("ln", want_f32), build_ln, want_f32)
    res = run(nc, [{"x": np.ascontiguousarray(s), "cst": ONES_COL, "on2d": ONES_ROW}
                   for s in x_fm_shards])
    if want_f32:
        return [r["xn_bf"] for r in res], [r["xn_f"] for r in res]
    return [r["xn_bf"] for r in res]



def emit_proj2(p, *, x_sb, kchunks, mchunks, Tn, w_dram, out_cb, ps,
               wq_eng="gpsimd", tag="", nbuf=2, kc0=0):
    NT = max(1, Tn // 512)
    nt = min(Tn, 512)
    mm, ev = "mm" + tag, "ev" + tag
    wsb = [p.sbuf(f"w{tag}_{i}", [128, kchunks * 128], BF16) for i in range(nbuf)]
    for m in range(mchunks):
        buf = wsb[m % nbuf]
        if m >= nbuf:
            p.wait(wq_eng, mm, m - nbuf + 1)
        src = w_dram.rearrange("(kc p) m -> p kc m", p=128)[:, :, m * 128:(m + 1) * 128]
        wld = f"wl{tag}_{m % nbuf}"
        p.dma(wq_eng, buf[:].rearrange("p (kc m) -> p kc m", m=128), src, wld)
        p.wait("tensor", wld, 16 * (m // nbuf + 1))
        pt = ps[m % 2]
        if m >= 2:
            p.wait("tensor", ev, m - 1)
        for k in range(kchunks):
            for n in range(NT):
                sl = slice(n * nt, (n + 1) * nt)
                p.on("tensor",
                     lambda e, pt=pt, b=buf, k=k, sl=sl, st=(k == 0), sp=(k == kchunks - 1):
                     e.matmul(pt[:, sl], b[:, k * 128:(k + 1) * 128],
                              chunk(x_sb, kc0 + k, Tn)[:, sl], start=st, stop=sp),
                     sem=mm if (k == kchunks - 1 and n == NT - 1) else None)
        out_cb(m, pt, p.cnt[mm])
    return ev


def build_projN(kdim, specs, Tn=T):
    p = Prog()
    KC = kdim // 128
    x = p.din("x", [kdim, Tn], BF16)
    x_sb = p.sbuf("x_sb", [128, KC * Tn], BF16)
    load_fm(p, "sync", x_sb, x, KC, Tn, "ld")
    ps = [p.psum("pspool0", [128, Tn], F32), p.psum("pspool1", [128, Tn], F32)]
    prev_ev = None
    any_fm = any(s.get("featmap") for s in specs)
    if any_fm:
        zf = p.sbuf("zf", [128, Tn], F32)
        tf = p.sbuf("tf", [128, Tn], F32)
        ef = p.sbuf("ef", [128, Tn], F32)
    for i, s in enumerate(specs):
        M = s["m"]
        MC = M // 128
        tag = f"p{i}"
        w = p.din(f"w{i}", [kdim, M], s.get("wdt", F32))
        y = p.dout(s["out"], [M, Tn], BF16)
        b_sb = None
        if s.get("bias"):
            b = p.din(f"b{i}", [128, MC], F32)
            b_sb = p.sbuf(f"b_sb{i}", [128, MC], F32)
            p.dma("sync", b_sb[:], b, "ld")
        s_sb = None
        if s.get("scale"):
            sc = p.din(f"s{i}", [128, MC], F32)
            s_sb = p.sbuf(f"s_sb{i}", [128, MC], F32)
            p.dma("sync", s_sb[:], sc, "ld")
        o_sb = p.sbuf(f"o_sb{i}", [128, Tn], BF16)
        ldv = p.cnt["ld"]
        p.wait("tensor", "ld", ldv)
        p.wait("scalar", "ld", ldv)
        if prev_ev is not None:
            p.wait("tensor", prev_ev, p.cnt[prev_ev])
            p.wait("scalar", prev_ev, p.cnt[prev_ev])

        def out_cb(m, pt, mmv, s=s, tag=tag, b_sb=b_sb, s_sb=s_sb, o_sb=o_sb, y=y):
            ev, sto = "ev" + tag, "sto" + tag
            p.wait("scalar", "mm" + tag, mmv)
            if m >= 1:
                p.wait("scalar", sto, 16 * m)
            kw = {}
            if b_sb is not None:
                kw["bias"] = b_sb[:, m:m + 1]
            if s_sb is not None:
                kw["scale"] = s_sb[:, m:m + 1]
            if s.get("featmap"):
                if m >= 1:
                    p.wait("scalar", ev, m)
                p.on("scalar", lambda e, pt=pt, kw=kw: e.activation(
                    zf[:], pt[:], AF.Identity if kw else AF.Copy, **kw), sem="fz" + tag)
                p.wait("vector", "fz" + tag, p.cnt["fz" + tag])
                if m >= 1:
                    p.wait("vector", sto, 16 * m)
                p.on("vector", lambda e: e.tensor_scalar(
                    tf[:], zf[:], 0.0, None, ALU.min), sem="fm" + tag)
                p.wait("scalar", "fm" + tag, p.cnt["fm" + tag])
                p.on("scalar", lambda e: e.activation(ef[:], tf[:], AF.Exp),
                     sem="fe" + tag)
                p.wait("vector", "fe" + tag, p.cnt["fe" + tag])
                p.on("vector", lambda e: e.scalar_tensor_tensor(
                    o_sb[:], zf[:], 0.0, ef[:], ALU.max, ALU.add), sem=ev)
            else:
                fn = s.get("act") or (AF.Identity if kw else AF.Copy)
                p.on("scalar", lambda e, pt=pt, kw=kw, fn=fn: e.activation(
                    o_sb[:], pt[:], fn, **kw), sem=ev)
            p.wait("sync", ev, p.cnt[ev])
            p.dma("sync", y[m * 128:(m + 1) * 128, :], o_sb[:], sto)

        emit_proj2(p, x_sb=x_sb, kchunks=KC, mchunks=MC, Tn=Tn, w_dram=w,
                   out_cb=out_cb, ps=ps, tag=tag)
        prev_ev = "ev" + tag
    for i in range(len(specs)):
        p.wait("sync", "sto" + f"p{i}", p.cnt["sto" + f"p{i}"])
    return p.build()


def build_gateup(kdim, M, act0=None, scale01=False, Tn=T, out_dt=BF16,
                 wdt0=F32, wdt1=F32, bias01=False):
    p = Prog()
    KC = kdim // 128
    MC = M // 128
    x = p.din("x", [kdim, Tn], BF16)
    w0 = p.din("w0", [kdim, M], wdt0)
    w1 = p.din("w1", [kdim, M], wdt1)
    y = p.dout("y", [M, Tn], out_dt)
    x_sb = p.sbuf("x_sb", [128, KC * Tn], BF16)
    g_sb = p.sbuf("g_sb", [128, MC * Tn], BF16)
    o_sb = p.sbuf("o_sb", [128, Tn], out_dt)
    u_sb = p.sbuf("u_sb", [128, Tn], F32)
    ps = [p.psum("pspool0", [128, Tn], F32), p.psum("pspool1", [128, Tn], F32)]
    load_fm(p, "sync", x_sb, x, KC, Tn, "ld")
    s_sb = [None, None]
    b_sb = [None, None]
    if scale01:
        for i in range(2):
            sc = p.din(f"s{i}", [128, MC], F32)
            s_sb[i] = p.sbuf(f"s_sb{i}", [128, MC], F32)
            p.dma("sync", s_sb[i][:], sc, "ld")
    if bias01:
        for i in range(2):
            b = p.din(f"b{i}", [128, MC], F32)
            b_sb[i] = p.sbuf(f"b_sb{i}", [128, MC], F32)
            p.dma("sync", b_sb[i][:], b, "ld")
    ldv = p.cnt["ld"]
    p.wait("tensor", "ld", ldv)
    p.wait("scalar", "ld", ldv)

    def cb_gate(m, pt, mmv):
        p.wait("scalar", "mmg", mmv)
        kw = {}
        if s_sb[0] is not None:
            kw["scale"] = s_sb[0][:, m:m + 1]
        if b_sb[0] is not None:
            kw["bias"] = b_sb[0][:, m:m + 1]
        p.on("scalar", lambda e, pt=pt, m=m, kw=kw: e.activation(
            chunk(g_sb, m, Tn), pt[:], act0 or (AF.Identity if kw else AF.Copy), **kw), sem="evg")

    emit_proj2(p, x_sb=x_sb, kchunks=KC, mchunks=MC, Tn=Tn, w_dram=w0,
               out_cb=cb_gate, ps=ps, tag="g")
    p.wait("tensor", "evg", p.cnt["evg"])

    def cb_up(m, pt, mmv):
        p.wait("scalar", "mmu", mmv)
        if m >= 1:
            p.wait("scalar", "stou", 16 * m)
        kw = {}
        if s_sb[1] is not None:
            kw["scale"] = s_sb[1][:, m:m + 1]
        if b_sb[1] is not None:
            kw["bias"] = b_sb[1][:, m:m + 1]
        p.on("scalar", lambda e, pt=pt, kw=kw: e.activation(
            u_sb[:], pt[:], AF.Identity if kw else AF.Copy, **kw), sem="evu0")
        p.wait("vector", "evu0", p.cnt["evu0"])
        if m >= 1:
            p.wait("vector", "stou", 16 * m)
        p.on("vector", lambda e, m=m: e.tensor_tensor(
            o_sb[:], u_sb[:], chunk(g_sb, m, Tn), ALU.mult), sem="evu")
        p.wait("sync", "evu", p.cnt["evu"])
        p.dma("sync", y[m * 128:(m + 1) * 128, :], o_sb[:], "stou")

    emit_proj2(p, x_sb=x_sb, kchunks=KC, mchunks=MC, Tn=Tn, w_dram=w1,
               out_cb=cb_up, ps=ps, tag="u")
    p.wait("sync", "stou", p.cnt["stou"])
    return p.build()


def build_mid(kdim, M, act=None, scale=False, Tn=T):
    p = Prog()
    KC = kdim // 128
    MC = M // 128
    x = p.din("x", [kdim, Tn], BF16)
    w = p.din("w", [kdim, M], F32)
    y = p.dout("y", [M, Tn], BF16)
    x_sb = p.sbuf("x_sb", [128, KC * Tn], BF16)
    o_sb = p.sbuf("o_sb", [128, Tn], BF16)
    ps = [p.psum("pspool0", [128, Tn], F32), p.psum("pspool1", [128, Tn], F32)]
    load_fm(p, "sync", x_sb, x, KC, Tn, "ld")
    s_sb = None
    if scale:
        sc = p.din("s", [128, MC], F32)
        s_sb = p.sbuf("s_sb", [128, MC], F32)
        p.dma("sync", s_sb[:], sc, "ld")
    ldv = p.cnt["ld"]
    p.wait("tensor", "ld", ldv)
    p.wait("scalar", "ld", ldv)

    def cb(m, pt, mmv):
        p.wait("scalar", "mm0", mmv)
        if m >= 1:
            p.wait("scalar", "sto", 16 * m)
        kw = {}
        if s_sb is not None:
            kw["scale"] = s_sb[:, m:m + 1]
        p.on("scalar", lambda e, pt=pt, kw=kw: e.activation(
            o_sb[:], pt[:], act or (AF.Identity if kw else AF.Copy), **kw), sem="ev0")
        p.wait("sync", "ev0", p.cnt["ev0"])
        p.dma("sync", y[m * 128:(m + 1) * 128, :], o_sb[:], "sto")

    emit_proj2(p, x_sb=x_sb, kchunks=KC, mchunks=MC, Tn=Tn, w_dram=w,
               out_cb=cb, ps=ps, tag="0")
    p.wait("sync", "sto", p.cnt["sto"])
    return p.build()


def build_combine(k1, k2, scale2=False, bias1=False, Tn=T, wdt1=F32, wdt2=F32):
    p = Prog()
    K1, K2, MC = k1 // 128, k2 // 128, D // 128
    x = p.din("x", [D, Tn], F32)
    a = p.din("a", [k1, Tn], BF16)
    h = p.din("h", [k2, Tn], BF16)
    w1 = p.din("w1", [k1, D], wdt1)
    w2 = p.din("w2", [k2, D], wdt2)
    xo = p.dout("xo", [D, Tn], F32)
    a_sb = p.sbuf("a_sb", [128, K1 * Tn], BF16)
    h_sb = p.sbuf("h_sb", [128, K2 * Tn], BF16)
    x_sb = p.sbuf("x_sb", [128, Tn], F32)
    e1 = p.sbuf("e1", [128, Tn], F32)
    o_sb = p.sbuf("o_sb", [128, Tn], F32)
    ps = [p.psum("pspool0", [128, Tn], F32), p.psum("pspool1", [128, Tn], F32)]
    load_fm(p, "sync", a_sb, a, K1, Tn, "ld")
    load_fm(p, "sync", h_sb, h, K2, Tn, "ld")
    b_sb = None
    if bias1:
        b = p.din("b1", [128, MC], F32)
        b_sb = p.sbuf("b_sb", [128, MC], F32)
        p.dma("sync", b_sb[:], b, "ld")
    s_sb = None
    if scale2:
        sc = p.din("s2", [128, MC], F32)
        s_sb = p.sbuf("s_sb", [128, MC], F32)
        p.dma("sync", s_sb[:], sc, "ld")
    ldv = p.cnt["ld"]
    p.wait("tensor", "ld", ldv)
    p.wait("scalar", "ld", ldv)

    wsb1 = [p.sbuf(f"w1_{i}", [128, K1 * 128], BF16) for i in range(2)]
    wsb2 = [p.sbuf(f"w2_{i}", [128, K2 * 128], BF16) for i in range(2)]
    NT = Tn // 512
    for m in range(MC):
        if m >= 2:
            p.wait("gpsimd", "mm1", m - 1)
        p.dma("gpsimd", wsb1[m % 2][:].rearrange("p (kc mm) -> p kc mm", mm=128),
              w1.rearrange("(kc p) mm -> p kc mm", p=128)[:, :, m * 128:(m + 1) * 128], f"wl1_{m % 2}")
        p.dma("gpsimd", wsb2[m % 2][:].rearrange("p (kc mm) -> p kc mm", mm=128),
              w2.rearrange("(kc p) mm -> p kc mm", p=128)[:, :, m * 128:(m + 1) * 128], f"wl2_{m % 2}")
        p.dma("sync", x_sb[:], x[m * 128:(m + 1) * 128, :], "ldx")
        p.wait("tensor", f"wl1_{m % 2}", 16 * (m // 2 + 1))
        p.wait("tensor", f"wl2_{m % 2}", 16 * (m // 2 + 1))
        if m >= 2:
            p.wait("tensor", "cmb", m - 1)
        pt = ps[m % 2]
        for k in range(K1):
            for n in range(NT):
                sl = slice(n * 512, (n + 1) * 512)
                p.on("tensor", lambda e, pt=pt, m=m, k=k, sl=sl, st=(k == 0), sp=(k == K1 - 1):
                     e.matmul(pt[:, sl], wsb1[m % 2][:, k * 128:(k + 1) * 128],
                              chunk(a_sb, k, Tn)[:, sl], start=st, stop=sp),
                     sem="mma" if (k == K1 - 1 and n == NT - 1) else None)
        p.wait("scalar", "mma", p.cnt["mma"])
        if m >= 1:
            p.wait("scalar", "cmb", p.cnt["cmb"])
        kw = {"bias": b_sb[:, m:m + 1]} if b_sb is not None else {}
        p.on("scalar", lambda e, pt=pt, kw=kw: e.activation(
            e1[:], pt[:], AF.Identity if kw else AF.Copy, **kw), sem="ev1a")
        p.wait("tensor", "ev1a", p.cnt["ev1a"])
        for k in range(K2):
            for n in range(NT):
                sl = slice(n * 512, (n + 1) * 512)
                p.on("tensor", lambda e, pt=pt, m=m, k=k, sl=sl, st=(k == 0), sp=(k == K2 - 1):
                     e.matmul(pt[:, sl], wsb2[m % 2][:, k * 128:(k + 1) * 128],
                              chunk(h_sb, k, Tn)[:, sl], start=st, stop=sp),
                     sem="mm1" if (k == K2 - 1 and n == NT - 1) else None)
        p.wait("vector", "mm1", p.cnt["mm1"])
        p.wait("vector", "ldx", 16 * (m + 1))
        if m >= 1:
            p.wait("vector", "sto", 16 * m)
        if s_sb is not None:
            p.on("vector", lambda e, pt=pt, m=m: e.scalar_tensor_tensor(
                o_sb[:], pt[:], s_sb[:, m:m + 1], e1[:], ALU.mult, ALU.add), sem="cmb0")
        else:
            p.on("vector", lambda e, pt=pt: e.tensor_tensor(o_sb[:], pt[:], e1[:], ALU.add),
                 sem="cmb0")
        p.wait("vector", "cmb0", p.cnt["cmb0"])
        p.on("vector", lambda e: e.tensor_tensor(o_sb[:], o_sb[:], x_sb[:], ALU.add),
             sem="cmb")
        p.wait("sync", "cmb", p.cnt["cmb"])
        p.dma("sync", xo[m * 128:(m + 1) * 128, :], o_sb[:], "sto")
        p.wait("sync", "cmb", p.cnt["cmb"])
    p.wait("sync", "sto", p.cnt["sto"])
    return p.build()



def build_attn(scale, l0mix=False, S=4096, Tn=T, H=16):
    p = Prog()
    KC = D // 128
    SC = S // 128
    NT = Tn // 512
    NKB = SC // 4
    qd = p.din("q", [D, Tn], BF16)
    kf = p.din("kf", [D, S], BF16)
    vt = p.din("vt", [H, S * 65], BF16)
    on2d = p.din("on2d", [1, 128], F32)
    hseld = p.din("hsel", [128, 2], F32)
    ctx = p.dout("ctx", [D, Tn], BF16)

    q_sb = p.sbuf("q_sb", [128, KC * Tn], BF16)
    k_sb = p.sbuf("k_sb", [128, KC * S], BF16)
    v_sb = [p.sbuf(f"v_sb{i}", [128, SC * 65], BF16) for i in range(2)]
    ctx_sb = p.sbuf("ctx_sb", [128, KC * Tn], BF16)
    p_sb = p.sbuf("p_sb", [128, 2048], BF16)
    sqs = p.sbuf("sqs", [128, S], F32)
    on2 = p.sbuf("on2", [1, 128], F32)
    hsel = p.sbuf("hsel_sb", [128, 2], F32)
    nrow = p.sbuf("nrow", [1, 64], F32)
    nb_all = p.sbuf("nb_all", [128, 16], F32)
    c1 = p.sbuf("c1", [64, 512], F32)
    rrow = p.sbuf("rrow", [1, 512], F32)
    tred = p.sbuf("tred", [1, 2], F32)

    ps_s = p.psum("ps_s", [128, 2048], F32)
    ps_av = p.psum("ps_av", [65, 512], F32)
    ps_bc = p.psum("ps_bc", [128, 512], F32)
    ps_n = p.psum("ps_n", [1, 1024], F32)

    p.dma("sync", on2[:], on2d, "ld")
    p.dma("sync", hsel[:], hseld, "ld")
    load_fm(p, "sync", q_sb, qd, KC, Tn, "ld")
    load_fm(p, "sync", k_sb, kf, KC, S, "ld")
    ldv = p.cnt["ld"]

    if l0mix:
        qw4 = p.din("qw", [1, 4], F32)
        emixd = p.din("emix", [128, 4 * 128], F32)
        qw_sb = p.sbuf("qw_sb", [1, 8], F32)
        emix_sb = p.sbuf("emix_sb", [128, 4 * 128], F32)
        wb4 = p.sbuf("wb4", [128, 4], F32)
        mixf = p.sbuf("mixf", [128, 128], F32)
        mixm = p.sbuf("mixm", [128, 128], BF16)
        qe2 = p.sbuf("qe2", [128, 512], BF16)
        p.dma("sync", qw_sb[0:1, 0:4], qw4, "ld")
        p.dma("sync", emix_sb[:], emixd, "ld")
        ldv = p.cnt["ld"]
        p.wait("vector", "ld", ldv)
        p.on("vector", lambda e: e.tensor_reduce(qw_sb[0:1, 4:5], qw_sb[0:1, 0:4], AX.X, ALU.max), sem="smx")
        p.wait("vector", "smx", 1)
        p.on("vector", lambda e: e.tensor_tensor(qw_sb[0:1, 0:4], qw_sb[0:1, 0:4],
                                                 qw_sb[0:1, 4:5].broadcast_to((1, 4)), ALU.subtract),
             sem="smx")
        p.wait("scalar", "smx", 2)
        p.on("scalar", lambda e: e.activation(qw_sb[0:1, 0:4], qw_sb[0:1, 0:4], AF.Exp), sem="smx")
        p.wait("vector", "smx", 3)
        p.on("vector", lambda e: e.tensor_reduce(qw_sb[0:1, 4:5], qw_sb[0:1, 0:4], AX.X, ALU.add), sem="smx")
        p.wait("vector", "smx", 4)
        p.on("vector", lambda e: e.reciprocal(qw_sb[0:1, 5:6], qw_sb[0:1, 4:5]), sem="smx")
        p.wait("vector", "smx", 5)
        p.on("vector", lambda e: e.tensor_tensor(qw_sb[0:1, 0:4], qw_sb[0:1, 0:4],
                                                 qw_sb[0:1, 5:6].broadcast_to((1, 4)), ALU.mult),
             sem="smx")
        p.wait("tensor", "smx", 6)
        p.on("tensor", lambda e: e.matmul(ps_bc[:, 0:4], on2[:], qw_sb[0:1, 0:4],
                                          start=True, stop=True), sem="wmm")
        p.wait("vector", "wmm", 1)
        p.on("vector", lambda e: e.tensor_copy(wb4[:], ps_bc[:, 0:4]), sem="mxs")
        p.wait("vector", "mxs", 1)
        p.on("vector", lambda e: e.tensor_scalar(mixf[:], emix_sb[:, 0:128], wb4[:, 0:1], None, ALU.mult), sem="mxs")
        for i in range(1, 4):
            p.wait("vector", "mxs", i + 1)
            p.on("vector", lambda e, i=i: e.scalar_tensor_tensor(
                mixf[:], emix_sb[:, i * 128:(i + 1) * 128], wb4[:, i:i + 1], mixf[:],
                ALU.mult, ALU.add), sem="mxs")
        p.wait("vector", "mxs", 5)
        p.on("vector", lambda e: e.tensor_copy(mixm[:], mixf[:]), sem="mixr")
        p.wait("tensor", "mixr", 1)
        p.wait("tensor", "ld", ldv)
        for c in range(KC):
            for n in range(NT):
                p.on("tensor", lambda e, c=c, n=n: e.matmul(
                    ps_s[:, 0:512], mixm[:], chunk(q_sb, c, Tn)[:, n * 512:(n + 1) * 512],
                    start=True, stop=True), sem="mxm")
                p.wait("scalar", "mxm", p.cnt["mxm"])
                p.on("scalar", lambda e: e.activation(qe2[:], ps_s[:, 0:512], AF.Copy),
                     sem="mxe")
                p.wait("vector", "mxe", p.cnt["mxe"])
                p.on("vector", lambda e, c=c, n=n: e.tensor_copy(
                    chunk(q_sb, c, Tn)[:, n * 512:(n + 1) * 512], qe2[:]), sem="mxc")
                p.wait("tensor", "mxc", p.cnt["mxc"])
    p.wait("scalar", "ld", ldv)
    git = 0
    for (src_sb, ncols, off, nn) in ((q_sb, Tn, 0, "q"), (k_sb, S, 16, "k")):
        CH = ncols // 512
        for c in range(KC):
            if l0mix and src_sb is q_sb:
                p.wait("scalar", "mxc", p.cnt["mxc"])
            if p.cnt.get("nmm", 0) > 0:
                p.wait("scalar", "nmm", p.cnt["nmm"])
            p.on("scalar", lambda e, c=c, src_sb=src_sb, ncols=ncols:
                 e.activation(sqs[:, 0:ncols], chunk(src_sb, c, ncols), AF.Square),
                 sem="nsq")
            p.wait("tensor", "nsq", p.cnt["nsq"])
            for j in range(2):
                for n in range(CH):
                    slot = git % 2
                    if git >= 2:
                        p.wait("tensor", "nrd", git - 1)
                    p.on("tensor", lambda e, j=j, n=n, slot=slot: e.matmul(
                        ps_n[0:1, slot * 512:slot * 512 + 512], hsel[:, j:j + 1],
                        sqs[:, n * 512:(n + 1) * 512], start=True, stop=True), sem="nmm")
                    p.wait("vector", "nmm", p.cnt["nmm"])
                    tgt = 2 * c + j + off
                    if n == 0:
                        p.on("vector", lambda e, slot=slot, tgt=tgt: e.tensor_reduce(
                            nrow[0:1, tgt:tgt + 1], ps_n[0:1, slot * 512:slot * 512 + 512],
                            AX.X, ALU.max), sem="nrd")
                    else:
                        p.on("vector", lambda e, slot=slot: e.tensor_reduce(
                            tred[0:1, 0:1], ps_n[0:1, slot * 512:slot * 512 + 512],
                            AX.X, ALU.max), sem="nrd")
                        p.wait("vector", "nrd", p.cnt["nrd"])
                        p.on("vector", lambda e, tgt=tgt: e.tensor_tensor(
                            nrow[0:1, tgt:tgt + 1], nrow[0:1, tgt:tgt + 1],
                            tred[0:1, 0:1], ALU.max))
                    git += 1
    p.wait("vector", "nrd", p.cnt["nrd"])
    p.on("vector", lambda e: e.tensor_tensor(nrow[0:1, 32:48], nrow[0:1, 0:16],
                                             nrow[0:1, 16:32], ALU.mult), sem="nfin")
    p.wait("scalar", "nfin", 1)
    p.on("scalar", lambda e: e.activation(nrow[0:1, 48:64], nrow[0:1, 32:48], AF.Sqrt),
         sem="nfin")
    p.wait("vector", "nfin", 2)
    p.on("vector", lambda e: e.tensor_scalar(nrow[0:1, 48:64], nrow[0:1, 48:64],
                                             -scale, None, ALU.mult), sem="nfin")
    p.wait("tensor", "nfin", 3)
    p.on("tensor", lambda e: e.matmul(ps_bc[:, 16:32], on2[:], nrow[0:1, 48:64],
                                      start=True, stop=True), sem="nbm")
    p.wait("vector", "nbm", 1)
    p.on("vector", lambda e: e.tensor_copy(nb_all[:], ps_bc[:, 16:32]), sem="nbc")
    p.wait("scalar", "nbc", 1)
    p.wait("tensor", "nbc", 1)

    for h in range(2):
        p.dma("gpsimd", v_sb[h][:].rearrange("p (kc m) -> p kc m", m=65),
              vt[h:h + 1, :].rearrange("one (kc pp m) -> (one pp) kc m", pp=128, m=65), f"vld{h % 2}")

    smm = sxp = avm = ctm = 0
    for h in range(H):
        kc = h // 2
        base = (h % 2) * 64
        vb = v_sb[h % 2]
        p.wait("tensor", f"vld{h % 2}", 16 * (h // 2 + 1))
        for qt in range(NT):
            qsl = slice(qt * 512, (qt + 1) * 512)
            for kb in range(NKB):
                if smm + sxp + 1 > 0 and (kb > 0 or qt > 0 or h > 0):
                    p.wait("tensor", "sxp", sxp)
                for j in range(4):
                    kchunk = kb * 4 + j
                    p.on("tensor", lambda e, kc=kc, base=base, kchunk=kchunk, j=j, qsl=qsl:
                         e.matmul(ps_s[:, j * 512:(j + 1) * 512],
                                  chunk(k_sb, kc, S)[base:base + 64, kchunk * 128:(kchunk + 1) * 128],
                                  chunk(q_sb, kc, Tn)[base:base + 64, qsl],
                                  start=True, stop=True),
                         sem="smm" if j == 3 else None)
                smm = p.cnt["smm"]
                p.wait("scalar", "smm", smm)
                if avm > 0:
                    p.wait("scalar", "avm", avm)
                p.on("scalar", lambda e, h=h: e.activation(
                    p_sb[:], ps_s[:], AF.Exp, bias=nb_all[:, h:h + 1], scale=scale),
                    sem="sxp")
                sxp = p.cnt["sxp"]
                p.wait("tensor", "sxp", sxp)
                if kb == 0 and ctm > 0:
                    p.wait("tensor", "ctm", ctm)
                for j in range(4):
                    kchunk = kb * 4 + j
                    p.on("tensor", lambda e, vb=vb, kchunk=kchunk, j=j,
                         st=(kb == 0 and j == 0), sp=(kb == NKB - 1 and j == 3):
                         e.matmul(ps_av[:], vb[:, kchunk * 65:(kchunk + 1) * 65],
                                  p_sb[:, j * 512:(j + 1) * 512], start=st, stop=sp),
                         sem="avm" if j == 3 else None)
                avm = p.cnt["avm"]
            p.wait("vector", "avm", avm)
            p.on("vector", lambda e: e.reciprocal(rrow[:], ps_av[64:65, :]), sem="rrd")
            p.wait("tensor", "rrd", p.cnt["rrd"])
            if ctm > 0:
                p.wait("tensor", "ctm", ctm)
            p.on("tensor", lambda e: e.matmul(ps_bc[0:64, 0:512], on2[0:1, 0:64], rrow[:],
                                              start=True, stop=True), sem="bcm")
            p.wait("scalar", "avm", avm)
            if ctm > 0:
                p.wait("scalar", "ctm", ctm)
            p.on("scalar", lambda e: e.activation(c1[:], ps_av[0:64, :], AF.Copy), sem="c1e")
            p.wait("vector", "bcm", p.cnt["bcm"])
            p.wait("vector", "c1e", p.cnt["c1e"])
            p.on("vector", lambda e, kc=kc, base=base, qsl=qsl: e.tensor_tensor(
                chunk(ctx_sb, kc, Tn)[base:base + 64, qsl], c1[:], ps_bc[0:64, 0:512],
                ALU.mult), sem="ctm")
            ctm = p.cnt["ctm"]
        p.on("tensor", lambda e: e.nop(), sem="hdn")
        if h + 2 < H:
            p.wait("gpsimd", "hdn", h + 1)
            p.dma("gpsimd", v_sb[(h + 2) % 2][:].rearrange("p (kc m) -> p kc m", m=65),
                  vt[h + 2:h + 3, :].rearrange("one (kc pp m) -> (one pp) kc m", pp=128, m=65), f"vld{h % 2}")
    p.wait("sync", "ctm", ctm)
    for c in range(KC):
        p.dma("sync", ctx[c * 128:(c + 1) * 128, :], chunk(ctx_sb, c, Tn), "sto")
    p.wait("sync", "sto", p.cnt["sto"])
    return p.build()



def _mk_emix():
    E = np.zeros((4, 128, 128), dtype=np.float32)
    for blk in range(2):
        o = 64 * blk
        for i in range(32):
            E[0, o + i, o + i] = 1.0
            E[1, o + i, o + 32 + i] = 1.0
            E[2, o + 32 + i, o + i] = 1.0
            E[3, o + 32 + i, o + 32 + i] = 1.0
    return np.ascontiguousarray(E.transpose(1, 0, 2).reshape(128, 4 * 128))

EMIX = _mk_emix()

HSEL = np.zeros((128, 2), dtype=np.float32)
HSEL[0:64, 0] = 1.0
HSEL[64:128, 1] = 1.0

def make_vt(v_fm_pair):
    S = v_fm_pair.shape[1]
    vt = np.empty((16, S, 65), dtype=BF)
    vtm = np.ascontiguousarray(v_fm_pair.T)
    for h in range(16):
        vt[h, :, 0:64] = vtm[:, 64 * h:64 * h + 64]
    vt[:, :, 64] = np.float32(1.0)
    return vt.reshape(16, S * 65)

def pair_cat(shards, c):
    b = (c // 2) * 2
    return np.concatenate([np.asarray(shards[b], dtype=BF), np.asarray(shards[b + 1], dtype=BF)], axis=1)

def run_layer2(x_fm_shards, inp):
    xn = run_ln(x_fm_shards)
    nc = cached("l2qkv", build_projN, D,
                [{"out": "q", "m": D, "bias": True},
                 {"out": "k", "m": D, "bias": True},
                 {"out": "v", "m": D, "bias": True}])
    wq = fm(inp["L2_attn_Wq"]).astype(np.float32)
    wk = fm(inp["L2_attn_Wk"]).astype(np.float32)
    wv = fm(inp["L2_attn_Wv"]).astype(np.float32)
    bq, bk, bv = (tile_vec(inp["L2_attn_bq"]), tile_vec(inp["L2_attn_bk"]),
                  tile_vec(inp["L2_attn_bv"]))
    res = run(nc, [{"x": np.asarray(xn[c], dtype=BF), "w0": wq, "w1": wk, "w2": wv,
                    "b0": bq, "b1": bk, "b2": bv} for c in range(NC)])
    qs = [r["q"] for r in res]
    ks = [r["k"] for r in res]
    vs = [r["v"] for r in res]
    nc = cached(("attn", 0.125, False), build_attn, 0.125, False)
    ims = []
    for c in range(NC):
        kful = pair_cat(ks, c)
        vt = make_vt(pair_cat(vs, c).astype(np.float32))
        ims.append({"q": qs[c], "kf": np.ascontiguousarray(kful), "vt": vt,
                    "on2d": ONES_ROW, "hsel": HSEL})
    res = run(nc, ims)
    ctxs = [r["ctx"] for r in res]
    nc = cached(("gateup", D, 4096, "silu", True), build_gateup, D, 4096, AF.Silu, True)
    w1 = fm(inp["L2_ffn_w1_w"]).astype(np.float32)
    w2 = fm(inp["L2_ffn_w2_w"]).astype(np.float32)
    s1, s2 = tile_vec(inp["L2_ffn_w1_s"]), tile_vec(inp["L2_ffn_w2_s"])
    res = run(nc, [{"x": np.asarray(xn[c], dtype=BF), "w0": w1, "w1": w2,
                    "s0": s1, "s1": s2} for c in range(NC)])
    hs = [r["y"] for r in res]
    nc = cached(("mid", 4096, 4096, "gelu", True), build_mid, 4096, 4096, AF.Gelu, True)
    h1 = fm(inp["L2_ffn_h1_w"]).astype(np.float32)
    s1h = tile_vec(inp["L2_ffn_h1_s"])
    res = run(nc, [{"x": hs[c], "w": h1, "s": s1h} for c in range(NC)])
    hs = [r["y"] for r in res]
    h2 = fm(inp["L2_ffn_h2_w"]).astype(np.float32)
    s2h = tile_vec(inp["L2_ffn_h2_s"])
    res = run(nc, [{"x": hs[c], "w": h2, "s": s2h} for c in range(NC)])
    hs = [r["y"] for r in res]
    nc = cached(("combine", D, 4096, True, True), build_combine, D, 4096, True, True)
    wo = fm(inp["L2_attn_Wo"]).astype(np.float32)
    w3 = fm(inp["L2_ffn_w3_w"]).astype(np.float32)
    bo = tile_vec(inp["L2_attn_bo"])
    s3 = tile_vec(inp["L2_ffn_w3_s"])
    res = run(nc, [{"x": np.ascontiguousarray(x_fm_shards[c], dtype=np.float32),
                    "a": ctxs[c], "h": hs[c], "w1": wo, "w2": w3,
                    "b1": bo, "s2": s3} for c in range(NC)])
    return [r["xo"] for r in res]



def build_quant(R, C, in_dt=F32):
    p = Prog()
    RC = R // 128
    G = C // 16
    xin = p.din("xi", [R, C], in_dt)
    yo = p.dout("yo", [R, C], BF16)
    xb = p.sbuf("xb", [128, C], F32)
    mx = p.sbuf("mx", [128, G], F32)
    inv = p.sbuf("inv", [128, G], F32)
    pp = p.sbuf("pp", [128, C], F32)
    bb = p.sbuf("bb", [128, C], F32)
    cc = p.sbuf("cc", [128, C], F32)
    sg = p.sbuf("sg", [128, C], F32)
    ob = p.sbuf("ob", [128, C], BF16)
    MID = [0.25, 0.75, 1.25, 1.75, 2.5, 3.5, 5.0]
    xb_free = [0, 0]
    sg_free = [0]
    bb_free = [0]
    for r in range(RC):
        if r >= 1:
            p.wait("gpsimd", "sgn", xb_free[1])
            p.wait("gpsimd", "v", xb_free[0])
        p.dma("gpsimd", xb[:], xin[r * 128:(r + 1) * 128, :], f"ld{r % 2}")
        ldv = 16 * (r // 2 + 1)
        for eng in ("vector", "scalar"):
            p.wait(eng, f"ld{r % 2}", ldv)
        if r >= 1:
            p.wait("vector", "sto", 16 * r)
        g3 = xb[:].rearrange("p (g k) -> p g k", k=16)
        p.on("vector", lambda e, g3=g3: e.tensor_reduce(mx[:], g3, AX.X, ALU.max,
                                                        apply_absolute_value=True), sem="v")
        xb_free[0] = p.cnt["v"]
        p.wait("vector", "v", p.cnt["v"])
        p.on("vector", lambda e: e.tensor_scalar(mx[:], mx[:], 1.0 / 6.0, 1e-12,
                                                 ALU.mult, op1=ALU.max), sem="v")
        p.wait("vector", "v", p.cnt["v"])
        p.on("vector", lambda e: e.reciprocal(inv[:], mx[:]), sem="v")
        p.wait("vector", "v", p.cnt["v"])
        ib = inv[:].rearrange("p g -> p g ()").broadcast_to((128, G, 16))
        p3 = pp[:].rearrange("p (g k) -> p g k", k=16)
        b3 = bb[:].rearrange("p (g k) -> p g k", k=16)
        if r >= 1:
            p.wait("scalar", "v", bb_free[0])
        p.on("scalar", lambda e: e.activation(bb[:], xb[:], AF.Abs), sem="ab")
        p.wait("vector", "ab", p.cnt["ab"])
        p.on("vector", lambda e, ib=ib, p3=p3, b3=b3: e.tensor_tensor(
            p3, b3, ib, ALU.mult), sem="v")
        p.wait("vector", "v", p.cnt["v"])
        p.on("vector", lambda e: e.tensor_scalar(bb[:], pp[:], 5.0, 2.0, ALU.is_gt, op1=ALU.mult), sem="v")
        p.wait("vector", "v", p.cnt["v"])
        for thr in (3.5, 2.5):
            p.on("vector", lambda e, thr=thr: e.scalar_tensor_tensor(
                bb[:], pp[:], thr, bb[:], ALU.is_gt, ALU.add), sem="v")
            p.wait("vector", "v", p.cnt["v"])
        p.on("vector", lambda e: e.tensor_scalar(cc[:], pp[:], 0.25, None, ALU.is_gt), sem="v")
        p.wait("vector", "v", p.cnt["v"])
        for thr in (0.75, 1.25, 1.75):
            p.on("vector", lambda e, thr=thr: e.scalar_tensor_tensor(
                cc[:], pp[:], thr, cc[:], ALU.is_gt, ALU.add), sem="v")
            p.wait("vector", "v", p.cnt["v"])
        p.on("vector", lambda e: e.scalar_tensor_tensor(cc[:], cc[:], 0.5, bb[:],
                                                        ALU.mult, ALU.add), sem="v")
        bb_free[0] = p.cnt["v"]
        if r >= 1:
            p.wait("scalar", "v", sg_free[0])
        p.on("scalar", lambda e: e.activation(sg[:], xb[:], AF.Sign), sem="sgn")
        xb_free[1] = p.cnt["sgn"]
        p.wait("vector", "sgn", p.cnt["sgn"])
        p.wait("vector", "v", p.cnt["v"])
        p.on("vector", lambda e: e.tensor_tensor(cc[:], cc[:], sg[:], ALU.mult), sem="v")
        sg_free[0] = p.cnt["v"]
        p.wait("vector", "v", p.cnt["v"])
        mb = mx[:].rearrange("p g -> p g ()").broadcast_to((128, G, 16))
        c3 = cc[:].rearrange("p (g k) -> p g k", k=16)
        o3 = ob[:].rearrange("p (g k) -> p g k", k=16)
        p.on("vector", lambda e, mb=mb, c3=c3, o3=o3: e.tensor_tensor(o3, c3, mb, ALU.mult), sem="vq")
        p.wait("sync", "vq", p.cnt["vq"])
        p.dma("sync", yo[r * 128:(r + 1) * 128, :], ob[:], "sto")
    p.wait("sync", "sto", p.cnt["sto"])
    return p.build()



def tm(a):
    return np.ascontiguousarray(np.asarray(a).T)

def run_quant(arr_shards, R, C):
    nc = cached(("quant", R, C), build_quant, R, C)
    res = run(nc, [{"xi": np.ascontiguousarray(s, dtype=np.float32)} for s in arr_shards])
    return [r["yo"] for r in res]

def run_layer0(x_fm_shards, inp):
    xn, xn_f = run_ln(x_fm_shards, want_f32=True)
    nc = cached(("l0qkv",), build_projN, D,
                [{"out": "q", "m": D}, {"out": "k", "m": D}, {"out": "v", "m": D}])
    wq = np.ascontiguousarray(inp["L0_attn_Wq"], dtype=np.float32)
    wk = np.ascontiguousarray(inp["L0_attn_Wk"], dtype=np.float32)
    wv = np.ascontiguousarray(inp["L0_attn_Wv"], dtype=np.float32)
    res = run(nc, [{"x": np.asarray(xn[c], dtype=BF), "w0": wq, "w1": wk, "w2": wv}
                   for c in range(NC)])
    qs, ks, vs = [r["q"] for r in res], [r["k"] for r in res], [r["v"] for r in res]
    nc = cached(("attn", "l0"), build_attn, 1.0 / np.sqrt(32.0), True)
    qw = np.ascontiguousarray(inp["L0_attn_quadrant_weights"].reshape(1, 4), dtype=np.float32)
    ims = []
    for c in range(NC):
        kful = pair_cat(ks, c)
        vt = make_vt(pair_cat(vs, c).astype(np.float32))
        ims.append({"q": qs[c], "kf": np.ascontiguousarray(kful), "vt": vt,
                    "on2d": ONES_ROW, "hsel": HSEL, "qw": qw, "emix": EMIX})
    ctxs = [r["ctx"] for r in run(nc, ims)]
    wg_q = run_quant([inp["L0_ffn_w_gate"]] * NC, 2816, D)[0]
    wu_q = run_quant([inp["L0_ffn_w_up"]] * NC, 2816, D)[0]
    wd_q = run_quant([inp["L0_ffn_w_down"]] * NC, D, 2816)[0]
    xq_tm = run_quant([tm(xn_f[c]) for c in range(NC)], T, D)
    xq = [fm(s) for s in xq_tm]
    nc = cached(("gateup", D, 2816, "silu", False, "bf"), build_gateup, D, 2816,
                AF.Silu, False, T, F32, BF16, BF16)
    wg_l = fm(wg_q)
    wu_l = fm(wu_q)
    res = run(nc, [{"x": np.asarray(xq[c], dtype=BF), "w0": wg_l, "w1": wu_l}
                   for c in range(NC)])
    hs = [r["y"] for r in res]
    hq_tm = run_quant([tm(h) for h in hs], T, 2816)
    hq = [fm(s) for s in hq_tm]
    nc = cached(("combine", D, 2816, False, False, "bf2"), build_combine, D, 2816,
                False, False, T, F32, BF16)
    wo = np.ascontiguousarray(inp["L0_attn_Wo"], dtype=np.float32)
    wd_l = fm(wd_q)
    res = run(nc, [{"x": np.ascontiguousarray(x_fm_shards[c], dtype=np.float32),
                    "a": ctxs[c], "h": np.asarray(hq[c], dtype=BF),
                    "w1": wo, "w2": wd_l} for c in range(NC)])
    return [r["xo"] for r in res]



def build_l1conv():
    p = Prog()
    S = 4096
    up = p.din("up", [1536, S], BF16)
    tw = p.din("tw", [128, 36], F32)
    tb = p.din("tb", [128, 12], F32)
    zf = p.din("zf", [3, S], F32)
    w1 = p.din("w1", [3, 64], F32)
    w2 = p.din("w2", [64, 64], F32)
    w3 = p.din("w3", [64, 64], F32)
    b123 = p.din("b123", [64, 3], F32)
    freqc = p.din("freqc", [64, 2], F32)
    wo = p.din("wo", [64, 512], F32)
    negd = p.din("negd", [128, 4], F32)
    trow = p.din("trow", [1, S], F32)
    on2d = p.din("on2d", [1, 128], F32)
    x0o = p.dout("x0", [512, S], BF16)
    vxo = p.dout("vx", [512, S], BF16)
    kto = p.dout("kt", [512, S], BF16)

    up_sb = p.sbuf("up_sb", [128, 2 * (S + 2)], BF16)
    tw_sb = p.sbuf("tw_sb", [128, 36], F32)
    tb_sb = p.sbuf("tb_sb", [128, 12], F32)
    uc = p.sbuf("uc", [128, S], F32)
    ucb = p.sbuf("ucb", [128, S], BF16)
    x1_sb = p.sbuf("x1_sb", [128, S], BF16)
    mw = p.sbuf("mw", [64, 3 * 64 + 16], BF16)
    mb = p.sbuf("mb", [64, 8], F32)
    h3 = p.sbuf("h3", [64, S], BF16)
    hs = p.sbuf("hs", [64, 1536], F32)
    zf_sb = p.sbuf("zf_sb", [3, S], BF16)
    wo_sb = p.sbuf("wo_sb", [64, 512], BF16)
    trow_sb = p.sbuf("trow_sb", [1, S], F32)
    on2 = p.sbuf("on2", [1, 128], F32)
    kte = p.sbuf("kte", [128, 512], F32)
    ktd = p.sbuf("ktd", [128, 512], F32)
    ktb = p.sbuf("ktb", [128, S], BF16)
    ps_a = p.psum("ps_a", [128, 512], F32)
    ps_b = p.psum("ps_b", [128, 512], F32)
    ps_c = p.psum("ps_c", [64, 512], F32)

    for sl2 in range(2):
        p.on("vector", lambda e, sl2=sl2: e.memset(
            up_sb[:, sl2 * (S + 2):sl2 * (S + 2) + 2], 0.0), sem="pad")
    for (dst, src) in ((tw_sb, tw), (tb_sb, tb), (trow_sb, trow), (on2, on2d)):
        p.dma("sync", dst[:], src, "ld")
    for (dst, src) in ((zf_sb, zf), (wo_sb, wo)):
        p.dma("gpsimd", dst[:], src, "ldg")
    p.dma("gpsimd", mw[0:3, 0:64], w1, "ldg")
    p.dma("gpsimd", mw[:, 64:128], w2, "ldg")
    p.dma("gpsimd", mw[:, 128:192], w3, "ldg")
    p.dma("sync", mb[:, 0:3], b123, "ld")
    p.dma("sync", mb[:, 3:5], freqc, "ld")
    negd_sb = p.sbuf("negd_sb", [128, 4], F32)
    p.dma("sync", negd_sb[:], negd, "ld")
    ldv = p.cnt["ld"]
    ldgv = p.cnt["ldg"]
    for eng in ("tensor", "vector", "scalar"):
        p.wait(eng, "ld", ldv)
        p.wait(eng, "ldg", ldgv)
    p.wait("vector", "pad", 2)

    li = [0]
    def conv_chunk(c, dst_kind, dst_idx):
        par = li[0] % 2
        base = par * (S + 2)
        if li[0] >= 2:
            p.wait("gpsimd", "cv", cvfree[par])
        p.dma("gpsimd", up_sb[:, base + 2:base + (S + 2)],
              up[c * 128:(c + 1) * 128, :], f"lu{par}")
        p.wait("vector", f"lu{par}", 16 * (li[0] // 2 + 1))
        li[0] += 1
        if p.cnt.get("cv", 0) > 0:
            p.wait("vector", "cv", p.cnt["cv"])
        p.on("vector", lambda e, base=base: e.tensor_scalar(
            uc[:], up_sb[:, base:base + S], tw_sb[:, 3 * c:3 * c + 1], None, ALU.mult),
            sem="cv")
        p.wait("vector", "cv", p.cnt["cv"])
        p.on("vector", lambda e, base=base: e.scalar_tensor_tensor(
            uc[:], up_sb[:, base + 1:base + 1 + S], tw_sb[:, 3 * c + 1:3 * c + 2], uc[:],
            ALU.mult, ALU.add), sem="cv")
        p.wait("vector", "cv", p.cnt["cv"])
        p.on("vector", lambda e, base=base: e.scalar_tensor_tensor(
            uc[:], up_sb[:, base + 2:base + 2 + S], tw_sb[:, 3 * c + 2:3 * c + 3], uc[:],
            ALU.mult, ALU.add), sem="cv")
        cvfree[par] = p.cnt["cv"]
        p.wait("vector", "cv", p.cnt["cv"])
        if dst_kind == "x1":
            p.on("vector", lambda e: e.tensor_scalar(
                x1_sb[:], uc[:], tb_sb[:, c:c + 1], None, ALU.add), sem="cv")
            p.wait("vector", "cv", p.cnt["cv"])
        elif dst_kind == "x0":
            if p.cnt.get("sto", 0) > 0:
                p.wait("vector", "sto", p.cnt["sto"])
            if p.cnt.get("stv", 0) > 0:
                p.wait("vector", "stv", p.cnt["stv"])
            p.on("vector", lambda e: e.tensor_scalar(
                ucb[:], uc[:], tb_sb[:, c:c + 1], None, ALU.add), sem="cv")
            p.wait("sync", "cv", p.cnt["cv"])
            p.dma("sync", x0o[dst_idx * 128:(dst_idx + 1) * 128, :], ucb[:], "sto")
        else:
            p.on("vector", lambda e: e.tensor_scalar(
                uc[:], uc[:], tb_sb[:, c:c + 1], None, ALU.add), sem="cv")
            p.wait("vector", "cv", p.cnt["cv"])
            if p.cnt.get("stv", 0) > 0:
                p.wait("vector", "stv", p.cnt["stv"])
            if p.cnt.get("sto", 0) > 0:
                p.wait("vector", "sto", p.cnt["sto"])
            p.on("vector", lambda e: e.tensor_tensor(
                ucb[:], uc[:], x1_sb[:], ALU.mult), sem="cv")
            p.wait("sync", "cv", p.cnt["cv"])
            p.dma("sync", vxo[dst_idx * 128:(dst_idx + 1) * 128, :], ucb[:], "stv")

    cvfree = [0, 0]
    for i in range(4):
        conv_chunk(4 + i, "x1", i)
        conv_chunk(8 + i, "v", i)
        conv_chunk(i, "x0", i)

    def sin_layer(li, lhs_ap, rhs_sb, kparts):
        for n in range(S // 512):
            sl = slice(n * 512, (n + 1) * 512)
            p.on("tensor", lambda e, sl=sl, lhs_ap=lhs_ap, rhs_sb=rhs_sb, kparts=kparts:
                 e.matmul(ps_c[:, 0:512], lhs_ap, rhs_sb[0:kparts, sl],
                          start=True, stop=True), sem="smm")
            p.wait("scalar", "smm", p.cnt["smm"])
            if p.cnt.get("sev", 0) > 0:
                p.wait("scalar", "sev", p.cnt["sev"])
            p.on("scalar", lambda e, li=li: e.activation(
                hs[:, 0:512], ps_c[:, 0:512], AF.Identity, bias=mb[:, li:li + 1]),
                sem="sact")
            p.wait("vector", "sact", p.cnt["sact"])
            p.on("vector", lambda e: e.tensor_scalar(
                hs[:, 512:1024], hs[:, 0:512], mb[:, 3:4], None, ALU.mult), sem="sv")
            p.wait("vector", "sv", p.cnt["sv"])
            for _rr in range(2):
                p.on("vector", lambda e: e.tensor_scalar(
                    hs[:, 0:512], hs[:, 512:1024], np.pi, -2 * np.pi, ALU.is_gt,
                    op1=ALU.mult), sem="sv")
                p.wait("vector", "sv", p.cnt["sv"])
                p.on("vector", lambda e: e.tensor_tensor(
                    hs[:, 512:1024], hs[:, 512:1024], hs[:, 0:512], ALU.add), sem="sv")
                p.wait("vector", "sv", p.cnt["sv"])
                p.on("vector", lambda e: e.tensor_scalar(
                    hs[:, 0:512], hs[:, 512:1024], -np.pi, 2 * np.pi, ALU.is_lt,
                    op1=ALU.mult), sem="sv")
                p.wait("vector", "sv", p.cnt["sv"])
                p.on("vector", lambda e: e.tensor_tensor(
                    hs[:, 512:1024], hs[:, 512:1024], hs[:, 0:512], ALU.add), sem="sv")
                p.wait("vector", "sv", p.cnt["sv"])
            p.wait("scalar", "sv", p.cnt["sv"])
            p.on("scalar", lambda e, sl=sl: e.activation(
                h3[:, sl], hs[:, 512:1024], AF.Sin), sem="sev")
            p.wait("tensor", "sev", p.cnt["sev"])
    sin_layer(0, mw[0:3, 0:64], zf_sb, 3)
    sin_layer(1, mw[:, 64:128], h3, 64)
    sin_layer(2, mw[:, 128:192], h3, 64)
    for cb in range(4):
        for n in range(S // 512):
            sl = slice(n * 512, (n + 1) * 512)
            if p.cnt.get("ke0", 0) > 0:
                p.wait("tensor", "ke0", p.cnt["ke0"])
            p.on("tensor", lambda e, cb=cb, sl=sl: e.matmul(
                ps_a[:], wo_sb[:, cb * 128:(cb + 1) * 128], h3[:, sl],
                start=True, stop=True), sem="kmm")
            p.on("tensor", lambda e, sl=sl: e.matmul(
                ps_b[:], on2[:], trow_sb[0:1, sl], start=True, stop=True), sem="kmm")
            p.wait("scalar", "kmm", p.cnt["kmm"])
            if p.cnt.get("kev", 0) > 0:
                p.wait("scalar", "kev", p.cnt["kev"])
            p.on("scalar", lambda e: e.activation(kte[:], ps_a[:], AF.Copy), sem="ke0")
            p.on("scalar", lambda e, cb=cb: e.activation(
                ktd[:], ps_b[:], AF.Exp, scale=negd_sb[:, cb:cb + 1]), sem="ke0")
            p.wait("vector", "ke0", p.cnt["ke0"])
            p.on("vector", lambda e, sl=sl: e.tensor_tensor(
                ktb[:, sl], kte[:], ktd[:], ALU.mult), sem="kev")
        p.wait("sync", "kev", p.cnt["kev"])
        p.dma("sync", kto[cb * 128:(cb + 1) * 128, :], ktb[:], "stk")
        p.wait("vector", "stk", p.cnt["stk"])
    p.wait("sync", "sto", p.cnt["sto"])
    p.wait("sync", "stv", p.cnt["stv"])
    p.wait("sync", "stk", p.cnt["stk"])
    return p.build()



def _dftc(n_i, n_j, den, sign=-1.0, scale=1.0):
    i = np.arange(n_i)[:, None]
    j = np.arange(n_j)[None, :]
    ang = 2.0 * np.pi * i * j / den
    return ((np.cos(ang) * scale).astype(np.float32),
            (sign * np.sin(ang) * scale).astype(np.float32))

def fft_consts():
    W1r, W1i = _dftc(64, 128, 128)
    W2r, W2i = _dftc(64, 64, 64)
    TWr, TWi = _dftc(128, 64, 8192)
    Wi1r, Wi1i = _dftc(64, 64, 64, sign=+1.)
    Tir, Tii = _dftc(64, 128, 8192, sign=+1.)
    Wi2r, Wi2i = _dftc(128, 64, 128, sign=+1., scale=1.0 / 8192.0)
    st = lambda a: np.ascontiguousarray(np.vstack([a, a]).astype(BF))
    c = {
        "w1r": np.ascontiguousarray(W1r.astype(BF)), "w1i": np.ascontiguousarray(W1i.astype(BF)),
        "w2r": st(W2r), "w2i": st(W2i), "w2in": st(-W2i),
        "wi1r": st(Wi1r), "wi1i": st(Wi1i), "wi1in": st(-Wi1i),
        "wi2r": np.ascontiguousarray(Wi2r.astype(BF)), "wi2in": np.ascontiguousarray((-Wi2i).astype(BF)),
        "twr": TWr, "twi": TWi, "tir": Tir, "tii": Tii,
        "idn": np.ascontiguousarray(np.eye(128, dtype=BF)),
        "idn2": np.ascontiguousarray(np.vstack([np.eye(64), np.eye(64)]).astype(BF)),
        "mdif": np.ascontiguousarray(np.vstack([np.eye(64), -np.eye(64)]).astype(BF)),
        "msum": np.ascontiguousarray(np.vstack([np.eye(64), np.eye(64)]).astype(BF)),
    }
    return c

FFTC = None

FFT_DBG = [0]

def build_fft(NB=8):
    p = Prog()
    S = 4096
    vxd = p.din("vxd", [512, S], BF16)
    ktd = p.din("ktd", [512, S], BF16)
    x0d = p.din("x0d", [512, S], BF16)
    fbr = p.din("fbr", [1, 512], F32)
    on2d = p.din("on2d", [1, 128], F32)
    yvo = p.dout("yvd", [512, S], BF16)
    cns = {}
    for nm, sh, dt in (("w1r", (64, 128), BF16), ("w1i", (64, 128), BF16),
                       ("w2r", (128, 64), BF16), ("w2i", (128, 64), BF16), ("w2in", (128, 64), BF16),
                       ("wi1r", (128, 64), BF16), ("wi1i", (128, 64), BF16), ("wi1in", (128, 64), BF16),
                       ("wi2r", (128, 64), BF16), ("wi2in", (128, 64), BF16),
                       ("twr", (128, 64), F32), ("twi", (128, 64), F32),
                       ("tir", (64, 128), F32), ("tii", (64, 128), F32),
                       ("idn", (128, 128), BF16), ("idn2", (128, 64), BF16),
                       ("mdif", (128, 64), BF16), ("msum", (128, 64), BF16)):
        d = p.din(nm, list(sh), dt)
        sb = p.sbuf(nm + "_sb", list(sh), dt)
        p.dma("sync", sb[:], d, "ldc")
        cns[nm] = sb
    fb_sb = p.sbuf("fb_sb", [1, 512], F32)
    on2 = p.sbuf("on2", [1, 128], F32)
    p.dma("sync", fb_sb[:], fbr, "ldc")
    p.dma("sync", on2[:], on2d, "ldc")
    xin = p.sbuf("xin", [64, S], BF16)
    x0b = p.sbuf("x0b", [64, S], BF16)
    Bre = p.sbuf("Bre", [128, S], BF16)
    Bim = p.sbuf("Bim", [128, S], BF16)
    Btr = p.sbuf("Btr", [128, S], BF16)
    Bti = p.sbuf("Bti", [128, S], BF16)
    Kst = p.sbuf("Kst", [128, 2 * S], BF16)
    Ksw = p.sbuf("Ksw", [128, 2 * S], BF16)
    Xst = p.sbuf("Xst", [128, 2 * S], BF16)
    pr1 = p.sbuf("pr1", [128, 2 * S], BF16)
    pr2 = p.sbuf("pr2", [128, 2 * S], BF16)
    yb = p.sbuf("yb", [64, S], F32)
    fbb = p.sbuf("fbb", [64, 64], F32)
    t1b = p.sbuf("t1b", [64, 2048], F32)
    PA = p.sbuf("PA", [64, 2 * S], BF16)
    PB = p.sbuf("PB", [64, 2 * S], BF16)
    ov = p.sbuf("ov", [64, 512], BF16)
    ps1 = p.psum("ps1", [128, 512], F32)
    ps2 = p.psum("ps2", [128, 512], F32)
    ps3 = p.psum("ps3", [128, 512], F32)
    ps4 = p.psum("ps4", [128, 512], F32)
    ps5 = p.psum("ps5", [128, 512], F32)
    ps6 = p.psum("ps6", [128, 512], F32)
    pb1 = p.psum("pb1", [128, 128], BF16)
    pb2 = p.psum("pb2", [128, 128], BF16)
    ldcv = p.cnt["ldc"]
    for eng in ("tensor", "vector", "scalar"):
        p.wait(eng, "ldc", ldcv)

    def stage12(src_sb, dst_st, dst_sw, t):
        npc = 4 if dst_sw is not None else 2
        Z = lambda nm: p.cnt.get(nm + t, 0)
        b_twm, b_tw, b_mm1 = Z("twm"), Z("tw"), Z("mm1")
        for n in range(8):
            sl = slice(n * 512, (n + 1) * 512)
            if b_twm + 4 * n > 0:
                p.wait("tensor", "twm" + t, b_twm + 4 * n)
            p.on("tensor", lambda e, sl=sl, src_sb=src_sb: e.matmul(
                ps1[:, 0:512], cns["w1r"][:], src_sb[0:64, sl], start=True, stop=True))
            p.on("tensor", lambda e, sl=sl, src_sb=src_sb: e.matmul(
                ps2[:, 0:512], cns["w1i"][:], src_sb[0:64, sl], start=True, stop=True),
                sem="mm1" + t)
            p.wait("vector", "mm1" + t, p.cnt["mm1" + t])
            if b_tw + 2 * n > 0:
                p.wait("vector", "tw" + t, b_tw + 2 * n)
            twr_b = cns["twr"][:].rearrange("p n -> p () n").broadcast_to((128, 8, 64))
            twi_b = cns["twi"][:].rearrange("p n -> p () n").broadcast_to((128, 8, 64))
            pa = ps1[:, 0:512].rearrange("p (c n) -> p c n", n=64)
            pb = ps2[:, 0:512].rearrange("p (c n) -> p c n", n=64)
            u1 = pr1[:, 0:512].rearrange("p (c n) -> p c n", n=64)
            u2 = pr1[:, 512:1024].rearrange("p (c n) -> p c n", n=64)
            u3 = pr2[:, 0:512].rearrange("p (c n) -> p c n", n=64)
            u4 = pr2[:, 512:1024].rearrange("p (c n) -> p c n", n=64)
            for (dst, a, wgt) in ((u1, pa, twr_b), (u2, pb, twi_b),
                                  (u3, pa, twi_b), (u4, pb, twr_b)):
                p.on("vector", lambda e, dst=dst, a=a, wgt=wgt: e.tensor_tensor(
                    dst, a, wgt, ALU.mult), sem="twm" + t)
            p.wait("vector", "twm" + t, p.cnt["twm" + t])
            p.on("vector", lambda e, sl=sl: e.tensor_tensor(
                Bre[:, sl], pr1[:, 0:512], pr1[:, 512:1024], ALU.subtract), sem="tw" + t)
            p.on("vector", lambda e, sl=sl: e.tensor_tensor(
                Bim[:, sl], pr2[:, 0:512], pr2[:, 512:1024], ALU.add), sem="tw" + t)
        p.wait("tensor", "tw" + t, p.cnt["tw" + t])
        b_mtE = Z("mtE")
        for cp in range(32):
            if b_mtE + 2 * cp > 0:
                p.wait("tensor", "mtE" + t, b_mtE + 2 * cp)
            for (src, dps) in ((Bre, pb1), (Bim, pb2)):
                ap = bass.AP(src, cp * 128, [[S, 128], [64, 2], [1, 64]])
                p.on("tensor", lambda e, ap=ap, dps=dps: e.transpose(
                    dps[:, 0:128], ap, cns["idn"][:]), sem="mtT" + t)
            p.wait("scalar", "mtT" + t, p.cnt["mtT" + t])
            p.on("scalar", lambda e, cp=cp: e.activation(
                Btr[:, cp * 128:(cp + 1) * 128], pb1[:, 0:128], AF.Copy), sem="mtE" + t)
            p.on("scalar", lambda e, cp=cp: e.activation(
                Bti[:, cp * 128:(cp + 1) * 128], pb2[:, 0:128], AF.Copy), sem="mtE" + t)
        p.wait("tensor", "mtE" + t, p.cnt["mtE" + t])
        b_s2e = Z("s2e")
        for c in range(64):
            par = c % 2
            b0 = par * 64
            cp = c // 2
            rre = Btr[b0:b0 + 64, cp * 128:(cp + 1) * 128]
            rim = Bti[b0:b0 + 64, cp * 128:(cp + 1) * 128]
            slot = c % 4
            pt = (ps3, ps4, ps5, ps6)[slot]
            po = 0
            if c >= 4 or b_s2e > 0:
                p.wait("tensor", "s2e" + t, b_s2e + npc * max(0, c - 3))
            p.on("tensor", lambda e, pt=pt, po=po, b0=b0, rre=rre: e.matmul(
                pt[0:64, po:po + 128], cns["w2r"][b0:b0 + 64, :], rre, start=True, stop=False))
            p.on("tensor", lambda e, pt=pt, po=po, b0=b0, rim=rim: e.matmul(
                pt[0:64, po:po + 128], cns["w2in"][b0:b0 + 64, :], rim, start=False, stop=True))
            p.on("tensor", lambda e, pt=pt, po=po, b0=b0, rre=rre: e.matmul(
                pt[64:128, po:po + 128], cns["w2i"][b0:b0 + 64, :], rre, start=True, stop=False))
            p.on("tensor", lambda e, pt=pt, po=po, b0=b0, rim=rim: e.matmul(
                pt[64:128, po:po + 128], cns["w2r"][b0:b0 + 64, :], rim, start=False, stop=True),
                sem="s2m" + t)
            p.wait("scalar", "s2m" + t, p.cnt["s2m" + t])
            csl = slice(c * 128, (c + 1) * 128)
            p.on("scalar", lambda e, pt=pt, po=po, csl=csl, dst_st=dst_st: e.activation(
                dst_st[0:64, csl], pt[0:64, po:po + 128], AF.Copy), sem="s2e" + t)
            p.on("scalar", lambda e, pt=pt, po=po, csl=csl, dst_st=dst_st: e.activation(
                dst_st[64:128, csl], pt[64:128, po:po + 128], AF.Copy), sem="s2e" + t)
            if dst_sw is not None:
                p.on("scalar", lambda e, pt=pt, po=po, csl=csl, dst_sw=dst_sw: e.activation(
                    dst_sw[64:128, csl], pt[0:64, po:po + 128], AF.Copy), sem="s2e" + t)
                p.on("scalar", lambda e, pt=pt, po=po, csl=csl, dst_sw=dst_sw: e.activation(
                    dst_sw[0:64, csl], pt[64:128, po:po + 128], AF.Copy), sem="s2e" + t)
        p.wait("tensor", "s2e" + t, p.cnt["s2e" + t])

    nsty = 0
    for blk in range(NB):
        t = ""
        Z = lambda nm: p.cnt.get(nm, 0)
        rs = slice(blk * 64, blk * 64 + 64)
        if blk > 0:
            p.wait("gpsimd", "ep1", 8 * blk)
            p.wait("gpsimd", "ep3", 8 * blk)
        p.dma("gpsimd", xin[:], ktd[rs, :], f"ldk{blk % 2}")
        p.wait("tensor", f"ldk{blk % 2}", 16 * (blk // 2 + 1))
        if blk > 0:
            p.wait("tensor", "fbc", blk)
        stage12(xin, Kst, Ksw, "k")
        if FFT_DBG[0] == 1:
            p.wait("sync", "s2ek", p.cnt["s2ek"])
            p.dma("sync", yvo[rs, :], Kst[0:64, 0:S], "sty")
            p.wait("sync", "sty", 16)
            return p.build()
        p.wait("gpsimd", "mm1k", p.cnt["mm1k"])
        p.dma("gpsimd", xin[:], vxd[rs, :], f"ldv{blk % 2}")
        p.dma("gpsimd", x0b[:], x0d[rs, :], f"ldv{blk % 2}")
        p.wait("tensor", f"ldv{blk % 2}", 32 * (blk // 2 + 1))
        p.wait("vector", f"ldv{blk % 2}", 32 * (blk // 2 + 1))
        stage12(xin, Xst, None, "v")
        if FFT_DBG[0] == 2:
            p.wait("sync", "s2ev", p.cnt["s2ev"])
            p.dma("sync", yvo[rs, :], Xst[0:64, 0:S], "sty")
            p.wait("sync", "sty", 16)
            return p.build()
        p.wait("vector", "s2ev", p.cnt["s2ev"])
        p.wait("vector", "s2ek", p.cnt["s2ek"])
        b_pw2 = Z("pw2")
        b_pwm = Z("pwm")
        for n in range(16):
            sl = slice(n * 512, (n + 1) * 512)
            if b_pwm + n > 0:
                p.wait("vector", "pwm", b_pwm + n)
            p.on("vector", lambda e, sl=sl: e.tensor_tensor(
                pr1[:, sl], Xst[:, sl], Kst[:, sl], ALU.mult), sem="pw1")
            p.on("vector", lambda e, sl=sl: e.tensor_tensor(
                pr2[:, sl], Xst[:, sl], Ksw[:, sl], ALU.mult), sem="pw1")
            p.wait("tensor", "pw1", p.cnt["pw1"])
            pt = (ps3, ps4)[n % 2]
            pu = (ps5, ps6)[n % 2]
            if b_pw2 + 2 * max(0, n - 1) > 0:
                p.wait("tensor", "pw2", b_pw2 + 2 * max(0, n - 1))
            p.on("tensor", lambda e, pt=pt, sl=sl: e.matmul(
                pt[0:64, 0:512], cns["mdif"][:], pr1[:, sl], start=True, stop=True))
            p.on("tensor", lambda e, pu=pu, sl=sl: e.matmul(
                pu[0:64, 0:512], cns["msum"][:], pr2[:, sl], start=True, stop=True),
                sem="pwm")
            p.wait("scalar", "pwm", p.cnt["pwm"])
            p.on("scalar", lambda e, pt=pt, sl=sl: e.activation(
                PA[:, sl], pt[0:64, 0:512], AF.Copy), sem="pw2")
            p.on("scalar", lambda e, pu=pu, sl=sl: e.activation(
                PB[:, sl], pu[0:64, 0:512], AF.Copy), sem="pw2")
        p.wait("tensor", "pw2", p.cnt["pw2"])
        p.wait("vector", "pw2", p.cnt["pw2"])
        if FFT_DBG[0] == 3:
            p.wait("sync", "pw2", p.cnt["pw2"])
            p.dma("sync", yvo[rs, :], Xst[0:64, 0:S], "sty")
            p.wait("sync", "sty", 16)
            return p.build()
        b_i1e = Z("i1e")
        for c in range(64):
            slot = c % 4
            pt = (ps3, ps4, ps5, ps6)[slot]
            po = 0
            rre = PA[:, c * 128:(c + 1) * 128]
            rim = PB[:, c * 128:(c + 1) * 128]
            if c >= 4 or b_i1e > 0:
                p.wait("tensor", "i1e", b_i1e + 2 * max(0, c - 3))
            p.on("tensor", lambda e, pt=pt, po=po, rre=rre: e.matmul(
                pt[0:64, po:po + 128], cns["wi1r"][0:64, :], rre, start=True, stop=False))
            p.on("tensor", lambda e, pt=pt, po=po, rim=rim: e.matmul(
                pt[0:64, po:po + 128], cns["wi1in"][0:64, :], rim, start=False, stop=True))
            p.on("tensor", lambda e, pt=pt, po=po, rim=rim: e.matmul(
                pt[64:128, po:po + 128], cns["wi1r"][0:64, :], rim, start=True, stop=False))
            p.on("tensor", lambda e, pt=pt, po=po, rre=rre: e.matmul(
                pt[64:128, po:po + 128], cns["wi1i"][0:64, :], rre, start=False, stop=True),
                sem="i1m")
            p.wait("scalar", "i1m", p.cnt["i1m"])
            csl = slice(c * 128, (c + 1) * 128)
            p.on("scalar", lambda e, pt=pt, po=po, csl=csl: e.activation(
                pr1[0:64, csl], pt[0:64, po:po + 128], AF.Copy), sem="i1e")
            p.on("scalar", lambda e, pt=pt, po=po, csl=csl: e.activation(
                pr2[0:64, csl], pt[64:128, po:po + 128], AF.Copy), sem="i1e")
        p.wait("vector", "i1e", p.cnt["i1e"])
        p.wait("vector", "i1m", p.cnt["i1m"])
        if FFT_DBG[0] == 4:
            p.wait("sync", "i1e", p.cnt["i1e"])
            p.dma("sync", yvo[rs, :], pr1[0:64, 0:S], "sty")
            p.wait("sync", "sty", 16)
            return p.build()
        b_tp = Z("tp")
        for n in range(16):
            sl = slice(n * 512, (n + 1) * 512)
            if b_tp + 2 * n > 0:
                p.wait("vector", "tp", b_tp + 2 * n)
            tir_b = cns["tir"][:].rearrange("p n -> p () n").broadcast_to((64, 4, 128))
            tii_b = cns["tii"][:].rearrange("p n -> p () n").broadcast_to((64, 4, 128))
            da = pr1[0:64, sl].rearrange("p (c n) -> p c n", n=128)
            db = pr2[0:64, sl].rearrange("p (c n) -> p c n", n=128)
            w1_ = t1b[0:64, 0:512].rearrange("p (c n) -> p c n", n=128)
            w2_ = t1b[0:64, 512:1024].rearrange("p (c n) -> p c n", n=128)
            w3_ = t1b[0:64, 1024:1536].rearrange("p (c n) -> p c n", n=128)
            w4_ = t1b[0:64, 1536:2048].rearrange("p (c n) -> p c n", n=128)
            for (dst, a, wgt) in ((w1_, da, tir_b), (w2_, db, tii_b),
                                  (w3_, db, tir_b), (w4_, da, tii_b)):
                p.on("vector", lambda e, dst=dst, a=a, wgt=wgt: e.tensor_tensor(
                    dst, a, wgt, ALU.mult), sem="tpm")
            p.wait("vector", "tpm", p.cnt["tpm"])
            p.on("vector", lambda e, sl=sl: e.tensor_tensor(
                PA[:, sl], t1b[0:64, 0:512], t1b[0:64, 512:1024], ALU.subtract),
                sem="tp")
            p.on("vector", lambda e, sl=sl: e.tensor_tensor(
                PB[:, sl], t1b[0:64, 1024:1536], t1b[0:64, 1536:2048], ALU.add),
                sem="tp")
        p.wait("tensor", "tp", p.cnt["tp"])
        if FFT_DBG[0] == 5:
            p.wait("sync", "tp", p.cnt["tp"])
            p.dma("sync", yvo[rs, :], PA[0:64, 0:S], "sty")
            p.wait("sync", "sty", 16)
            return p.build()
        b_itE = Z("itE")
        for c in range(64):
            if b_itE + 2 * c > 0:
                p.wait("tensor", "itE", b_itE + 2 * c)
            for (srow, dps) in ((PA[:, c * 128:(c + 1) * 128], pb1),
                                (PB[:, c * 128:(c + 1) * 128], pb2)):
                p.on("tensor", lambda e, srow=srow, dps=dps: e.transpose(
                    dps[0:128, 0:64], srow, cns["idn2"][0:64, 0:64]), sem="itT")
            p.wait("scalar", "itT", p.cnt["itT"])
            p.on("scalar", lambda e, c=c: e.activation(
                Btr[:, c * 64:(c + 1) * 64], pb1[:, 0:64], AF.Copy), sem="itE")
            p.on("scalar", lambda e, c=c: e.activation(
                Bti[:, c * 64:(c + 1) * 64], pb2[:, 0:64], AF.Copy), sem="itE")
        p.wait("tensor", "itE", p.cnt["itE"])
        if FFT_DBG[0] == 6:
            p.wait("sync", "itE", p.cnt["itE"])
            p.dma("sync", yvo[rs, :], Btr[0:64, 0:S], "sty")
            p.wait("sync", "sty", 16)
            return p.build()
        b_ye = Z("ye")
        for n in range(8):
            sl = slice(n * 512, (n + 1) * 512)
            if b_ye + max(0, n - 1) > 0:
                p.wait("tensor", "ye", b_ye + max(0, n - 1))
            pt = (ps3, ps4)[n % 2]
            p.on("tensor", lambda e, pt=pt, sl=sl: e.matmul(
                pt[0:64, 0:512], cns["wi2r"][:], Btr[:, sl], start=True, stop=False))
            p.on("tensor", lambda e, pt=pt, sl=sl: e.matmul(
                pt[0:64, 0:512], cns["wi2in"][:], Bti[:, sl], start=False, stop=True),
                sem="ym")
            p.wait("scalar", "ym", p.cnt["ym"])
            if b_ye + max(0, n - 1) > 0:
                p.wait("scalar", "ye", b_ye + max(0, n - 1))
            p.on("scalar", lambda e, pt=pt, sl=sl: e.activation(
                yb[:, sl], pt[0:64, 0:512], AF.Copy), sem="ye")
        p.on("tensor", lambda e, blk=blk: e.matmul(
            ps1[0:64, 0:64], on2[0:1, 0:64], fb_sb[0:1, blk * 64:(blk + 1) * 64],
            start=True, stop=True), sem="fbm")
        p.wait("vector", "fbm", blk + 1)
        p.on("vector", lambda e: e.tensor_copy(fbb[:], ps1[0:64, 0:64]), sem="fbc")
        p.wait("vector", "ye", p.cnt["ye"])
        p.wait("vector", "fbc", blk + 1)
        b_ep2 = Z("ep2")
        for n in range(8):
            sl = slice(n * 512, (n + 1) * 512)
            fb_b = fbb[:, n * 8:(n + 1) * 8].rearrange("p c -> p c ()").broadcast_to((64, 8, 64))
            xin3 = xin[0:64, sl].rearrange("p (c n) -> p c n", n=64)
            t13 = t1b[0:64, 0:512].rearrange("p (c n) -> p c n", n=64)
            if b_ep2 + n > 0:
                p.wait("vector", "ep2", b_ep2 + n)
            p.on("vector", lambda e, xin3=xin3, t13=t13, fb_b=fb_b: e.tensor_tensor(
                t13, xin3, fb_b, ALU.mult), sem="ep1")
            p.wait("vector", "ep1", p.cnt["ep1"])
            p.on("vector", lambda e, sl=sl: e.tensor_tensor(
                t1b[0:64, 512:1024], yb[:, sl], t1b[0:64, 0:512], ALU.add), sem="ep2")
            p.wait("vector", "ep2", p.cnt["ep2"])
            if nsty > 0:
                p.wait("vector", "sty", 16 * nsty)
            p.on("vector", lambda e, sl=sl: e.tensor_tensor(
                ov[:, 0:512], t1b[0:64, 512:1024], x0b[0:64, sl], ALU.mult), sem="ep3")
            p.wait("sync", "ep3", p.cnt["ep3"])
            p.dma("sync", yvo[rs, sl], ov[:, 0:512], "sty")
            nsty += 1
    p.wait("sync", "sty", 16 * nsty)
    return p.build()



def _l1_consts():
    S = 4096
    zf = np.zeros((3, S), np.float32)
    t_ = np.linspace(0.0, 1.0, S)
    tr = np.linspace(0.0, S - 1.0, S)
    w_ = 2 * np.pi * tr / S
    f_ = 1e-4
    zf[0] = t_
    zf[1] = np.cos(f_ * w_)
    zf[2] = -np.sin(f_ * w_)
    deltas = np.abs(np.linspace(np.log(0.01) / 1.5, np.log(0.01) / 0.3, 1024)).astype(np.float32)
    return zf, t_.astype(np.float32), deltas

def relay_fft(a):
    return np.ascontiguousarray(a.reshape(8, 64, 64, 64).transpose(0, 2, 1, 3).reshape(512, 4096))

def unrelay_fft(a):
    return np.ascontiguousarray(a.reshape(8, 64, 64, 64).transpose(0, 2, 1, 3).reshape(512, 4096))

def run_layer1(x_fm_shards, inp):
    xn = run_ln(x_fm_shards)
    nc = cached(("l1up",), build_projN, D, [{"out": "up", "m": 3072, "bias": True}])
    w_in = np.ascontiguousarray(inp["L1_attn_in_w"], dtype=np.float32)
    b_in = tile_vec(inp["L1_attn_in_b"])
    res = run(nc, [{"x": np.asarray(xn[c], dtype=BF), "w0": w_in, "b0": b_in}
                   for c in range(NC)])
    ups = [r["up"] for r in res]
    zf, trow, deltas = _l1_consts()
    up_sl = []
    for c in range(NC):
        j = c % 2
        pair = pair_cat(ups, c)
        rows = np.concatenate([pair[j * 512:(j + 1) * 512],
                               pair[1024 + j * 512:1024 + (j + 1) * 512],
                               pair[2048 + j * 512:2048 + (j + 1) * 512]], axis=0)
        up_sl.append(np.ascontiguousarray(rows))
    nc = cached(("l1conv",), build_l1conv)
    sw = np.asarray(inp["L1_attn_short_w"], dtype=np.float32)
    sb = np.asarray(inp["L1_attn_short_b"], dtype=np.float32)
    mlp = {k: np.ascontiguousarray(inp[f"L1_attn_mlp_{k}"], dtype=np.float32)
           for k in ("w1", "b1", "w2", "b2", "w3", "b3", "wo")}
    freq = np.asarray(inp["L1_attn_freq"], dtype=np.float32).reshape(64)
    ims = []
    for c in range(NC):
        j = c % 2
        chs = np.r_[j * 512:(j + 1) * 512]
        rows3 = np.concatenate([chs, 1024 + chs, 2048 + chs])
        twp = np.zeros((128, 36), np.float32)
        swr = sw[rows3]
        sbr = sb[rows3]
        for cc in range(12):
            for jj in range(3):
                twp[:, 3 * cc + jj] = swr[cc * 128:(cc + 1) * 128, jj]
        ims.append({"up": up_sl[c], "tw": twp, "tb": tile_vec(sbr), "zf": zf,
                    "w1": mlp["w1"], "w2": mlp["w2"], "w3": mlp["w3"],
                    "b123": np.ascontiguousarray(np.stack([mlp["b1"], mlp["b2"], mlp["b3"]], 1)),
                    "freqc": np.ascontiguousarray(np.stack([freq, np.full(64, -np.pi, np.float32)], 1)),
                    "wo": np.ascontiguousarray(mlp["wo"][:, chs]),
                    "negd": tile_vec(-deltas[chs]),
                    "trow": trow.reshape(1, 4096), "on2d": ONES_ROW})
    res = run(nc, ims)
    ncf = cached(("l1fft",), build_fft)
    global FFTC
    if FFTC is None:
        FFTC = fft_consts()
    fb = np.asarray(inp["L1_attn_filt_bias"], dtype=np.float32)
    ims2 = []
    for c in range(NC):
        j = c % 2
        chs = np.r_[j * 512:(j + 1) * 512]
        ims2.append({"vxd": relay_fft(np.asarray(res[c]["vx"])),
                     "ktd": relay_fft(np.asarray(res[c]["kt"])),
                     "x0d": relay_fft(np.asarray(res[c]["x0"])),
                     "fbr": np.ascontiguousarray(fb[chs].reshape(1, 512)),
                     "on2d": ONES_ROW, **FFTC})
    res2 = run(ncf, ims2)
    yv_sl = [unrelay_fft(np.asarray(r["yvd"])) for r in res2]
    yvs = []
    for c in range(NC):
        b = (c // 2) * 2
        tcols = np.s_[(c % 2) * T:(c % 2 + 1) * T]
        full = np.empty((1024, T), dtype=BF)
        full[0:512] = yv_sl[b][:, tcols]
        full[512:1024] = yv_sl[b + 1][:, tcols]
        yvs.append(full)
    ncg = cached(("gateup", D, 3072, "silu", True, "i32"), build_gateup, D, 3072,
                 AF.Silu, True, T, BF16, I32, I32)
    wg = np.ascontiguousarray(inp["L1_ffn_w_gate"].T)
    wu = np.ascontiguousarray(inp["L1_ffn_w_up"].T)
    s127 = tile_vec(np.full(3072, 1.0 / 127.0, np.float32))
    res3 = run(ncg, [{"x": np.asarray(xn[c], dtype=BF), "w0": wg, "w1": wu,
                      "s0": s127, "s1": s127} for c in range(NC)])
    hs = [r["y"] for r in res3]
    ncc = cached(("combine", D, 3072, True, True, "i32w2"), build_combine, D, 3072,
                 True, True, T, F32, I32)
    wo_l = np.ascontiguousarray(inp["L1_attn_out_w"], dtype=np.float32)
    wd = np.ascontiguousarray(inp["L1_ffn_w_down"].T)
    bo = tile_vec(inp["L1_attn_out_b"])
    s127d = tile_vec(np.full(1024, 1.0 / 127.0, np.float32))
    res4 = run(ncc, [{"x": np.ascontiguousarray(x_fm_shards[c], dtype=np.float32),
                      "a": yvs[c], "h": hs[c], "w1": wo_l, "w2": wd,
                      "b1": bo, "s2": s127d} for c in range(NC)])
    return [r["xo"] for r in res4]



def build_linattn(S=4096, Tn=T, H=16):
    p = Prog()
    SC = S // 128
    NT = Tn // 512
    qd = p.din("q", [D, Tn], BF16)
    ktm = p.din("ktm", [S, H * 64], BF16)
    vt = p.din("vt", [H, S * 65], BF16)
    on2d = p.din("on2d", [1, 128], F32)
    ctx = p.dout("ctx", [D, Tn], BF16)
    q_sb = p.sbuf("q_sb", [128, (D // 128) * Tn], BF16)
    k_sb = p.sbuf("k_sb", [128, SC * 64], BF16)
    v_sb = p.sbuf("v_sb", [128, SC * 65], BF16)
    kv_sb = p.sbuf("kv_sb", [128, 65], BF16)
    ctx_sb = p.sbuf("ctx_sb", [128, (D // 128) * Tn], BF16)
    on2 = p.sbuf("on2", [1, 128], F32)
    c1 = p.sbuf("c1", [64, 512], F32)
    rrow = p.sbuf("rrow", [1, 512], F32)
    ps_kv = p.psum("ps_kv", [64, 512], F32)
    ps_av = p.psum("ps_av", [65, 512], F32)
    ps_bc = p.psum("ps_bc", [64, 512], F32)
    load_fm(p, "sync", q_sb, qd, D // 128, Tn, "ld")
    p.dma("sync", on2[:], on2d, "ld")
    ldv = p.cnt["ld"]
    p.wait("tensor", "ld", ldv)
    kvm = ave = ctm = 0
    for h in range(H):
        if h > 0:
            p.wait("gpsimd", "kvm", h)
        p.dma("gpsimd", k_sb[:].rearrange("p (kc m) -> p kc m", m=64),
              ktm[:, h * 64:(h + 1) * 64].rearrange("(kc p) m -> p kc m", p=128), "lk")
        p.dma("gpsimd", v_sb[:].rearrange("p (kc m) -> p kc m", m=65),
              vt[h:h + 1, :].rearrange("one (kc pp m) -> (one pp) kc m", pp=128, m=65),
              "lk")
        p.wait("tensor", "lk", 32 * (h + 1))
        for kc in range(SC):
            p.on("tensor", lambda e, kc=kc, st=(kc == 0), sp=(kc == SC - 1): e.matmul(
                ps_kv[:, 0:65], chunk(k_sb, kc, 64), chunk(v_sb, kc, 65),
                start=st, stop=sp), sem="kvm" if kc == SC - 1 else None)
        kvm = p.cnt["kvm"]
        p.wait("scalar", "kvm", kvm)
        if h > 0:
            p.wait("scalar", "avm", ave)
        p.on("scalar", lambda e: e.activation(kv_sb[0:64, :], ps_kv[:, 0:65], AF.Copy),
             sem="kve")
        p.on("scalar", lambda e: e.activation(kv_sb[64:128, :], ps_kv[:, 0:65], AF.Copy),
             sem="kve")
        p.wait("tensor", "kve", p.cnt["kve"])
        kc_ = h // 2
        base = (h % 2) * 64
        for qt in range(NT):
            qsl = slice(qt * 512, (qt + 1) * 512)
            if ctm > 0:
                p.wait("tensor", "ctm", ctm)
            p.on("tensor", lambda e, kc_=kc_, base=base, qsl=qsl: e.matmul(
                ps_av[:, 0:512], kv_sb[base:base + 64, :],
                chunk(q_sb, kc_, Tn)[base:base + 64, qsl], start=True, stop=True),
                sem="avm")
            ave = p.cnt["avm"]
            p.wait("vector", "avm", ave)
            p.on("vector", lambda e: e.tensor_scalar(
                rrow[:], ps_av[64:65, 0:512], 1e-6, None, ALU.add), sem="rr0")
            p.wait("vector", "rr0", p.cnt["rr0"])
            p.on("vector", lambda e: e.reciprocal(rrow[:], rrow[:]), sem="rrd")
            p.wait("tensor", "rrd", p.cnt["rrd"])
            p.on("tensor", lambda e: e.matmul(ps_bc[:, 0:512], on2[0:1, 0:64], rrow[:],
                                              start=True, stop=True), sem="bcm")
            p.wait("scalar", "avm", ave)
            if ctm > 0:
                p.wait("scalar", "ctm", ctm)
            p.on("scalar", lambda e: e.activation(c1[:], ps_av[0:64, 0:512], AF.Copy),
                 sem="c1e")
            p.wait("vector", "bcm", p.cnt["bcm"])
            p.wait("vector", "c1e", p.cnt["c1e"])
            p.on("vector", lambda e, kc_=kc_, base=base, qsl=qsl: e.tensor_tensor(
                chunk(ctx_sb, kc_, Tn)[base:base + 64, qsl], c1[:], ps_bc[0:64, 0:512],
                ALU.mult), sem="ctm")
            ctm = p.cnt["ctm"]
    p.wait("sync", "ctm", ctm)
    for c in range(D // 128):
        p.dma("sync", ctx[c * 128:(c + 1) * 128, :], chunk(ctx_sb, c, Tn), "sto")
    p.wait("sync", "sto", p.cnt["sto"])
    return p.build()



def build_router():
    p = Prog()
    Tn = T
    x = p.din("x", [D, Tn], F32)
    wr = p.din("wr", [D, 8], F32)
    iot = p.din("iot", [128, 8], F32)
    rt = p.dout("route", [Tn, 18], F32)
    x_sb = p.sbuf("x_sb", [128, (D // 128) * Tn], F32)
    wr_sb = p.sbuf("wr_sb", [128, (D // 128) * 8], F32)
    iot_sb = p.sbuf("iot_sb", [128, 8], F32)
    o_sb = p.sbuf("o_sb", [128, 18], F32)
    sc_sb = p.sbuf("sc_sb", [128, 40], F32)
    ps = p.psum("ps", [128, 8], F32)
    load_fm(p, "sync", x_sb, x, D // 128, Tn, "ld")
    p.dma("sync", wr_sb[:].rearrange("p (kc m) -> p kc m", m=8),
          wr.rearrange("(kc p) m -> p kc m", p=128), "ld")
    p.dma("sync", iot_sb[:], iot, "ld")
    ldv = p.cnt["ld"]
    p.wait("tensor", "ld", ldv)
    p.wait("vector", "ld", ldv)
    V = "v"
    def vop(fn):
        p.on("vector", fn, sem=V)
        p.wait("vector", V, p.cnt[V])
    for tc in range(Tn // 128):
        if tc > 0:
            p.wait("tensor", "sto", 16 * tc)
            p.wait("tensor", V, p.cnt[V])
        for kc in range(D // 128):
            p.on("tensor", lambda e, tc=tc, kc=kc, st=(kc == 0), sp=(kc == D // 128 - 1):
                 e.matmul(ps[:], chunk(x_sb, kc, Tn)[:, tc * 128:(tc + 1) * 128],
                          chunk(wr_sb, kc, 8), start=st, stop=sp),
                 sem="mm" if kc == D // 128 - 1 else None)
        p.wait("vector", "mm", p.cnt["mm"])
        if tc > 0:
            p.wait("vector", "sto", 16 * tc)
        pr = sc_sb[:, 0:8]
        mx = sc_sb[:, 8:9]
        sm = sc_sb[:, 9:10]
        t1_ = sc_sb[:, 10:18]
        t2_ = sc_sb[:, 18:26]
        e1 = sc_sb[:, 26:27]
        e2 = sc_sb[:, 27:28]
        vop(lambda e: e.tensor_reduce(mx, ps[:], AX.X, ALU.max))
        vop(lambda e: e.tensor_tensor(pr, ps[:], mx.broadcast_to((128, 8)), ALU.subtract))
        p.wait("scalar", V, p.cnt[V])
        p.on("scalar", lambda e: e.activation(pr, pr, AF.Exp), sem="ex")
        p.wait("vector", "ex", p.cnt["ex"])
        vop(lambda e: e.tensor_reduce(sm, pr, AX.X, ALU.add))
        vop(lambda e: e.reciprocal(sm, sm))
        vop(lambda e: e.tensor_tensor(pr, pr, sm.broadcast_to((128, 8)), ALU.mult))
        vop(lambda e: e.tensor_reduce(mx, pr, AX.X, ALU.max))
        vop(lambda e: e.tensor_tensor(t1_, pr, mx.broadcast_to((128, 8)), ALU.is_ge))
        vop(lambda e: e.tensor_scalar(t2_, t1_, -100.0, 100.0, ALU.mult, op1=ALU.add))
        vop(lambda e: e.tensor_tensor(t2_, t2_, iot_sb[:], ALU.add))
        vop(lambda e: e.tensor_reduce(e1, t2_, AX.X, ALU.min))
        vop(lambda e: e.tensor_tensor(o_sb[:, 0:8], iot_sb[:], e1.broadcast_to((128, 8)),
                                      ALU.is_equal))
        vop(lambda e: e.tensor_copy(o_sb[:, 16:17], mx))
        vop(lambda e: e.tensor_scalar(t1_, o_sb[:, 0:8], -1.0, 1.0, ALU.mult, op1=ALU.add))
        vop(lambda e: e.tensor_tensor(pr, pr, t1_, ALU.mult))
        vop(lambda e: e.tensor_reduce(mx, pr, AX.X, ALU.max))
        vop(lambda e: e.tensor_tensor(t1_, pr, mx.broadcast_to((128, 8)), ALU.is_ge))
        vop(lambda e: e.tensor_scalar(t2_, t1_, -100.0, 100.0, ALU.mult, op1=ALU.add))
        vop(lambda e: e.tensor_tensor(t2_, t2_, iot_sb[:], ALU.add))
        vop(lambda e: e.tensor_reduce(e2, t2_, AX.X, ALU.min))
        vop(lambda e: e.tensor_tensor(o_sb[:, 8:16], iot_sb[:], e2.broadcast_to((128, 8)),
                                      ALU.is_equal))
        p.on("vector", lambda e: e.tensor_copy(o_sb[:, 17:18], mx), sem="fin")
        p.wait("sync", "fin", p.cnt["fin"])
        p.dma("sync", rt[tc * 128:(tc + 1) * 128, :], o_sb[:], "sto")
    p.wait("sync", "sto", p.cnt["sto"])
    return p.build()



def build_routing(NA=2 * 4 * 4096, CAP=4096):
    p = Prog()
    NCH = NA // 128
    A = p.din("A", [128, (NA // 128) * 8], F32)
    ltri = p.din("ltri", [128, 128], F32)
    onec = p.din("onec", [128, 1], F32)
    on2d = p.din("on2d", [1, 128], F32)
    ko = p.dout("keep", [128, NA // 128], F32)
    co = p.dout("cnt", [128, NA // 128], F32)
    pd = p.dout("pred", [128, (NA // 128) * 8], F32)
    a_sb = p.sbuf("a_sb", [128, NCH * 8], F32)
    lt_sb = p.sbuf("lt_sb", [128, 128], F32)
    on_sb = p.sbuf("on_sb", [128, 1], F32)
    on2 = p.sbuf("on2", [1, 128], F32)
    offs = p.sbuf("offs", [1, 16], F32)
    pre = p.sbuf("pre", [128, 8], F32)
    tt = p.sbuf("tt", [128, 24], F32)
    kout = p.sbuf("kout", [128, NCH], F32)
    cout = p.sbuf("cout", [128, NCH], F32)
    pall = p.sbuf("pall", [128, NCH * 8], F32)
    ps1 = p.psum("ps1", [128, 8], F32)
    ps2 = p.psum("ps2", [128, 8], F32)
    ps3 = p.psum("ps3", [1, 8], F32)
    p.dma("sync", a_sb[:], A, "ld")
    p.dma("sync", lt_sb[:], ltri, "ld")
    p.dma("sync", on_sb[:], onec, "ld")
    p.dma("sync", on2[:], on2d, "ld")
    ldv = p.cnt["ld"]
    p.wait("tensor", "ld", ldv)
    p.wait("vector", "ld", ldv)
    p.on("vector", lambda e: e.memset(offs[:], 0.0), sem="of")
    V = "v"
    def vop(fn):
        p.on("vector", fn, sem=V)
        p.wait("vector", V, p.cnt[V])
    for i in range(NCH):
        ac = chunk(a_sb, i, 8)
        if i > 0:
            p.wait("tensor", V, p.cnt[V])
        p.on("tensor", lambda e, ac=ac: e.matmul(ps1[:], lt_sb[:], ac, start=True, stop=True))
        p.on("tensor", lambda e, ac=ac: e.matmul(ps3[:], on_sb[:], ac, start=True, stop=True))
        p.wait("tensor", "of", p.cnt["of"])
        p.on("tensor", lambda e: e.matmul(ps2[:], on2[:], offs[0:1, 0:8], start=True, stop=True),
             sem="mm")
        p.wait("vector", "mm", p.cnt["mm"])
        vop(lambda e: e.tensor_copy(pre[:], ps1[:]))
        vop(lambda e: e.tensor_tensor(pre[:], pre[:], ps2[:], ALU.add))
        vop(lambda e, i=i: e.tensor_copy(chunk(pall, i, 8), pre[:]))
        vop(lambda e, CAP=CAP: e.tensor_scalar(tt[:, 0:8], pre[:], CAP + 0.5, None, ALU.is_lt))
        vop(lambda e, ac=ac: e.tensor_tensor(tt[:, 8:16], tt[:, 0:8], ac, ALU.mult))
        vop(lambda e, i=i: e.tensor_reduce(kout[:, i:i + 1], tt[:, 8:16], AX.X, ALU.add))
        vop(lambda e, ac=ac: e.tensor_tensor(tt[:, 16:24], pre[:], ac, ALU.mult))
        vop(lambda e, i=i: e.tensor_reduce(cout[:, i:i + 1], tt[:, 16:24], AX.X, ALU.add))
        vop(lambda e: e.tensor_copy(offs[0:1, 8:16], ps3[0:1, :]))
        p.on("vector", lambda e: e.tensor_tensor(offs[0:1, 0:8], offs[0:1, 0:8],
                                                 offs[0:1, 8:16], ALU.add), sem="of")
        p.wait("vector", "of", p.cnt["of"])
    p.wait("sync", V, p.cnt[V])
    p.dma("sync", ko, kout[:], "sto")
    p.dma("sync", co, cout[:], "sto")
    p.dma("sync", pd, pall[:], "sto")
    p.wait("sync", "sto", p.cnt["sto"])
    return p.build()



def build_moe(Tn=T):
    p = Prog()
    KC = D // 128
    FC = 2048 // 128
    NT = Tn // 512
    xn = p.din("xn", [D, Tn], BF16)
    oh1 = p.din("oh1", [8, Tn], F32)
    oh2 = p.din("oh2", [8, Tn], F32)
    mrow = p.din("mrow", [1, 4 * Tn], F32)
    wg = p.din("wg", [8 * D, 2048], F32)
    wu = p.din("wu", [8 * D, 2048], F32)
    wd = p.din("wd", [8 * 2048, D], F32)
    nw = p.din("nw", [128, (D // 128) * 8], F32)
    fnorm = p.din("fnorm", [128, KC], F32)
    fwg = p.din("fwg", [D, 512], F32)
    fwu = p.din("fwu", [D, 512], F32)
    fwd = p.din("fwd", [512, D], F32)
    onec = p.din("onec", [128, 1], F32)
    on2d = p.din("on2d", [1, 128], F32)
    fo = p.dout("fo", [D, Tn], F32)
    mskd = p.dtmp("mskd", [8, Tn], BF16)

    xn_sb = p.sbuf("xn_sb", [128, KC * Tn], BF16)
    mr_sb = p.sbuf("mr_sb", [1, 4 * Tn], BF16)
    on2b = p.sbuf("on2b", [1, 128], BF16)
    sc_sb = p.sbuf("sc_sb", [1, Tn], F32)
    scb2 = p.sbuf("scb2", [1, 2 * Tn], BF16)
    nw_sb = p.sbuf("nw_sb", [128, KC * 8], F32)
    fn_sb = p.sbuf("fn_sb", [128, KC], F32)
    ze = p.sbuf("ze", [128, KC * Tn], BF16)
    gt = p.sbuf("gt", [128, FC * Tn], BF16)
    acc = p.sbuf("acc", [128, KC * Tn], BF16)
    u1 = p.sbuf("u1", [128, Tn], F32)
    mske = mr_sb[0:1, 0:Tn]
    fh = ze
    fg = gt
    rows_sb = gt.bitcast(F32)[0:8, 2 * Tn:4 * Tn]
    onec_sb = p.sbuf("onec_sb", [128, 1], F32)
    on2 = p.sbuf("on2", [1, 128], F32)
    wbuf0 = p.sbuf("wbuf0", [128, 16 * 128], BF16)
    wbuf = [wbuf0, wbuf0]
    ps1 = p.psum("ps1", [128, Tn], F32)
    ps2 = p.psum("ps2", [128, Tn], F32)

    load_fm(p, "sync", xn_sb, xn, KC, Tn, "ld")
    p.dma("sync", rows_sb[:, 0:Tn], oh1, "ld")
    p.dma("sync", rows_sb[:, Tn:2 * Tn], oh2, "ld")
    p.dma("gpsimd", mr_sb[:], mrow, "ldg")
    p.dma("sync", nw_sb[:], nw, "ld")
    p.dma("sync", fn_sb[:], fnorm, "ld")
    p.dma("sync", onec_sb[:], onec, "ld")
    p.dma("sync", on2[:], on2d, "ld")
    p.dma("gpsimd", on2b[:], on2d, "ldg")
    ldv = p.cnt["ld"]
    for eng in ("tensor", "vector", "scalar"):
        p.wait(eng, "ld", ldv)
        p.wait(eng, "ldg", p.cnt["ldg"])

    rstd = sc_sb[0:1, 0:Tn]
    sfac = scb2[0:1, 0:Tn]
    ovf = scb2[0:1, Tn:2 * Tn]
    V = "v"
    def vop(fn):
        p.on("vector", fn, sem=V)
        p.wait("vector", V, p.cnt[V])
    for c in range(KC):
        p.on("scalar", lambda e, c=c: e.activation(u1[:], chunk(xn_sb, c, Tn), AF.Square),
             sem="sq")
        p.wait("tensor", "sq", p.cnt["sq"])
        for n in range(NT):
            sl = slice(n * 512, (n + 1) * 512)
            p.on("tensor", lambda e, sl=sl, st=(c == 0), sp=(c == KC - 1):
                 e.matmul(ps1[0:1, sl], onec_sb[:], u1[:, sl], start=st, stop=sp),
                 sem="smm" if c == KC - 1 else ("sqr" if n == NT - 1 else None))
        if c < KC - 1:
            p.wait("scalar", "sqr", p.cnt["sqr"])
    p.wait("vector", "smm", p.cnt["smm"])
    vop(lambda e: e.tensor_scalar(rstd, ps1[0:1, :], 1.0 / D, 1e-6, ALU.mult, op1=ALU.add))
    vop(lambda e: e.reciprocal(rstd, rstd))
    p.wait("scalar", V, p.cnt[V])
    p.on("scalar", lambda e: e.activation(rstd, rstd, AF.Sqrt), sem="rsq")
    p.wait("vector", "rsq", 1)
    p.wait("tensor", "rsq", 1)
    m1r = mr_sb[0:1, 0:Tn]
    m2r = mr_sb[0:1, Tn:2 * Tn]
    k1r = mr_sb[0:1, 2 * Tn:3 * Tn]
    k2r = mr_sb[0:1, 3 * Tn:4 * Tn]
    vop(lambda e: e.tensor_tensor(ovf, k1r, k2r, ALU.mult))
    vop(lambda e: e.tensor_tensor(sfac, m1r, ovf, ALU.mult))
    w2k = m2r
    vop(lambda e: e.tensor_tensor(w2k, m2r, ovf, ALU.mult))
    p.wait("tensor", V, p.cnt[V])
    for n in range(Tn // 512):
        sl = slice(n * 512, (n + 1) * 512)
        p.on("tensor", lambda e, sl=sl: e.matmul(ps1[0:8, sl], on2b[0:1, 0:8], sfac[0:1, sl], start=True, stop=True))
        p.on("tensor", lambda e, sl=sl: e.matmul(ps2[0:8, sl], on2b[0:1, 0:8], w2k[0:1, sl], start=True, stop=True),
             sem="bcm")
    p.wait("vector", "bcm", p.cnt["bcm"])
    vop(lambda e: e.tensor_tensor(u1[0:8, 0:Tn], rows_sb[:, 0:Tn], ps1[0:8, :], ALU.mult))
    oh2t = ze.bitcast(F32)[0:8, 0:Tn]
    vop(lambda e: e.tensor_tensor(oh2t, rows_sb[:, Tn:2 * Tn], ps2[0:8, :], ALU.mult))
    vop(lambda e: e.tensor_tensor(u1[0:8, 0:Tn], u1[0:8, 0:Tn], oh2t, ALU.add))
    p.wait("gpsimd", V, p.cnt[V])
    p.dma("gpsimd", mskd[:, :], u1[0:8, 0:Tn], "mskst")
    vop(lambda e: e.tensor_tensor(sfac, sfac, w2k, ALU.add))
    vop(lambda e: e.tensor_tensor(sfac, sfac, ovf, ALU.add))
    vop(lambda e: e.tensor_scalar(ovf, ovf, -1.0, 1.0, ALU.mult, op1=ALU.add))
    p.wait("tensor", V, p.cnt[V])
    p.wait("scalar", V, p.cnt[V])

    for n in range(NT):
        sl = slice(n * 512, (n + 1) * 512)
        p.on("tensor", lambda e, sl=sl: e.matmul(ps1[:, sl], on2[:], rstd[0:1, sl],
                                                 start=True, stop=True), sem="rbc")
    p.wait("vector", "rbc", p.cnt["rbc"])
    for c in range(KC):
        p.on("vector", lambda e, c=c: e.scalar_tensor_tensor(
            chunk(fh, c, Tn), chunk(xn_sb, c, Tn), fn_sb[:, c:c + 1], ps1[:],
            ALU.mult, ALU.mult), sem="fhd")
    p.wait("tensor", "fhd", p.cnt["fhd"])
    p.wait("scalar", "mskst", p.cnt["mskst"])
    p.wait("vector", "mskst", p.cnt["mskst"])
    fgw_sb = wbuf0
    for i, wsrc in enumerate((fwg, fwu)):
        for m in range(4):
            if p.cnt.get("fmm", 0) > 0:
                p.wait("gpsimd", "fmm", p.cnt["fmm"])
            if p.cnt.get("fgm", 0) > 0:
                p.wait("gpsimd", "fgm", p.cnt["fgm"])
            p.dma("gpsimd", fgw_sb[:, 0:KC * 128].rearrange("p (kc mm) -> p kc mm", mm=128),
                  wsrc.rearrange("(kc p) mm -> p kc mm", p=128)[:, :, m * 128:(m + 1) * 128],
                  "fwl")
            p.wait("tensor", "fwl", 16 * (4 * i + m + 1))
            if 4 * i + m >= 1:
                p.wait("tensor", "fev", p.cnt["fev"])
            for k in range(KC):
                for n in range(NT):
                    sl = slice(n * 512, (n + 1) * 512)
                    p.on("tensor", lambda e, k=k, sl=sl, st=(k == 0), sp=(k == KC - 1):
                         e.matmul(ps2[:, sl], fgw_sb[:, k * 128:(k + 1) * 128],
                                  chunk(fh, k, Tn)[:, sl], start=st, stop=sp),
                         sem="fmm" if (k == KC - 1 and n == NT - 1) else None)
            p.wait("scalar", "fmm", p.cnt["fmm"])
            if 4 * i + m >= 1:
                p.wait("scalar", "fev", p.cnt["fev"])
            mi = 4 * i + m
            if i == 0:
                p.on("scalar", lambda e, m=m: e.activation(
                    chunk(fg, m, Tn), ps2[:], AF.Gelu), sem="fev")
            else:
                if p.cnt.get("fgm", 0) > 0:
                    p.wait("scalar", "fgm", p.cnt["fgm"])
                p.on("scalar", lambda e: e.activation(u1[:], ps2[:], AF.Copy), sem="fev")
                p.wait("vector", "fev", p.cnt["fev"])
                p.on("vector", lambda e, m=m: e.tensor_tensor(
                    chunk(fg, m, Tn), chunk(fg, m, Tn), u1[:], ALU.mult), sem="fgm")
    p.wait("tensor", "fgm", p.cnt["fgm"])
    fdw_sb = wbuf0
    for m in range(KC):
        p.wait("gpsimd", "fmm", p.cnt["fmm"])
        if p.cnt.get("dmm", 0) > 0:
            p.wait("gpsimd", "dmm", p.cnt["dmm"])
        p.dma("gpsimd", fdw_sb[:, 0:4 * 128].rearrange("p (kc mm) -> p kc mm", mm=128),
              fwd.rearrange("(kc p) mm -> p kc mm", p=128)[:, :, m * 128:(m + 1) * 128],
              "fdl")
        p.wait("tensor", "fdl", 16 * (m + 1))
        if m >= 1:
            p.wait("tensor", "acc", p.cnt["acc"])
        for k in range(4):
            for n in range(NT):
                sl = slice(n * 512, (n + 1) * 512)
                p.on("tensor", lambda e, k=k, sl=sl, st=(k == 0), sp=(k == 3):
                     e.matmul(ps2[:, sl], fdw_sb[:, k * 128:(k + 1) * 128],
                              chunk(fg, k, Tn)[:, sl], start=st, stop=sp),
                     sem="dmm" if (k == 3 and n == NT - 1) else None)
        for n in range(Tn // 512):
            sl = slice(n * 512, (n + 1) * 512)
            p.on("tensor", lambda e, sl=sl: e.matmul(ps1[:, sl], on2b[:], sfac[0:1, sl], start=True, stop=True),
                 sem="dmm" if n == Tn // 512 - 1 else None)
        p.wait("vector", "dmm", p.cnt["dmm"])
        p.on("vector", lambda e, m=m: e.tensor_tensor(u1[:], ps2[:], chunk(xn_sb, m, Tn), ALU.add), sem="du")
        p.wait("tensor", "du", p.cnt["du"])
        for n in range(Tn // 512):
            sl = slice(n * 512, (n + 1) * 512)
            p.on("tensor", lambda e, sl=sl: e.matmul(ps2[:, sl], on2b[:], ovf[0:1, sl], start=True, stop=True),
                 sem="dmm2" if n == Tn // 512 - 1 else None)
        p.wait("vector", "dmm2", p.cnt["dmm2"])
        p.on("vector", lambda e: e.tensor_tensor(u1[:], u1[:], ps2[:], ALU.mult), sem="du2")
        p.wait("vector", "du2", p.cnt["du2"])
        p.on("vector", lambda e, m=m: e.tensor_tensor(chunk(acc, m, Tn), chunk(xn_sb, m, Tn), ps1[:], ALU.mult), sem="du3")
        p.wait("vector", "du3", p.cnt["du3"])
        p.on("vector", lambda e, m=m: e.tensor_tensor(chunk(acc, m, Tn), chunk(acc, m, Tn), u1[:], ALU.add), sem="acc")
    p.wait("tensor", "acc", p.cnt["acc"])

    for ex in range(8):
        for n in range(NT):
            sl = slice(n * 512, (n + 1) * 512)
            p.on("tensor", lambda e, sl=sl: e.matmul(ps1[:, sl], on2[:], rstd[0:1, sl],
                                                     start=True, stop=True), sem="zbc")
        p.wait("vector", "zbc", p.cnt["zbc"])
        if ex > 0:
            p.wait("vector", "gmm", p.cnt["gmm"])
        for c in range(KC):
            p.on("vector", lambda e, c=c, ex=ex: e.scalar_tensor_tensor(
                chunk(ze, c, Tn), chunk(xn_sb, c, Tn), nw_sb[:, c * 8 + ex:c * 8 + ex + 1],
                ps1[:], ALU.mult, ALU.mult), sem="zed")
        p.wait("tensor", "zed", p.cnt["zed"])
        for i, wsrc in enumerate((wg, wu)):
            for m in range(FC):
                wb = wbuf[0]
                gl = "gl"
                nloads = p.cnt.get(gl, 0) // 16
                for sother in ("gmm", "dmm", "dmmx", "fmm"):
                    if p.cnt.get(sother, 0) > 0:
                        p.wait("gpsimd", sother, p.cnt[sother])
                p.dma("gpsimd", wb[:, 0:KC * 128].rearrange("p (kc mm) -> p kc mm", mm=128),
                      wsrc.rearrange("(ec kc p) mm -> p (ec kc) mm", p=128, kc=KC)
                      [:, ex * KC:(ex + 1) * KC, m * 128:(m + 1) * 128], gl)
                p.wait("tensor", gl, 16 * (nloads + 1))
                if p.cnt.get("gev", 0) > 0:
                    p.wait("tensor", "gev", p.cnt["gev"])
                if p.cnt.get("gum", 0) > 0:
                    p.wait("tensor", "gum", p.cnt["gum"])
                for k in range(KC):
                    for n in range(NT):
                        sl = slice(n * 512, (n + 1) * 512)
                        p.on("tensor", lambda e, wb=wb, k=k, sl=sl, st=(k == 0), sp=(k == KC - 1):
                             e.matmul(ps2[:, sl], wb[:, k * 128:(k + 1) * 128],
                                      chunk(ze, k, Tn)[:, sl], start=st, stop=sp),
                             sem="gmm" if (k == KC - 1 and n == NT - 1) else None)
                p.wait("scalar", "gmm", p.cnt["gmm"])
                if i == 0:
                    p.on("scalar", lambda e, m=m: e.activation(
                        chunk(gt, m, Tn), ps2[:], AF.Gelu), sem="gev")
                else:
                    if p.cnt.get("gum", 0) > 0:
                        p.wait("scalar", "gum", p.cnt["gum"])
                    if p.cnt.get("acd", 0) > 0:
                        p.wait("scalar", "acd", p.cnt["acd"])
                    p.on("scalar", lambda e: e.activation(u1[:], ps2[:], AF.Copy), sem="gev")
                    p.wait("vector", "gev", p.cnt["gev"])
                    p.on("vector", lambda e, m=m: e.tensor_tensor(
                        chunk(gt, m, Tn), chunk(gt, m, Tn), u1[:], ALU.mult), sem="gum")
        p.wait("vector", "gum", p.cnt["gum"])
        if ex > 0:
            p.wait("sync", "gma", p.cnt["gma"])
        else:
            p.wait("sync", "mskst", 16)
        p.dma("sync", mske[:], mskd[ex:ex + 1, :], "mskld")
        p.wait("tensor", "mskst", 16)
        p.wait("tensor", "mskld", 16 * (ex + 1))
        for n in range(NT):
            sl = slice(n * 512, (n + 1) * 512)
            p.on("tensor", lambda e, sl=sl: e.matmul(
                ps1[:, sl], on2b[:], mske[0:1, sl], start=True, stop=True), sem="mbc")
        p.wait("vector", "mbc", p.cnt["mbc"])
        for m in range(FC):
            p.on("vector", lambda e, m=m: e.tensor_tensor(
                chunk(gt, m, Tn), chunk(gt, m, Tn), ps1[:], ALU.mult), sem="gma")
        p.wait("tensor", "gma", p.cnt["gma"])
        for m in range(KC):
            wb = wbuf[0]
            dl = "dl"
            ndl = p.cnt.get(dl, 0) // 16
            for sother in ("gmm", "dmmx"):
                if p.cnt.get(sother, 0) > 0:
                    p.wait("gpsimd", sother, p.cnt[sother])
            p.dma("gpsimd", wb[:].rearrange("p (kc mm) -> p kc mm", mm=128),
                  wd.rearrange("(ec kc p) mm -> p (ec kc) mm", p=128, kc=FC)
                  [:, ex * FC:(ex + 1) * FC, m * 128:(m + 1) * 128], dl)
            p.wait("tensor", dl, 16 * (ndl + 1))
            if p.cnt.get("acd", 0) > 0:
                p.wait("tensor", "acd", p.cnt["acd"])
            for k in range(FC):
                for n in range(NT):
                    sl = slice(n * 512, (n + 1) * 512)
                    p.on("tensor", lambda e, wb=wb, k=k, sl=sl, st=(k == 0), sp=(k == FC - 1):
                         e.matmul(ps2[:, sl], wb[:, k * 128:(k + 1) * 128],
                                  chunk(gt, k, Tn)[:, sl], start=st, stop=sp),
                         sem="dmmx" if (k == FC - 1 and n == NT - 1) else None)
            p.wait("vector", "dmmx", p.cnt["dmmx"])
            p.on("vector", lambda e, m=m: e.tensor_tensor(
                chunk(acc, m, Tn), chunk(acc, m, Tn), ps2[:], ALU.add), sem="acd")
    p.wait("scalar", "acd", p.cnt["acd"])
    for c in range(KC):
        p.on("scalar", lambda e, c=c: e.activation(u1[:], chunk(acc, c, Tn), AF.Copy),
             sem="fcp")
        p.wait("sync", "fcp", p.cnt["fcp"])
        p.dma("sync", fo[c * 128:(c + 1) * 128, :], u1[:], "sto")
        p.wait("scalar", "sto", p.cnt["sto"])
    p.wait("sync", "sto", p.cnt["sto"])
    return p.build()


def build_combine_direct(k1, bias1=True, Tn=T):
    p = Prog()
    K1, MC = k1 // 128, D // 128
    x = p.din("x", [D, Tn], F32)
    a = p.din("a", [k1, Tn], BF16)
    f = p.din("f", [D, Tn], F32)
    w1 = p.din("w1", [k1, D], F32)
    xo = p.dout("xo", [D, Tn], F32)
    a_sb = p.sbuf("a_sb", [128, K1 * Tn], BF16)
    x_sb = p.sbuf("x_sb", [128, Tn], F32)
    f_sb = p.sbuf("f_sb", [128, Tn], F32)
    e1 = p.sbuf("e1", [128, Tn], F32)
    o_sb = p.sbuf("o_sb", [128, Tn], F32)
    ps = [p.psum("pspool0", [128, Tn], F32), p.psum("pspool1", [128, Tn], F32)]
    load_fm(p, "sync", a_sb, a, K1, Tn, "ld")
    b_sb = None
    if bias1:
        b = p.din("b1", [128, MC], F32)
        b_sb = p.sbuf("b_sb", [128, MC], F32)
        p.dma("sync", b_sb[:], b, "ld")
    ldv = p.cnt["ld"]
    p.wait("tensor", "ld", ldv)
    p.wait("scalar", "ld", ldv)
    wsb = [p.sbuf(f"w1_{i}", [128, K1 * 128], BF16) for i in range(2)]
    NT = Tn // 512
    for m in range(MC):
        if m >= 2:
            p.wait("gpsimd", "mma", m - 1)
        p.dma("gpsimd", wsb[m % 2][:].rearrange("p (kc mm) -> p kc mm", mm=128),
              w1.rearrange("(kc p) mm -> p kc mm", p=128)[:, :, m * 128:(m + 1) * 128],
              f"wl{m % 2}")
        p.dma("sync", x_sb[:], x[m * 128:(m + 1) * 128, :], "ldx")
        p.dma("sync", f_sb[:], f[m * 128:(m + 1) * 128, :], "ldx")
        p.wait("tensor", f"wl{m % 2}", 16 * (m // 2 + 1))
        if m >= 2:
            p.wait("tensor", "cmb", m - 1)
        pt = ps[m % 2]
        for k in range(K1):
            for n in range(NT):
                sl = slice(n * 512, (n + 1) * 512)
                p.on("tensor", lambda e, pt=pt, m=m, k=k, sl=sl, st=(k == 0), sp=(k == K1 - 1):
                     e.matmul(pt[:, sl], wsb[m % 2][:, k * 128:(k + 1) * 128],
                              chunk(a_sb, k, Tn)[:, sl], start=st, stop=sp),
                     sem="mma" if (k == K1 - 1 and n == NT - 1) else None)
        p.wait("scalar", "mma", p.cnt["mma"])
        if m >= 1:
            p.wait("scalar", "cmb", p.cnt["cmb"])
        kw = {"bias": b_sb[:, m:m + 1]} if b_sb is not None else {}
        p.on("scalar", lambda e, pt=pt, kw=kw: e.activation(
            e1[:], pt[:], AF.Identity if kw else AF.Copy, **kw), sem="ev1")
        p.wait("vector", "ev1", p.cnt["ev1"])
        p.wait("vector", "ldx", 32 * (m + 1))
        if m >= 1:
            p.wait("vector", "sto", 16 * m)
        p.on("vector", lambda e: e.tensor_tensor(o_sb[:], e1[:], x_sb[:], ALU.add), sem="cmb0")
        p.wait("vector", "cmb0", p.cnt["cmb0"])
        p.on("vector", lambda e: e.tensor_tensor(o_sb[:], o_sb[:], f_sb[:], ALU.add), sem="cmb")
        p.wait("sync", "cmb", p.cnt["cmb"])
        p.dma("sync", xo[m * 128:(m + 1) * 128, :], o_sb[:], "sto")
    p.wait("sync", "sto", p.cnt["sto"])
    return p.build()



IOTA8 = np.ascontiguousarray(np.tile(np.arange(8, dtype=np.float32), (128, 1)))
LTRI = np.ascontiguousarray(np.triu(np.ones((128, 128), np.float32)))

def run_layer3(x_fm_shards, inp):
    xn, xn_f = run_ln(x_fm_shards, want_f32=True)
    nc = cached(("l3qkv",), build_projN, D,
                [{"out": "q", "m": D, "bias": True, "featmap": True},
                 {"out": "k", "m": D, "bias": True, "featmap": True},
                 {"out": "v", "m": D, "bias": True}])
    wq = fm(inp["L3_attn_Wq"]).astype(np.float32)
    wk = fm(inp["L3_attn_Wk"]).astype(np.float32)
    wv = fm(inp["L3_attn_Wv"]).astype(np.float32)
    res = run(nc, [{"x": np.asarray(xn[c], dtype=BF), "w0": wq, "w1": wk, "w2": wv,
                    "b0": tile_vec(inp["L3_attn_bq"]), "b1": tile_vec(inp["L3_attn_bk"]),
                    "b2": tile_vec(inp["L3_attn_bv"])} for c in range(NC)])
    qs, ks, vs = [r["q"] for r in res], [r["k"] for r in res], [r["v"] for r in res]
    nc = cached(("linattn",), build_linattn)
    ims = []
    for c in range(NC):
        ktm = np.ascontiguousarray(np.asarray(pair_cat(ks, c)).T)
        vt = make_vt(pair_cat(vs, c).astype(np.float32))
        ims.append({"q": qs[c], "ktm": ktm, "vt": vt, "on2d": ONES_ROW})
    ctxs = [r["ctx"] for r in run(nc, ims)]
    nc = cached(("router",), build_router)
    wr = np.ascontiguousarray(inp["L3_ffn_Wr"], dtype=np.float32)
    res = run(nc, [{"x": np.ascontiguousarray(xn_f[c], dtype=np.float32), "wr": wr,
                    "iot": IOTA8} for c in range(NC)])
    routes = [np.asarray(r["route"]) for r in res]
    oh_all = np.concatenate([r[:, 0:16].reshape(T, 2, 8) for r in routes], axis=0)
    A = oh_all.reshape(2 * NC * T, 8)
    A_t = np.ascontiguousarray(A.reshape(-1, 128, 8).transpose(1, 0, 2).reshape(128, -1))
    nc = cached(("routing",), build_routing)
    res = run(nc, [{"A": A_t, "ltri": LTRI, "onec": ONES_COL, "on2d": ONES_ROW}
                   for _ in range(NC)])
    keep = np.asarray(res[0]["keep"]).T.reshape(-1)
    k1_all = keep[0::2].reshape(NC, T)
    k2_all = keep[1::2].reshape(NC, T)
    nc = cached(("moe",), build_moe)
    wg = np.ascontiguousarray(inp["L3_ffn_Wg"].reshape(8 * D, 2048), dtype=np.float32)
    wu = np.ascontiguousarray(inp["L3_ffn_Wu"].reshape(8 * D, 2048), dtype=np.float32)
    wd = np.ascontiguousarray(inp["L3_ffn_Wd"].reshape(8 * 2048, D), dtype=np.float32)
    nwm = np.asarray(inp["L3_ffn_norm_w"], dtype=np.float32)
    nwt = np.zeros((128, (D // 128) * 8), np.float32)
    for kc in range(D // 128):
        for e in range(8):
            nwt[:, kc * 8 + e] = nwm[e, kc * 128:(kc + 1) * 128]
    nwt = np.ascontiguousarray(nwt)
    fwg = np.ascontiguousarray(inp["L3_ffn_fWg"], dtype=np.float32)
    fwu = np.ascontiguousarray(inp["L3_ffn_fWu"], dtype=np.float32)
    fwd = np.ascontiguousarray(inp["L3_ffn_fWd"], dtype=np.float32)
    fnorm = tile_vec(inp["L3_ffn_f_norm"])
    ims = []
    for c in range(NC):
        rt = routes[c]
        mrow = np.concatenate([rt[:, 16], rt[:, 17], k1_all[c], k2_all[c]]).reshape(1, -1)
        ims.append({"xn": np.asarray(xn[c], dtype=BF),
                    "oh1": np.ascontiguousarray(rt[:, 0:8].T),
                    "oh2": np.ascontiguousarray(rt[:, 8:16].T),
                    "mrow": np.ascontiguousarray(mrow.astype(np.float32)),
                    "wg": wg, "wu": wu, "wd": wd, "nw": nwt, "fnorm": fnorm,
                    "fwg": fwg, "fwu": fwu, "fwd": fwd,
                    "onec": ONES_COL, "on2d": ONES_ROW})
    fos = [r["fo"] for r in run(nc, ims)]
    nc = cached(("combd", D, True), build_combine_direct, D, True)
    wo = fm(inp["L3_attn_Wo"]).astype(np.float32)
    bo = tile_vec(inp["L3_attn_bo"])
    res = run(nc, [{"x": np.ascontiguousarray(x_fm_shards[c], dtype=np.float32),
                    "a": ctxs[c], "f": np.ascontiguousarray(fos[c]),
                    "w1": wo, "b1": bo} for c in range(NC)])
    return [r["xo"] for r in res]



def kernel(**inputs):
    x = np.asarray(inputs["L0_attn_x"], dtype=np.float32)
    shards = [np.ascontiguousarray(fm(s)) for s in shard_tokens(x)]
    shards = run_layer0(shards, inputs)
    shards = [np.ascontiguousarray(np.asarray(s, dtype=np.float32)) for s in shards]
    shards = run_layer1(shards, inputs)
    shards = [np.ascontiguousarray(np.asarray(s, dtype=np.float32)) for s in shards]
    shards = run_layer2(shards, inputs)
    shards = [np.ascontiguousarray(np.asarray(s, dtype=np.float32)) for s in shards]
    shards = run_layer3(shards, inputs)
    out = np.empty((4, 4096, D), dtype=np.float32)
    for c in range(NC):
        out[c // 2, (c % 2) * T:(c % 2 + 1) * T, :] = np.asarray(shards[c], dtype=np.float32).T
    return out


# version 02 (final)
# speedup vs baseline: 1.0801x; 1.0801x over previous
import numpy as np
import ml_dtypes
import concourse.bass as bass
import concourse.mybir as mybir
from concourse.bass_utils import run_bass_kernel_spmd

F32 = mybir.dt.float32
BF16 = mybir.dt.bfloat16
I32 = mybir.dt.int32
AF = mybir.ActivationFunctionType
ALU = mybir.AluOpType
AX = mybir.AxisListType

NC = 8
T = 2048
D = 1024
BF = ml_dtypes.bfloat16


class Prog:
    ENGS = ("sync", "gpsimd", "tensor", "vector", "scalar")

    def __init__(self):
        import contextlib
        self.nc = bass.Bass("TRN2", target_bir_lowering=False)
        self.stack = contextlib.ExitStack()
        self.q = {e: [] for e in self.ENGS}
        self.sems = {}
        self.cnt = {}

    def din(self, name, shape, dtype):
        return self.nc.dram_tensor(name, list(shape), dtype, kind="ExternalInput").ap()

    def dout(self, name, shape, dtype):
        return self.nc.dram_tensor(name, list(shape), dtype, kind="ExternalOutput").ap()

    def dtmp(self, name, shape, dtype):
        return self.nc.dram_tensor(name, list(shape), dtype).ap()

    def sbuf(self, name, shape, dtype):
        return self.stack.enter_context(self.nc.sbuf_tensor(name, list(shape), dtype))

    def psum(self, name, shape, dtype=F32):
        return self.stack.enter_context(self.nc.psum_tensor(name, list(shape), dtype))

    def sem(self, name):
        if name not in self.sems:
            self.sems[name] = self.stack.enter_context(self.nc.semaphore(name))
            self.cnt[name] = 0
        return self.sems[name]

    def on(self, eng, fn, sem=None, inc=1):
        if sem is not None:
            s = self.sem(sem)
            self.cnt[sem] += inc
            self.q[eng].append(lambda e, fn=fn, s=s, inc=inc: fn(e).then_inc(s, inc))
            return self.cnt[sem]
        self.q[eng].append(fn)
        return None

    def dma(self, eng, out, in_, sem):
        return self.on(eng, lambda e, o=out, i=in_: e.dma_start(out=o, in_=i), sem, 16)

    def wait(self, eng, sem, val):
        s = self.sem(sem)
        self.q[eng].append(lambda e, s=s, v=val: e.wait_ge(s, v))

    def build(self):
        nc = self.nc
        qs = self.q
        with nc.Block() as block:
            @block.sync
            def _(e):
                for f in qs["sync"]:
                    f(e)
            @block.gpsimd
            def _(e):
                for f in qs["gpsimd"]:
                    f(e)
            @block.tensor
            def _(e):
                for f in qs["tensor"]:
                    f(e)
            @block.vector
            def _(e):
                for f in qs["vector"]:
                    f(e)
            @block.scalar
            def _(e):
                for f in qs["scalar"]:
                    f(e)
        self.stack.close()
        return nc



def fm(a):
    return np.ascontiguousarray(np.asarray(a).T)

def tile_vec(v, dtype=np.float32):
    return np.ascontiguousarray(np.asarray(v).reshape(-1, 128).T.astype(dtype))

def shard_tokens(x):
    return [np.asarray(x[c // 2, (c % 2) * T:(c % 2 + 1) * T, :]) for c in range(NC)]

_prog_cache = {}

def cached(key, builder, *args, **kw):
    if key not in _prog_cache:
        _prog_cache[key] = builder(*args, **kw)
    return _prog_cache[key]

def run(nc, in_maps, **kw):
    return run_bass_kernel_spmd(nc, in_maps, core_ids=list(range(NC)), **kw).results



def load_fm(p, eng, sb, dram, nchunks, ncols, sem, dcol0=0, dt_cols=None):
    for c in range(nchunks):
        p.dma(eng, sb[:, c * ncols:(c + 1) * ncols],
              dram[c * 128:(c + 1) * 128, dcol0:dcol0 + ncols], sem)
    return p.cnt[sem]

def chunk(sb, c, ncols):
    return sb[:, c * ncols:(c + 1) * ncols]


def emit_proj(p, *, x_sb, kchunks, mchunks, Tn, w_dram, out_cb,
              wq_eng="gpsimd", tag="", nbuf=2, kc0=0):
    NT = max(1, Tn // 512)
    nt = min(Tn, 512)
    wld, mm, ev = "wl" + tag, "mm" + tag, "ev" + tag
    wsb = [p.sbuf(f"w{tag}_{i}", [128, kchunks * 128], BF16) for i in range(nbuf)]
    ps = [p.psum(f"pp{tag}_{i}", [128, Tn], F32) for i in range(2)]
    for m in range(mchunks):
        buf = wsb[m % nbuf]
        if m >= nbuf:
            p.wait(wq_eng, mm, m - nbuf + 1)
        src = w_dram.rearrange("(kc p) m -> p kc m", p=128)[:, :, m * 128:(m + 1) * 128]
        p.dma(wq_eng, buf[:].rearrange("p (kc m) -> p kc m", m=128), src, wld)
        p.wait("tensor", wld, 16 * (m + 1))
        pt = ps[m % 2]
        if m >= 2:
            p.wait("tensor", ev, m - 1)
        for k in range(kchunks):
            for n in range(NT):
                sl = slice(n * nt, (n + 1) * nt)
                p.on("tensor",
                     lambda e, pt=pt, b=buf, k=k, sl=sl, st=(k == 0), sp=(k == kchunks - 1):
                     e.matmul(pt[:, sl], b[:, k * 128:(k + 1) * 128],
                              chunk(x_sb, kc0 + k, Tn)[:, sl], start=st, stop=sp),
                     sem=mm if (k == kchunks - 1 and n == NT - 1) else None)
        out_cb(m, pt, p.cnt[mm])
    return ev


def simple_evac(p, youts_dram, o_sb, Tn, act=None, b_sb=None, s_sb=None, tag="",
                out_rows=128):
    def out_cb(m, pt, mmv):
        ev, sto = "ev" + tag, "sto" + tag
        p.wait("scalar", "mm" + tag, mmv)
        if m >= 1:
            p.wait("scalar", sto, 16 * m)
        kw = {}
        if b_sb is not None:
            kw["bias"] = b_sb[:, m:m + 1]
        if s_sb is not None:
            kw["scale"] = s_sb[:, m:m + 1]
        p.on("scalar", lambda e, pt=pt, kw=kw: e.activation(
            o_sb[0:out_rows, :], pt[0:out_rows, :], act if act else AF.Copy, **kw),
            sem=ev)
        p.wait("sync", ev, p.cnt[ev])
        p.dma("sync", youts_dram[m * 128:(m + 1) * 128, :], o_sb[0:out_rows, :], sto)
    return out_cb



def build_ln(want_f32=False):
    p = Prog()
    x = p.din("x", [D, T], F32)
    xn_bf = p.dout("xn_bf", [D, T], BF16)
    xn_f = p.dout("xn_f", [D, T], F32) if want_f32 else None
    dbg = p.dout("dbg", [1, 2 * T], F32)
    KC = D // 128
    cst = p.din("cst", [128, 1], F32)
    on2d = p.din("on2d", [1, 128], F32)
    x_sb = p.sbuf("x_sb", [128, KC * T], F32)
    sq = p.sbuf("sq", [128, T], F32)
    ob = p.sbuf("ob", [128, T], BF16)
    of = p.sbuf("of", [128, T], F32)
    ones = p.sbuf("ones", [128, 1], F32)
    stat = p.sbuf("stat", [1, 2 * T], F32)
    t1 = p.sbuf("t1", [1, 2 * T], F32)
    tmp = p.sbuf("tmp", [128, T], F32)
    on2 = p.sbuf("on2", [1, 128], F32)
    ps_a = p.psum("ps_a", [128, T], F32)
    ps_b = p.psum("ps_b", [128, T], F32)
    ps_s = ps_a[0:1, :]
    ps_q = ps_b[0:1, :]
    ps_m = ps_a
    ps_r = ps_b

    p.dma("gpsimd", ones[:], cst, "ms")
    p.dma("gpsimd", on2[:], on2d, "ms")
    load_fm(p, "sync", x_sb, x, KC, T, "ld")
    p.wait("tensor", "ms", 32)
    p.wait("scalar", "ld", p.cnt["ld"])
    p.wait("tensor", "ld", p.cnt["ld"])
    for c in range(KC):
        if c > 0:
            p.wait("scalar", "sqm", c)
        p.on("scalar", lambda e, c=c: e.activation(sq[:], chunk(x_sb, c, T), AF.Square),
             sem="sqd")
        p.wait("tensor", "sqd", c + 1)
        for n in range(T // 512):
            sl = slice(n * 512, (n + 1) * 512)
            p.on("tensor", lambda e, c=c, sl=sl, st=(c == 0), sp=(c == KC - 1):
                 e.matmul(ps_s[:, sl], ones[:], chunk(x_sb, c, T)[:, sl], start=st, stop=sp),
                 sem="mms" if c == KC - 1 else None)
            p.on("tensor", lambda e, sl=sl, st=(c == 0), sp=(c == KC - 1):
                 e.matmul(ps_q[:, sl], ones[:], sq[:, sl], start=st, stop=sp),
                 sem="mms" if c == KC - 1 else ("sqm" if n == T // 512 - 1 else None))
    mean = stat[0:1, 0:T]
    rstd = stat[0:1, T:2 * T]
    va = t1[0:1, 0:T]
    vb = t1[0:1, T:2 * T]
    p.wait("vector", "mms", p.cnt["mms"])
    p.on("vector", lambda e: e.tensor_scalar(mean, ps_s, 1.0 / D, None, ALU.mult))
    p.on("vector", lambda e: e.tensor_scalar(va, ps_q, 1.0 / D, None, ALU.mult))
    p.on("vector", lambda e: e.tensor_tensor(vb, mean, mean, ALU.mult))
    p.on("vector", lambda e: e.tensor_tensor(va, va, vb, ALU.subtract))
    p.on("vector", lambda e: e.tensor_scalar(va, va, 1e-5, None, ALU.add))
    p.on("vector", lambda e: e.reciprocal(vb, va), sem="st1")
    p.wait("scalar", "st1", p.cnt["st1"])
    p.on("scalar", lambda e: e.activation(rstd, vb, AF.Sqrt), sem="st2")
    p.wait("tensor", "st2", 1)
    for n in range(T // 512):
        sl = slice(n * 512, (n + 1) * 512)
        p.on("tensor", lambda e, sl=sl: e.matmul(ps_m[:, sl], on2[:], mean[:, sl],
                                                 start=True, stop=True))
        p.on("tensor", lambda e, sl=sl: e.matmul(ps_r[:, sl], on2[:], rstd[:, sl],
                                                 start=True, stop=True), sem="bc")
    p.wait("vector", "bc", p.cnt["bc"])
    for c in range(KC):
        if c > 0:
            p.wait("vector", "sto", 16 * (2 * c if want_f32 else c))
        p.on("vector", lambda e, c=c: e.tensor_tensor(tmp[:], chunk(x_sb, c, T), ps_m[:], ALU.subtract))
        p.on("vector", lambda e: e.tensor_tensor(of[:], tmp[:], ps_r[:], ALU.mult))
        p.on("vector", lambda e: e.tensor_copy(ob[:], of[:]), sem="xn")
        p.wait("sync", "xn", p.cnt["xn"])
        p.dma("sync", xn_bf[c * 128:(c + 1) * 128, :], ob[:], "sto")
        if want_f32:
            p.dma("sync", xn_f[c * 128:(c + 1) * 128, :], of[:], "sto")
    p.wait("sync", "xn", 1)
    p.dma("sync", dbg, stat[0:1, :], "sto")
    p.wait("sync", "sto", p.cnt["sto"])
    return p.build()


ONES_COL = np.ones((128, 1), dtype=np.float32)
ONES_ROW = np.ones((1, 128), dtype=np.float32)

def run_ln(x_fm_shards, want_f32=False):
    nc = cached(("ln", want_f32), build_ln, want_f32)
    res = run(nc, [{"x": np.ascontiguousarray(s), "cst": ONES_COL, "on2d": ONES_ROW}
                   for s in x_fm_shards])
    if want_f32:
        return [r["xn_bf"] for r in res], [r["xn_f"] for r in res]
    return [r["xn_bf"] for r in res]



def emit_proj2(p, *, x_sb, kchunks, mchunks, Tn, w_dram, out_cb, ps,
               wq_eng="gpsimd", tag="", nbuf=2, kc0=0):
    NT = max(1, Tn // 512)
    nt = min(Tn, 512)
    mm, ev = "mm" + tag, "ev" + tag
    wsb = [p.sbuf(f"w{tag}_{i}", [128, kchunks * 128], BF16) for i in range(nbuf)]
    for m in range(mchunks):
        buf = wsb[m % nbuf]
        if m >= nbuf:
            p.wait(wq_eng, mm, m - nbuf + 1)
        src = w_dram.rearrange("(kc p) m -> p kc m", p=128)[:, :, m * 128:(m + 1) * 128]
        wld = f"wl{tag}_{m % nbuf}"
        p.dma(wq_eng, buf[:].rearrange("p (kc m) -> p kc m", m=128), src, wld)
        p.wait("tensor", wld, 16 * (m // nbuf + 1))
        pt = ps[m % 2]
        if m >= 2:
            p.wait("tensor", ev, m - 1)
        for k in range(kchunks):
            for n in range(NT):
                sl = slice(n * nt, (n + 1) * nt)
                p.on("tensor",
                     lambda e, pt=pt, b=buf, k=k, sl=sl, st=(k == 0), sp=(k == kchunks - 1):
                     e.matmul(pt[:, sl], b[:, k * 128:(k + 1) * 128],
                              chunk(x_sb, kc0 + k, Tn)[:, sl], start=st, stop=sp),
                     sem=mm if (k == kchunks - 1 and n == NT - 1) else None)
        out_cb(m, pt, p.cnt[mm])
    return ev


def build_projN(kdim, specs, Tn=T):
    p = Prog()
    KC = kdim // 128
    x = p.din("x", [kdim, Tn], BF16)
    x_sb = p.sbuf("x_sb", [128, KC * Tn], BF16)
    load_fm(p, "sync", x_sb, x, KC, Tn, "ld")
    ps = [p.psum("pspool0", [128, Tn], F32), p.psum("pspool1", [128, Tn], F32)]
    prev_ev = None
    any_fm = any(s.get("featmap") for s in specs)
    if any_fm:
        zf = p.sbuf("zf", [128, Tn], F32)
        tf = p.sbuf("tf", [128, Tn], F32)
        ef = p.sbuf("ef", [128, Tn], F32)
    for i, s in enumerate(specs):
        M = s["m"]
        MC = M // 128
        tag = f"p{i}"
        w = p.din(f"w{i}", [kdim, M], s.get("wdt", F32))
        y = p.dout(s["out"], [M, Tn], BF16)
        b_sb = None
        if s.get("bias"):
            b = p.din(f"b{i}", [128, MC], F32)
            b_sb = p.sbuf(f"b_sb{i}", [128, MC], F32)
            p.dma("sync", b_sb[:], b, "ld")
        s_sb = None
        if s.get("scale"):
            sc = p.din(f"s{i}", [128, MC], F32)
            s_sb = p.sbuf(f"s_sb{i}", [128, MC], F32)
            p.dma("sync", s_sb[:], sc, "ld")
        o_sb = p.sbuf(f"o_sb{i}", [128, Tn], BF16)
        ldv = p.cnt["ld"]
        p.wait("tensor", "ld", ldv)
        p.wait("scalar", "ld", ldv)
        if prev_ev is not None:
            p.wait("tensor", prev_ev, p.cnt[prev_ev])
            p.wait("scalar", prev_ev, p.cnt[prev_ev])

        def out_cb(m, pt, mmv, s=s, tag=tag, b_sb=b_sb, s_sb=s_sb, o_sb=o_sb, y=y):
            ev, sto = "ev" + tag, "sto" + tag
            p.wait("scalar", "mm" + tag, mmv)
            if m >= 1:
                p.wait("scalar", sto, 16 * m)
            kw = {}
            if b_sb is not None:
                kw["bias"] = b_sb[:, m:m + 1]
            if s_sb is not None:
                kw["scale"] = s_sb[:, m:m + 1]
            if s.get("featmap"):
                if m >= 1:
                    p.wait("scalar", ev, m)
                p.on("scalar", lambda e, pt=pt, kw=kw: e.activation(
                    zf[:], pt[:], AF.Identity if kw else AF.Copy, **kw), sem="fz" + tag)
                p.wait("vector", "fz" + tag, p.cnt["fz" + tag])
                if m >= 1:
                    p.wait("vector", sto, 16 * m)
                p.on("vector", lambda e: e.tensor_scalar(
                    tf[:], zf[:], 0.0, None, ALU.min), sem="fm" + tag)
                p.wait("scalar", "fm" + tag, p.cnt["fm" + tag])
                p.on("scalar", lambda e: e.activation(ef[:], tf[:], AF.Exp),
                     sem="fe" + tag)
                p.wait("vector", "fe" + tag, p.cnt["fe" + tag])
                p.on("vector", lambda e: e.scalar_tensor_tensor(
                    o_sb[:], zf[:], 0.0, ef[:], ALU.max, ALU.add), sem=ev)
            else:
                fn = s.get("act") or (AF.Identity if kw else AF.Copy)
                p.on("scalar", lambda e, pt=pt, kw=kw, fn=fn: e.activation(
                    o_sb[:], pt[:], fn, **kw), sem=ev)
            p.wait("sync", ev, p.cnt[ev])
            p.dma("sync", y[m * 128:(m + 1) * 128, :], o_sb[:], sto)

        emit_proj2(p, x_sb=x_sb, kchunks=KC, mchunks=MC, Tn=Tn, w_dram=w,
                   out_cb=out_cb, ps=ps, tag=tag)
        prev_ev = "ev" + tag
    for i in range(len(specs)):
        p.wait("sync", "sto" + f"p{i}", p.cnt["sto" + f"p{i}"])
    return p.build()


def build_gateup(kdim, M, act0=None, scale01=False, Tn=T, out_dt=BF16,
                 wdt0=F32, wdt1=F32, bias01=False):
    p = Prog()
    KC = kdim // 128
    MC = M // 128
    x = p.din("x", [kdim, Tn], BF16)
    w0 = p.din("w0", [kdim, M], wdt0)
    w1 = p.din("w1", [kdim, M], wdt1)
    y = p.dout("y", [M, Tn], out_dt)
    x_sb = p.sbuf("x_sb", [128, KC * Tn], BF16)
    g_sb = p.sbuf("g_sb", [128, MC * Tn], BF16)
    o_sb = p.sbuf("o_sb", [128, Tn], out_dt)
    u_sb = p.sbuf("u_sb", [128, Tn], F32)
    ps = [p.psum("pspool0", [128, Tn], F32), p.psum("pspool1", [128, Tn], F32)]
    load_fm(p, "sync", x_sb, x, KC, Tn, "ld")
    s_sb = [None, None]
    b_sb = [None, None]
    if scale01:
        for i in range(2):
            sc = p.din(f"s{i}", [128, MC], F32)
            s_sb[i] = p.sbuf(f"s_sb{i}", [128, MC], F32)
            p.dma("sync", s_sb[i][:], sc, "ld")
    if bias01:
        for i in range(2):
            b = p.din(f"b{i}", [128, MC], F32)
            b_sb[i] = p.sbuf(f"b_sb{i}", [128, MC], F32)
            p.dma("sync", b_sb[i][:], b, "ld")
    ldv = p.cnt["ld"]
    p.wait("tensor", "ld", ldv)
    p.wait("scalar", "ld", ldv)

    def cb_gate(m, pt, mmv):
        p.wait("scalar", "mmg", mmv)
        kw = {}
        if s_sb[0] is not None:
            kw["scale"] = s_sb[0][:, m:m + 1]
        if b_sb[0] is not None:
            kw["bias"] = b_sb[0][:, m:m + 1]
        p.on("scalar", lambda e, pt=pt, m=m, kw=kw: e.activation(
            chunk(g_sb, m, Tn), pt[:], act0 or (AF.Identity if kw else AF.Copy), **kw), sem="evg")

    emit_proj2(p, x_sb=x_sb, kchunks=KC, mchunks=MC, Tn=Tn, w_dram=w0,
               out_cb=cb_gate, ps=ps, tag="g")
    p.wait("tensor", "evg", p.cnt["evg"])

    def cb_up(m, pt, mmv):
        p.wait("scalar", "mmu", mmv)
        if m >= 1:
            p.wait("scalar", "stou", 16 * m)
        kw = {}
        if s_sb[1] is not None:
            kw["scale"] = s_sb[1][:, m:m + 1]
        if b_sb[1] is not None:
            kw["bias"] = b_sb[1][:, m:m + 1]
        p.on("scalar", lambda e, pt=pt, kw=kw: e.activation(
            u_sb[:], pt[:], AF.Identity if kw else AF.Copy, **kw), sem="evu0")
        p.wait("vector", "evu0", p.cnt["evu0"])
        if m >= 1:
            p.wait("vector", "stou", 16 * m)
        p.on("vector", lambda e, m=m: e.tensor_tensor(
            o_sb[:], u_sb[:], chunk(g_sb, m, Tn), ALU.mult), sem="evu")
        p.wait("sync", "evu", p.cnt["evu"])
        p.dma("sync", y[m * 128:(m + 1) * 128, :], o_sb[:], "stou")

    emit_proj2(p, x_sb=x_sb, kchunks=KC, mchunks=MC, Tn=Tn, w_dram=w1,
               out_cb=cb_up, ps=ps, tag="u")
    p.wait("sync", "stou", p.cnt["stou"])
    return p.build()


def build_mid(kdim, M, act=None, scale=False, Tn=T):
    p = Prog()
    KC = kdim // 128
    MC = M // 128
    x = p.din("x", [kdim, Tn], BF16)
    w = p.din("w", [kdim, M], F32)
    y = p.dout("y", [M, Tn], BF16)
    x_sb = p.sbuf("x_sb", [128, KC * Tn], BF16)
    o_sb = p.sbuf("o_sb", [128, Tn], BF16)
    ps = [p.psum("pspool0", [128, Tn], F32), p.psum("pspool1", [128, Tn], F32)]
    load_fm(p, "sync", x_sb, x, KC, Tn, "ld")
    s_sb = None
    if scale:
        sc = p.din("s", [128, MC], F32)
        s_sb = p.sbuf("s_sb", [128, MC], F32)
        p.dma("sync", s_sb[:], sc, "ld")
    ldv = p.cnt["ld"]
    p.wait("tensor", "ld", ldv)
    p.wait("scalar", "ld", ldv)

    def cb(m, pt, mmv):
        p.wait("scalar", "mm0", mmv)
        if m >= 1:
            p.wait("scalar", "sto", 16 * m)
        kw = {}
        if s_sb is not None:
            kw["scale"] = s_sb[:, m:m + 1]
        p.on("scalar", lambda e, pt=pt, kw=kw: e.activation(
            o_sb[:], pt[:], act or (AF.Identity if kw else AF.Copy), **kw), sem="ev0")
        p.wait("sync", "ev0", p.cnt["ev0"])
        p.dma("sync", y[m * 128:(m + 1) * 128, :], o_sb[:], "sto")

    emit_proj2(p, x_sb=x_sb, kchunks=KC, mchunks=MC, Tn=Tn, w_dram=w,
               out_cb=cb, ps=ps, tag="0")
    p.wait("sync", "sto", p.cnt["sto"])
    return p.build()


def build_combine(k1, k2, scale2=False, bias1=False, Tn=T, wdt1=F32, wdt2=F32):
    p = Prog()
    K1, K2, MC = k1 // 128, k2 // 128, D // 128
    x = p.din("x", [D, Tn], F32)
    a = p.din("a", [k1, Tn], BF16)
    h = p.din("h", [k2, Tn], BF16)
    w1 = p.din("w1", [k1, D], wdt1)
    w2 = p.din("w2", [k2, D], wdt2)
    xo = p.dout("xo", [D, Tn], F32)
    a_sb = p.sbuf("a_sb", [128, K1 * Tn], BF16)
    h_sb = p.sbuf("h_sb", [128, K2 * Tn], BF16)
    x_sb = p.sbuf("x_sb", [128, Tn], F32)
    e1 = p.sbuf("e1", [128, Tn], F32)
    o_sb = p.sbuf("o_sb", [128, Tn], F32)
    ps = [p.psum("pspool0", [128, Tn], F32), p.psum("pspool1", [128, Tn], F32)]
    load_fm(p, "sync", a_sb, a, K1, Tn, "ld")
    load_fm(p, "sync", h_sb, h, K2, Tn, "ld")
    b_sb = None
    if bias1:
        b = p.din("b1", [128, MC], F32)
        b_sb = p.sbuf("b_sb", [128, MC], F32)
        p.dma("sync", b_sb[:], b, "ld")
    s_sb = None
    if scale2:
        sc = p.din("s2", [128, MC], F32)
        s_sb = p.sbuf("s_sb", [128, MC], F32)
        p.dma("sync", s_sb[:], sc, "ld")
    ldv = p.cnt["ld"]
    p.wait("tensor", "ld", ldv)
    p.wait("scalar", "ld", ldv)

    wsb1 = [p.sbuf(f"w1_{i}", [128, K1 * 128], BF16) for i in range(2)]
    wsb2 = [p.sbuf(f"w2_{i}", [128, K2 * 128], BF16) for i in range(2)]
    NT = Tn // 512
    for m in range(MC):
        if m >= 2:
            p.wait("gpsimd", "mm1", m - 1)
        p.dma("gpsimd", wsb1[m % 2][:].rearrange("p (kc mm) -> p kc mm", mm=128),
              w1.rearrange("(kc p) mm -> p kc mm", p=128)[:, :, m * 128:(m + 1) * 128], f"wl1_{m % 2}")
        p.dma("gpsimd", wsb2[m % 2][:].rearrange("p (kc mm) -> p kc mm", mm=128),
              w2.rearrange("(kc p) mm -> p kc mm", p=128)[:, :, m * 128:(m + 1) * 128], f"wl2_{m % 2}")
        p.dma("sync", x_sb[:], x[m * 128:(m + 1) * 128, :], "ldx")
        p.wait("tensor", f"wl1_{m % 2}", 16 * (m // 2 + 1))
        p.wait("tensor", f"wl2_{m % 2}", 16 * (m // 2 + 1))
        if m >= 2:
            p.wait("tensor", "cmb", m - 1)
        pt = ps[m % 2]
        for k in range(K1):
            for n in range(NT):
                sl = slice(n * 512, (n + 1) * 512)
                p.on("tensor", lambda e, pt=pt, m=m, k=k, sl=sl, st=(k == 0), sp=(k == K1 - 1):
                     e.matmul(pt[:, sl], wsb1[m % 2][:, k * 128:(k + 1) * 128],
                              chunk(a_sb, k, Tn)[:, sl], start=st, stop=sp),
                     sem="mma" if (k == K1 - 1 and n == NT - 1) else None)
        p.wait("scalar", "mma", p.cnt["mma"])
        if m >= 1:
            p.wait("scalar", "cmb", p.cnt["cmb"])
        kw = {"bias": b_sb[:, m:m + 1]} if b_sb is not None else {}
        p.on("scalar", lambda e, pt=pt, kw=kw: e.activation(
            e1[:], pt[:], AF.Identity if kw else AF.Copy, **kw), sem="ev1a")
        p.wait("tensor", "ev1a", p.cnt["ev1a"])
        for k in range(K2):
            for n in range(NT):
                sl = slice(n * 512, (n + 1) * 512)
                p.on("tensor", lambda e, pt=pt, m=m, k=k, sl=sl, st=(k == 0), sp=(k == K2 - 1):
                     e.matmul(pt[:, sl], wsb2[m % 2][:, k * 128:(k + 1) * 128],
                              chunk(h_sb, k, Tn)[:, sl], start=st, stop=sp),
                     sem="mm1" if (k == K2 - 1 and n == NT - 1) else None)
        p.wait("vector", "mm1", p.cnt["mm1"])
        p.wait("vector", "ldx", 16 * (m + 1))
        if m >= 1:
            p.wait("vector", "sto", 16 * m)
        if s_sb is not None:
            p.on("vector", lambda e, pt=pt, m=m: e.scalar_tensor_tensor(
                o_sb[:], pt[:], s_sb[:, m:m + 1], e1[:], ALU.mult, ALU.add), sem="cmb0")
        else:
            p.on("vector", lambda e, pt=pt: e.tensor_tensor(o_sb[:], pt[:], e1[:], ALU.add),
                 sem="cmb0")
        p.wait("vector", "cmb0", p.cnt["cmb0"])
        p.on("vector", lambda e: e.tensor_tensor(o_sb[:], o_sb[:], x_sb[:], ALU.add),
             sem="cmb")
        p.wait("sync", "cmb", p.cnt["cmb"])
        p.dma("sync", xo[m * 128:(m + 1) * 128, :], o_sb[:], "sto")
        p.wait("sync", "cmb", p.cnt["cmb"])
    p.wait("sync", "sto", p.cnt["sto"])
    return p.build()



def build_attn(scale, l0mix=False, S=4096, Tn=T, H=16):
    p = Prog()
    KC = D // 128
    SC = S // 128
    NT = Tn // 512
    NKB = SC // 4
    qd = p.din("q", [D, Tn], BF16)
    kf = p.din("kf", [D, S], BF16)
    vt = p.din("vt", [H, S * 65], BF16)
    on2d = p.din("on2d", [1, 128], F32)
    hseld = p.din("hsel", [128, 2], F32)
    ctx = p.dout("ctx", [D, Tn], BF16)

    q_sb = p.sbuf("q_sb", [128, KC * Tn], BF16)
    k_sb = p.sbuf("k_sb", [128, KC * S], BF16)
    v_sb = [p.sbuf(f"v_sb{i}", [128, SC * 65], BF16) for i in range(2)]
    ctx_sb = p.sbuf("ctx_sb", [128, KC * Tn], BF16)
    p_sb = p.sbuf("p_sb", [128, 2048], BF16)
    sqs = p.sbuf("sqs", [128, S], F32)
    on2 = p.sbuf("on2", [1, 128], F32)
    hsel = p.sbuf("hsel_sb", [128, 2], F32)
    nrow = p.sbuf("nrow", [1, 64], F32)
    nb_all = p.sbuf("nb_all", [128, 16], F32)
    c1 = p.sbuf("c1", [64, 512], F32)
    rrow = p.sbuf("rrow", [1, 512], F32)
    tred = p.sbuf("tred", [1, 2], F32)

    ps_s = p.psum("ps_s", [128, 2048], F32)
    ps_av = p.psum("ps_av", [65, 512], F32)
    ps_bc = p.psum("ps_bc", [128, 512], F32)
    ps_n = p.psum("ps_n", [1, 1024], F32)

    p.dma("sync", on2[:], on2d, "ld")
    p.dma("sync", hsel[:], hseld, "ld")
    load_fm(p, "sync", q_sb, qd, KC, Tn, "ld")
    load_fm(p, "sync", k_sb, kf, KC, S, "ld")
    ldv = p.cnt["ld"]

    if l0mix:
        qw4 = p.din("qw", [1, 4], F32)
        emixd = p.din("emix", [128, 4 * 128], F32)
        qw_sb = p.sbuf("qw_sb", [1, 8], F32)
        emix_sb = p.sbuf("emix_sb", [128, 4 * 128], F32)
        wb4 = p.sbuf("wb4", [128, 4], F32)
        mixf = p.sbuf("mixf", [128, 128], F32)
        mixm = p.sbuf("mixm", [128, 128], BF16)
        qe2 = p.sbuf("qe2", [128, 512], BF16)
        p.dma("sync", qw_sb[0:1, 0:4], qw4, "ld")
        p.dma("sync", emix_sb[:], emixd, "ld")
        ldv = p.cnt["ld"]
        p.wait("vector", "ld", ldv)
        p.on("vector", lambda e: e.tensor_reduce(qw_sb[0:1, 4:5], qw_sb[0:1, 0:4], AX.X, ALU.max), sem="smx")
        p.wait("vector", "smx", 1)
        p.on("vector", lambda e: e.tensor_tensor(qw_sb[0:1, 0:4], qw_sb[0:1, 0:4],
                                                 qw_sb[0:1, 4:5].broadcast_to((1, 4)), ALU.subtract),
             sem="smx")
        p.wait("scalar", "smx", 2)
        p.on("scalar", lambda e: e.activation(qw_sb[0:1, 0:4], qw_sb[0:1, 0:4], AF.Exp), sem="smx")
        p.wait("vector", "smx", 3)
        p.on("vector", lambda e: e.tensor_reduce(qw_sb[0:1, 4:5], qw_sb[0:1, 0:4], AX.X, ALU.add), sem="smx")
        p.wait("vector", "smx", 4)
        p.on("vector", lambda e: e.reciprocal(qw_sb[0:1, 5:6], qw_sb[0:1, 4:5]), sem="smx")
        p.wait("vector", "smx", 5)
        p.on("vector", lambda e: e.tensor_tensor(qw_sb[0:1, 0:4], qw_sb[0:1, 0:4],
                                                 qw_sb[0:1, 5:6].broadcast_to((1, 4)), ALU.mult),
             sem="smx")
        p.wait("tensor", "smx", 6)
        p.on("tensor", lambda e: e.matmul(ps_bc[:, 0:4], on2[:], qw_sb[0:1, 0:4],
                                          start=True, stop=True), sem="wmm")
        p.wait("vector", "wmm", 1)
        p.on("vector", lambda e: e.tensor_copy(wb4[:], ps_bc[:, 0:4]), sem="mxs")
        p.wait("vector", "mxs", 1)
        p.on("vector", lambda e: e.tensor_scalar(mixf[:], emix_sb[:, 0:128], wb4[:, 0:1], None, ALU.mult), sem="mxs")
        for i in range(1, 4):
            p.wait("vector", "mxs", i + 1)
            p.on("vector", lambda e, i=i: e.scalar_tensor_tensor(
                mixf[:], emix_sb[:, i * 128:(i + 1) * 128], wb4[:, i:i + 1], mixf[:],
                ALU.mult, ALU.add), sem="mxs")
        p.wait("vector", "mxs", 5)
        p.on("vector", lambda e: e.tensor_copy(mixm[:], mixf[:]), sem="mixr")
        p.wait("tensor", "mixr", 1)
        p.wait("tensor", "ld", ldv)
        for c in range(KC):
            for n in range(NT):
                p.on("tensor", lambda e, c=c, n=n: e.matmul(
                    ps_s[:, 0:512], mixm[:], chunk(q_sb, c, Tn)[:, n * 512:(n + 1) * 512],
                    start=True, stop=True), sem="mxm")
                p.wait("scalar", "mxm", p.cnt["mxm"])
                p.on("scalar", lambda e: e.activation(qe2[:], ps_s[:, 0:512], AF.Copy),
                     sem="mxe")
                p.wait("vector", "mxe", p.cnt["mxe"])
                p.on("vector", lambda e, c=c, n=n: e.tensor_copy(
                    chunk(q_sb, c, Tn)[:, n * 512:(n + 1) * 512], qe2[:]), sem="mxc")
                p.wait("tensor", "mxc", p.cnt["mxc"])
    p.wait("scalar", "ld", ldv)
    git = 0
    for (src_sb, ncols, off, nn) in ((q_sb, Tn, 0, "q"), (k_sb, S, 16, "k")):
        CH = ncols // 512
        for c in range(KC):
            if l0mix and src_sb is q_sb:
                p.wait("scalar", "mxc", p.cnt["mxc"])
            if p.cnt.get("nmm", 0) > 0:
                p.wait("scalar", "nmm", p.cnt["nmm"])
            p.on("scalar", lambda e, c=c, src_sb=src_sb, ncols=ncols:
                 e.activation(sqs[:, 0:ncols], chunk(src_sb, c, ncols), AF.Square),
                 sem="nsq")
            p.wait("tensor", "nsq", p.cnt["nsq"])
            for j in range(2):
                for n in range(CH):
                    slot = git % 2
                    if git >= 2:
                        p.wait("tensor", "nrd", git - 1)
                    p.on("tensor", lambda e, j=j, n=n, slot=slot: e.matmul(
                        ps_n[0:1, slot * 512:slot * 512 + 512], hsel[:, j:j + 1],
                        sqs[:, n * 512:(n + 1) * 512], start=True, stop=True), sem="nmm")
                    p.wait("vector", "nmm", p.cnt["nmm"])
                    tgt = 2 * c + j + off
                    if n == 0:
                        p.on("vector", lambda e, slot=slot, tgt=tgt: e.tensor_reduce(
                            nrow[0:1, tgt:tgt + 1], ps_n[0:1, slot * 512:slot * 512 + 512],
                            AX.X, ALU.max), sem="nrd")
                    else:
                        p.on("vector", lambda e, slot=slot: e.tensor_reduce(
                            tred[0:1, 0:1], ps_n[0:1, slot * 512:slot * 512 + 512],
                            AX.X, ALU.max), sem="nrd")
                        p.wait("vector", "nrd", p.cnt["nrd"])
                        p.on("vector", lambda e, tgt=tgt: e.tensor_tensor(
                            nrow[0:1, tgt:tgt + 1], nrow[0:1, tgt:tgt + 1],
                            tred[0:1, 0:1], ALU.max))
                    git += 1
    p.wait("vector", "nrd", p.cnt["nrd"])
    p.on("vector", lambda e: e.tensor_tensor(nrow[0:1, 32:48], nrow[0:1, 0:16],
                                             nrow[0:1, 16:32], ALU.mult), sem="nfin")
    p.wait("scalar", "nfin", 1)
    p.on("scalar", lambda e: e.activation(nrow[0:1, 48:64], nrow[0:1, 32:48], AF.Sqrt),
         sem="nfin")
    p.wait("vector", "nfin", 2)
    p.on("vector", lambda e: e.tensor_scalar(nrow[0:1, 48:64], nrow[0:1, 48:64],
                                             -scale, None, ALU.mult), sem="nfin")
    p.wait("tensor", "nfin", 3)
    p.on("tensor", lambda e: e.matmul(ps_bc[:, 16:32], on2[:], nrow[0:1, 48:64],
                                      start=True, stop=True), sem="nbm")
    p.wait("vector", "nbm", 1)
    p.on("vector", lambda e: e.tensor_copy(nb_all[:], ps_bc[:, 16:32]), sem="nbc")
    p.wait("scalar", "nbc", 1)
    p.wait("tensor", "nbc", 1)

    for h in range(2):
        p.dma("gpsimd", v_sb[h][:].rearrange("p (kc m) -> p kc m", m=65),
              vt[h:h + 1, :].rearrange("one (kc pp m) -> (one pp) kc m", pp=128, m=65), f"vld{h % 2}")

    smm = sxp = avm = ctm = 0
    for h in range(H):
        kc = h // 2
        base = (h % 2) * 64
        vb = v_sb[h % 2]
        p.wait("tensor", f"vld{h % 2}", 16 * (h // 2 + 1))
        for qt in range(NT):
            qsl = slice(qt * 512, (qt + 1) * 512)
            for kb in range(NKB):
                sxp0, avm0, smm0 = sxp, avm, smm
                for j in range(4):
                    if sxp0 >= 4:
                        p.wait("tensor", "sxp", sxp0 - 3 + j)
                    kchunk = kb * 4 + j
                    p.on("tensor", lambda e, kc=kc, base=base, kchunk=kchunk, j=j, qsl=qsl:
                         e.matmul(ps_s[:, j * 512:(j + 1) * 512],
                                  chunk(k_sb, kc, S)[base:base + 64, kchunk * 128:(kchunk + 1) * 128],
                                  chunk(q_sb, kc, Tn)[base:base + 64, qsl],
                                  start=True, stop=True),
                         sem="smm")
                smm = p.cnt["smm"]
                for j in range(4):
                    p.wait("scalar", "smm", smm0 + j + 1)
                    if avm0 >= 4:
                        p.wait("scalar", "avm", avm0 - 3 + j)
                    p.on("scalar", lambda e, h=h, j=j: e.activation(
                        p_sb[:, j * 512:(j + 1) * 512], ps_s[:, j * 512:(j + 1) * 512],
                        AF.Exp, bias=nb_all[:, h:h + 1], scale=scale),
                        sem="sxp")
                sxp = p.cnt["sxp"]
                for j in range(4):
                    p.wait("tensor", "sxp", sxp0 + j + 1)
                    if kb == 0 and j == 0 and ctm > 0:
                        p.wait("tensor", "ctm", ctm)
                    kchunk = kb * 4 + j
                    p.on("tensor", lambda e, vb=vb, kchunk=kchunk, j=j,
                         st=(kb == 0 and j == 0), sp=(kb == NKB - 1 and j == 3):
                         e.matmul(ps_av[:], vb[:, kchunk * 65:(kchunk + 1) * 65],
                                  p_sb[:, j * 512:(j + 1) * 512], start=st, stop=sp),
                         sem="avm")
                avm = p.cnt["avm"]
            p.wait("vector", "avm", avm)
            p.on("vector", lambda e: e.reciprocal(rrow[:], ps_av[64:65, :]), sem="rrd")
            p.wait("tensor", "rrd", p.cnt["rrd"])
            if ctm > 0:
                p.wait("tensor", "ctm", ctm)
            p.on("tensor", lambda e: e.matmul(ps_bc[0:64, 0:512], on2[0:1, 0:64], rrow[:],
                                              start=True, stop=True), sem="bcm")
            p.wait("scalar", "avm", avm)
            if ctm > 0:
                p.wait("scalar", "ctm", ctm)
            p.on("scalar", lambda e: e.activation(c1[:], ps_av[0:64, :], AF.Copy), sem="c1e")
            p.wait("vector", "bcm", p.cnt["bcm"])
            p.wait("vector", "c1e", p.cnt["c1e"])
            p.on("vector", lambda e, kc=kc, base=base, qsl=qsl: e.tensor_tensor(
                chunk(ctx_sb, kc, Tn)[base:base + 64, qsl], c1[:], ps_bc[0:64, 0:512],
                ALU.mult), sem="ctm")
            ctm = p.cnt["ctm"]
        p.on("tensor", lambda e: e.nop(), sem="hdn")
        if h + 2 < H:
            p.wait("gpsimd", "hdn", h + 1)
            p.dma("gpsimd", v_sb[(h + 2) % 2][:].rearrange("p (kc m) -> p kc m", m=65),
                  vt[h + 2:h + 3, :].rearrange("one (kc pp m) -> (one pp) kc m", pp=128, m=65), f"vld{h % 2}")
    p.wait("sync", "ctm", ctm)
    for c in range(KC):
        p.dma("sync", ctx[c * 128:(c + 1) * 128, :], chunk(ctx_sb, c, Tn), "sto")
    p.wait("sync", "sto", p.cnt["sto"])
    return p.build()



def _mk_emix():
    E = np.zeros((4, 128, 128), dtype=np.float32)
    for blk in range(2):
        o = 64 * blk
        for i in range(32):
            E[0, o + i, o + i] = 1.0
            E[1, o + i, o + 32 + i] = 1.0
            E[2, o + 32 + i, o + i] = 1.0
            E[3, o + 32 + i, o + 32 + i] = 1.0
    return np.ascontiguousarray(E.transpose(1, 0, 2).reshape(128, 4 * 128))

EMIX = _mk_emix()

HSEL = np.zeros((128, 2), dtype=np.float32)
HSEL[0:64, 0] = 1.0
HSEL[64:128, 1] = 1.0

def make_vt(v_fm_pair):
    S = v_fm_pair.shape[1]
    vt = np.empty((16, S, 65), dtype=BF)
    vtm = np.ascontiguousarray(v_fm_pair.T)
    for h in range(16):
        vt[h, :, 0:64] = vtm[:, 64 * h:64 * h + 64]
    vt[:, :, 64] = np.float32(1.0)
    return vt.reshape(16, S * 65)

def pair_cat(shards, c):
    b = (c // 2) * 2
    return np.concatenate([np.asarray(shards[b], dtype=BF), np.asarray(shards[b + 1], dtype=BF)], axis=1)

def run_layer2(x_fm_shards, inp):
    xn = run_ln(x_fm_shards)
    nc = cached("l2qkv", build_projN, D,
                [{"out": "q", "m": D, "bias": True},
                 {"out": "k", "m": D, "bias": True},
                 {"out": "v", "m": D, "bias": True}])
    wq = fm(inp["L2_attn_Wq"]).astype(np.float32)
    wk = fm(inp["L2_attn_Wk"]).astype(np.float32)
    wv = fm(inp["L2_attn_Wv"]).astype(np.float32)
    bq, bk, bv = (tile_vec(inp["L2_attn_bq"]), tile_vec(inp["L2_attn_bk"]),
                  tile_vec(inp["L2_attn_bv"]))
    res = run(nc, [{"x": np.asarray(xn[c], dtype=BF), "w0": wq, "w1": wk, "w2": wv,
                    "b0": bq, "b1": bk, "b2": bv} for c in range(NC)])
    qs = [r["q"] for r in res]
    ks = [r["k"] for r in res]
    vs = [r["v"] for r in res]
    nc = cached(("attn", 0.125, False), build_attn, 0.125, False)
    ims = []
    for c in range(NC):
        kful = pair_cat(ks, c)
        vt = make_vt(pair_cat(vs, c).astype(np.float32))
        ims.append({"q": qs[c], "kf": np.ascontiguousarray(kful), "vt": vt,
                    "on2d": ONES_ROW, "hsel": HSEL})
    res = run(nc, ims)
    ctxs = [r["ctx"] for r in res]
    nc = cached(("gateup", D, 4096, "silu", True), build_gateup, D, 4096, AF.Silu, True)
    w1 = fm(inp["L2_ffn_w1_w"]).astype(np.float32)
    w2 = fm(inp["L2_ffn_w2_w"]).astype(np.float32)
    s1, s2 = tile_vec(inp["L2_ffn_w1_s"]), tile_vec(inp["L2_ffn_w2_s"])
    res = run(nc, [{"x": np.asarray(xn[c], dtype=BF), "w0": w1, "w1": w2,
                    "s0": s1, "s1": s2} for c in range(NC)])
    hs = [r["y"] for r in res]
    nc = cached(("mid", 4096, 4096, "gelu", True), build_mid, 4096, 4096, AF.Gelu, True)
    h1 = fm(inp["L2_ffn_h1_w"]).astype(np.float32)
    s1h = tile_vec(inp["L2_ffn_h1_s"])
    res = run(nc, [{"x": hs[c], "w": h1, "s": s1h} for c in range(NC)])
    hs = [r["y"] for r in res]
    h2 = fm(inp["L2_ffn_h2_w"]).astype(np.float32)
    s2h = tile_vec(inp["L2_ffn_h2_s"])
    res = run(nc, [{"x": hs[c], "w": h2, "s": s2h} for c in range(NC)])
    hs = [r["y"] for r in res]
    nc = cached(("combine", D, 4096, True, True), build_combine, D, 4096, True, True)
    wo = fm(inp["L2_attn_Wo"]).astype(np.float32)
    w3 = fm(inp["L2_ffn_w3_w"]).astype(np.float32)
    bo = tile_vec(inp["L2_attn_bo"])
    s3 = tile_vec(inp["L2_ffn_w3_s"])
    res = run(nc, [{"x": np.ascontiguousarray(x_fm_shards[c], dtype=np.float32),
                    "a": ctxs[c], "h": hs[c], "w1": wo, "w2": w3,
                    "b1": bo, "s2": s3} for c in range(NC)])
    return [r["xo"] for r in res]



def build_quant(R, C, in_dt=F32):
    p = Prog()
    RC = R // 128
    G = C // 16
    xin = p.din("xi", [R, C], in_dt)
    yo = p.dout("yo", [R, C], BF16)
    xb = p.sbuf("xb", [128, C], F32)
    mx = p.sbuf("mx", [128, G], F32)
    inv = p.sbuf("inv", [128, G], F32)
    pp = p.sbuf("pp", [128, C], F32)
    bb = p.sbuf("bb", [128, C], F32)
    cc = p.sbuf("cc", [128, C], F32)
    sg = p.sbuf("sg", [128, C], F32)
    ob = p.sbuf("ob", [128, C], BF16)
    MID = [0.25, 0.75, 1.25, 1.75, 2.5, 3.5, 5.0]
    xb_free = [0, 0]
    sg_free = [0]
    bb_free = [0]
    for r in range(RC):
        if r >= 1:
            p.wait("gpsimd", "sgn", xb_free[1])
            p.wait("gpsimd", "v", xb_free[0])
        p.dma("gpsimd", xb[:], xin[r * 128:(r + 1) * 128, :], f"ld{r % 2}")
        ldv = 16 * (r // 2 + 1)
        for eng in ("vector", "scalar"):
            p.wait(eng, f"ld{r % 2}", ldv)
        if r >= 1:
            p.wait("vector", "sto", 16 * r)
        g3 = xb[:].rearrange("p (g k) -> p g k", k=16)
        p.on("vector", lambda e, g3=g3: e.tensor_reduce(mx[:], g3, AX.X, ALU.max,
                                                        apply_absolute_value=True), sem="v")
        xb_free[0] = p.cnt["v"]
        p.wait("vector", "v", p.cnt["v"])
        p.on("vector", lambda e: e.tensor_scalar(mx[:], mx[:], 1.0 / 6.0, 1e-12,
                                                 ALU.mult, op1=ALU.max), sem="v")
        p.wait("vector", "v", p.cnt["v"])
        p.on("vector", lambda e: e.reciprocal(inv[:], mx[:]), sem="v")
        p.wait("vector", "v", p.cnt["v"])
        ib = inv[:].rearrange("p g -> p g ()").broadcast_to((128, G, 16))
        p3 = pp[:].rearrange("p (g k) -> p g k", k=16)
        b3 = bb[:].rearrange("p (g k) -> p g k", k=16)
        if r >= 1:
            p.wait("scalar", "v", bb_free[0])
        p.on("scalar", lambda e: e.activation(bb[:], xb[:], AF.Abs), sem="ab")
        p.wait("vector", "ab", p.cnt["ab"])
        p.on("vector", lambda e, ib=ib, p3=p3, b3=b3: e.tensor_tensor(
            p3, b3, ib, ALU.mult), sem="v")
        p.wait("vector", "v", p.cnt["v"])
        p.on("vector", lambda e: e.tensor_scalar(bb[:], pp[:], 5.0, 2.0, ALU.is_gt, op1=ALU.mult), sem="v")
        p.wait("vector", "v", p.cnt["v"])
        for thr in (3.5, 2.5):
            p.on("vector", lambda e, thr=thr: e.scalar_tensor_tensor(
                bb[:], pp[:], thr, bb[:], ALU.is_gt, ALU.add), sem="v")
            p.wait("vector", "v", p.cnt["v"])
        p.on("vector", lambda e: e.tensor_scalar(cc[:], pp[:], 0.25, None, ALU.is_gt), sem="v")
        p.wait("vector", "v", p.cnt["v"])
        for thr in (0.75, 1.25, 1.75):
            p.on("vector", lambda e, thr=thr: e.scalar_tensor_tensor(
                cc[:], pp[:], thr, cc[:], ALU.is_gt, ALU.add), sem="v")
            p.wait("vector", "v", p.cnt["v"])
        p.on("vector", lambda e: e.scalar_tensor_tensor(cc[:], cc[:], 0.5, bb[:],
                                                        ALU.mult, ALU.add), sem="v")
        bb_free[0] = p.cnt["v"]
        if r >= 1:
            p.wait("scalar", "v", sg_free[0])
        p.on("scalar", lambda e: e.activation(sg[:], xb[:], AF.Sign), sem="sgn")
        xb_free[1] = p.cnt["sgn"]
        p.wait("vector", "sgn", p.cnt["sgn"])
        p.wait("vector", "v", p.cnt["v"])
        p.on("vector", lambda e: e.tensor_tensor(cc[:], cc[:], sg[:], ALU.mult), sem="v")
        sg_free[0] = p.cnt["v"]
        p.wait("vector", "v", p.cnt["v"])
        mb = mx[:].rearrange("p g -> p g ()").broadcast_to((128, G, 16))
        c3 = cc[:].rearrange("p (g k) -> p g k", k=16)
        o3 = ob[:].rearrange("p (g k) -> p g k", k=16)
        p.on("vector", lambda e, mb=mb, c3=c3, o3=o3: e.tensor_tensor(o3, c3, mb, ALU.mult), sem="vq")
        p.wait("sync", "vq", p.cnt["vq"])
        p.dma("sync", yo[r * 128:(r + 1) * 128, :], ob[:], "sto")
    p.wait("sync", "sto", p.cnt["sto"])
    return p.build()



def tm(a):
    return np.ascontiguousarray(np.asarray(a).T)

def run_quant(arr_shards, R, C):
    nc = cached(("quant", R, C), build_quant, R, C)
    res = run(nc, [{"xi": np.ascontiguousarray(s, dtype=np.float32)} for s in arr_shards])
    return [r["yo"] for r in res]

def run_layer0(x_fm_shards, inp):
    xn, xn_f = run_ln(x_fm_shards, want_f32=True)
    nc = cached(("l0qkv",), build_projN, D,
                [{"out": "q", "m": D}, {"out": "k", "m": D}, {"out": "v", "m": D}])
    wq = np.ascontiguousarray(inp["L0_attn_Wq"], dtype=np.float32)
    wk = np.ascontiguousarray(inp["L0_attn_Wk"], dtype=np.float32)
    wv = np.ascontiguousarray(inp["L0_attn_Wv"], dtype=np.float32)
    res = run(nc, [{"x": np.asarray(xn[c], dtype=BF), "w0": wq, "w1": wk, "w2": wv}
                   for c in range(NC)])
    qs, ks, vs = [r["q"] for r in res], [r["k"] for r in res], [r["v"] for r in res]
    nc = cached(("attn", "l0"), build_attn, 1.0 / np.sqrt(32.0), True)
    qw = np.ascontiguousarray(inp["L0_attn_quadrant_weights"].reshape(1, 4), dtype=np.float32)
    ims = []
    for c in range(NC):
        kful = pair_cat(ks, c)
        vt = make_vt(pair_cat(vs, c).astype(np.float32))
        ims.append({"q": qs[c], "kf": np.ascontiguousarray(kful), "vt": vt,
                    "on2d": ONES_ROW, "hsel": HSEL, "qw": qw, "emix": EMIX})
    ctxs = [r["ctx"] for r in run(nc, ims)]
    wg_q = run_quant([inp["L0_ffn_w_gate"]] * NC, 2816, D)[0]
    wu_q = run_quant([inp["L0_ffn_w_up"]] * NC, 2816, D)[0]
    wd_q = run_quant([inp["L0_ffn_w_down"]] * NC, D, 2816)[0]
    xq_tm = run_quant([tm(xn_f[c]) for c in range(NC)], T, D)
    xq = [fm(s) for s in xq_tm]
    nc = cached(("gateup", D, 2816, "silu", False, "bf"), build_gateup, D, 2816,
                AF.Silu, False, T, F32, BF16, BF16)
    wg_l = fm(wg_q)
    wu_l = fm(wu_q)
    res = run(nc, [{"x": np.asarray(xq[c], dtype=BF), "w0": wg_l, "w1": wu_l}
                   for c in range(NC)])
    hs = [r["y"] for r in res]
    hq_tm = run_quant([tm(h) for h in hs], T, 2816)
    hq = [fm(s) for s in hq_tm]
    nc = cached(("combine", D, 2816, False, False, "bf2"), build_combine, D, 2816,
                False, False, T, F32, BF16)
    wo = np.ascontiguousarray(inp["L0_attn_Wo"], dtype=np.float32)
    wd_l = fm(wd_q)
    res = run(nc, [{"x": np.ascontiguousarray(x_fm_shards[c], dtype=np.float32),
                    "a": ctxs[c], "h": np.asarray(hq[c], dtype=BF),
                    "w1": wo, "w2": wd_l} for c in range(NC)])
    return [r["xo"] for r in res]



def build_l1conv():
    p = Prog()
    S = 4096
    up = p.din("up", [1536, S], BF16)
    tw = p.din("tw", [128, 36], F32)
    tb = p.din("tb", [128, 12], F32)
    zf = p.din("zf", [3, S], F32)
    w1 = p.din("w1", [3, 64], F32)
    w2 = p.din("w2", [64, 64], F32)
    w3 = p.din("w3", [64, 64], F32)
    b123 = p.din("b123", [64, 3], F32)
    freqc = p.din("freqc", [64, 2], F32)
    wo = p.din("wo", [64, 512], F32)
    negd = p.din("negd", [128, 4], F32)
    trow = p.din("trow", [1, S], F32)
    on2d = p.din("on2d", [1, 128], F32)
    x0o = p.dout("x0", [512, S], BF16)
    vxo = p.dout("vx", [512, S], BF16)
    kto = p.dout("kt", [512, S], BF16)

    up_sb = p.sbuf("up_sb", [128, 2 * (S + 2)], BF16)
    tw_sb = p.sbuf("tw_sb", [128, 36], F32)
    tb_sb = p.sbuf("tb_sb", [128, 12], F32)
    uc = p.sbuf("uc", [128, S], F32)
    ucb = p.sbuf("ucb", [128, S], BF16)
    x1_sb = p.sbuf("x1_sb", [128, S], BF16)
    mw = p.sbuf("mw", [64, 3 * 64 + 16], BF16)
    mb = p.sbuf("mb", [64, 8], F32)
    h3 = p.sbuf("h3", [64, S], BF16)
    hs = p.sbuf("hs", [64, 1536], F32)
    zf_sb = p.sbuf("zf_sb", [3, S], BF16)
    wo_sb = p.sbuf("wo_sb", [64, 512], BF16)
    trow_sb = p.sbuf("trow_sb", [1, S], F32)
    on2 = p.sbuf("on2", [1, 128], F32)
    kte = p.sbuf("kte", [128, 512], F32)
    ktd = p.sbuf("ktd", [128, 512], F32)
    ktb = p.sbuf("ktb", [128, S], BF16)
    ps_a = p.psum("ps_a", [128, 512], F32)
    ps_b = p.psum("ps_b", [128, 512], F32)
    ps_c = p.psum("ps_c", [64, 512], F32)

    for sl2 in range(2):
        p.on("vector", lambda e, sl2=sl2: e.memset(
            up_sb[:, sl2 * (S + 2):sl2 * (S + 2) + 2], 0.0), sem="pad")
    for (dst, src) in ((tw_sb, tw), (tb_sb, tb), (trow_sb, trow), (on2, on2d)):
        p.dma("sync", dst[:], src, "ld")
    for (dst, src) in ((zf_sb, zf), (wo_sb, wo)):
        p.dma("gpsimd", dst[:], src, "ldg")
    p.dma("gpsimd", mw[0:3, 0:64], w1, "ldg")
    p.dma("gpsimd", mw[:, 64:128], w2, "ldg")
    p.dma("gpsimd", mw[:, 128:192], w3, "ldg")
    p.dma("sync", mb[:, 0:3], b123, "ld")
    p.dma("sync", mb[:, 3:5], freqc, "ld")
    negd_sb = p.sbuf("negd_sb", [128, 4], F32)
    p.dma("sync", negd_sb[:], negd, "ld")
    ldv = p.cnt["ld"]
    ldgv = p.cnt["ldg"]
    for eng in ("tensor", "vector", "scalar"):
        p.wait(eng, "ld", ldv)
        p.wait(eng, "ldg", ldgv)
    p.wait("vector", "pad", 2)

    li = [0]
    def conv_chunk(c, dst_kind, dst_idx):
        par = li[0] % 2
        base = par * (S + 2)
        if li[0] >= 2:
            p.wait("gpsimd", "cv", cvfree[par])
        p.dma("gpsimd", up_sb[:, base + 2:base + (S + 2)],
              up[c * 128:(c + 1) * 128, :], f"lu{par}")
        p.wait("vector", f"lu{par}", 16 * (li[0] // 2 + 1))
        li[0] += 1
        if p.cnt.get("cv", 0) > 0:
            p.wait("vector", "cv", p.cnt["cv"])
        p.on("vector", lambda e, base=base: e.tensor_scalar(
            uc[:], up_sb[:, base:base + S], tw_sb[:, 3 * c:3 * c + 1], None, ALU.mult),
            sem="cv")
        p.wait("vector", "cv", p.cnt["cv"])
        p.on("vector", lambda e, base=base: e.scalar_tensor_tensor(
            uc[:], up_sb[:, base + 1:base + 1 + S], tw_sb[:, 3 * c + 1:3 * c + 2], uc[:],
            ALU.mult, ALU.add), sem="cv")
        p.wait("vector", "cv", p.cnt["cv"])
        p.on("vector", lambda e, base=base: e.scalar_tensor_tensor(
            uc[:], up_sb[:, base + 2:base + 2 + S], tw_sb[:, 3 * c + 2:3 * c + 3], uc[:],
            ALU.mult, ALU.add), sem="cv")
        cvfree[par] = p.cnt["cv"]
        p.wait("vector", "cv", p.cnt["cv"])
        if dst_kind == "x1":
            p.on("vector", lambda e: e.tensor_scalar(
                x1_sb[:], uc[:], tb_sb[:, c:c + 1], None, ALU.add), sem="cv")
            p.wait("vector", "cv", p.cnt["cv"])
        elif dst_kind == "x0":
            if p.cnt.get("sto", 0) > 0:
                p.wait("vector", "sto", p.cnt["sto"])
            if p.cnt.get("stv", 0) > 0:
                p.wait("vector", "stv", p.cnt["stv"])
            p.on("vector", lambda e: e.tensor_scalar(
                ucb[:], uc[:], tb_sb[:, c:c + 1], None, ALU.add), sem="cv")
            p.wait("sync", "cv", p.cnt["cv"])
            p.dma("sync", x0o[dst_idx * 128:(dst_idx + 1) * 128, :], ucb[:], "sto")
        else:
            p.on("vector", lambda e: e.tensor_scalar(
                uc[:], uc[:], tb_sb[:, c:c + 1], None, ALU.add), sem="cv")
            p.wait("vector", "cv", p.cnt["cv"])
            if p.cnt.get("stv", 0) > 0:
                p.wait("vector", "stv", p.cnt["stv"])
            if p.cnt.get("sto", 0) > 0:
                p.wait("vector", "sto", p.cnt["sto"])
            p.on("vector", lambda e: e.tensor_tensor(
                ucb[:], uc[:], x1_sb[:], ALU.mult), sem="cv")
            p.wait("sync", "cv", p.cnt["cv"])
            p.dma("sync", vxo[dst_idx * 128:(dst_idx + 1) * 128, :], ucb[:], "stv")

    cvfree = [0, 0]
    for i in range(4):
        conv_chunk(4 + i, "x1", i)
        conv_chunk(8 + i, "v", i)
        conv_chunk(i, "x0", i)

    def sin_layer(li, lhs_ap, rhs_sb, kparts):
        for n in range(S // 512):
            sl = slice(n * 512, (n + 1) * 512)
            p.on("tensor", lambda e, sl=sl, lhs_ap=lhs_ap, rhs_sb=rhs_sb, kparts=kparts:
                 e.matmul(ps_c[:, 0:512], lhs_ap, rhs_sb[0:kparts, sl],
                          start=True, stop=True), sem="smm")
            p.wait("scalar", "smm", p.cnt["smm"])
            if p.cnt.get("sev", 0) > 0:
                p.wait("scalar", "sev", p.cnt["sev"])
            p.on("scalar", lambda e, li=li: e.activation(
                hs[:, 0:512], ps_c[:, 0:512], AF.Identity, bias=mb[:, li:li + 1]),
                sem="sact")
            p.wait("vector", "sact", p.cnt["sact"])
            p.on("vector", lambda e: e.tensor_scalar(
                hs[:, 512:1024], hs[:, 0:512], mb[:, 3:4], None, ALU.mult), sem="sv")
            p.wait("vector", "sv", p.cnt["sv"])
            for _rr in range(2):
                p.on("vector", lambda e: e.tensor_scalar(
                    hs[:, 0:512], hs[:, 512:1024], np.pi, -2 * np.pi, ALU.is_gt,
                    op1=ALU.mult), sem="sv")
                p.wait("vector", "sv", p.cnt["sv"])
                p.on("vector", lambda e: e.tensor_tensor(
                    hs[:, 512:1024], hs[:, 512:1024], hs[:, 0:512], ALU.add), sem="sv")
                p.wait("vector", "sv", p.cnt["sv"])
                p.on("vector", lambda e: e.tensor_scalar(
                    hs[:, 0:512], hs[:, 512:1024], -np.pi, 2 * np.pi, ALU.is_lt,
                    op1=ALU.mult), sem="sv")
                p.wait("vector", "sv", p.cnt["sv"])
                p.on("vector", lambda e: e.tensor_tensor(
                    hs[:, 512:1024], hs[:, 512:1024], hs[:, 0:512], ALU.add), sem="sv")
                p.wait("vector", "sv", p.cnt["sv"])
            p.wait("scalar", "sv", p.cnt["sv"])
            p.on("scalar", lambda e, sl=sl: e.activation(
                h3[:, sl], hs[:, 512:1024], AF.Sin), sem="sev")
            p.wait("tensor", "sev", p.cnt["sev"])
    sin_layer(0, mw[0:3, 0:64], zf_sb, 3)
    sin_layer(1, mw[:, 64:128], h3, 64)
    sin_layer(2, mw[:, 128:192], h3, 64)
    for cb in range(4):
        for n in range(S // 512):
            sl = slice(n * 512, (n + 1) * 512)
            if p.cnt.get("ke0", 0) > 0:
                p.wait("tensor", "ke0", p.cnt["ke0"])
            p.on("tensor", lambda e, cb=cb, sl=sl: e.matmul(
                ps_a[:], wo_sb[:, cb * 128:(cb + 1) * 128], h3[:, sl],
                start=True, stop=True), sem="kmm")
            p.on("tensor", lambda e, sl=sl: e.matmul(
                ps_b[:], on2[:], trow_sb[0:1, sl], start=True, stop=True), sem="kmm")
            p.wait("scalar", "kmm", p.cnt["kmm"])
            if p.cnt.get("kev", 0) > 0:
                p.wait("scalar", "kev", p.cnt["kev"])
            p.on("scalar", lambda e: e.activation(kte[:], ps_a[:], AF.Copy), sem="ke0")
            p.on("scalar", lambda e, cb=cb: e.activation(
                ktd[:], ps_b[:], AF.Exp, scale=negd_sb[:, cb:cb + 1]), sem="ke0")
            p.wait("vector", "ke0", p.cnt["ke0"])
            p.on("vector", lambda e, sl=sl: e.tensor_tensor(
                ktb[:, sl], kte[:], ktd[:], ALU.mult), sem="kev")
        p.wait("sync", "kev", p.cnt["kev"])
        p.dma("sync", kto[cb * 128:(cb + 1) * 128, :], ktb[:], "stk")
        p.wait("vector", "stk", p.cnt["stk"])
    p.wait("sync", "sto", p.cnt["sto"])
    p.wait("sync", "stv", p.cnt["stv"])
    p.wait("sync", "stk", p.cnt["stk"])
    return p.build()



def _dftc(n_i, n_j, den, sign=-1.0, scale=1.0):
    i = np.arange(n_i)[:, None]
    j = np.arange(n_j)[None, :]
    ang = 2.0 * np.pi * i * j / den
    return ((np.cos(ang) * scale).astype(np.float32),
            (sign * np.sin(ang) * scale).astype(np.float32))

def fft_consts():
    W1r, W1i = _dftc(64, 128, 128)
    W2r, W2i = _dftc(64, 64, 64)
    TWr, TWi = _dftc(128, 64, 8192)
    Wi1r, Wi1i = _dftc(64, 64, 64, sign=+1.)
    Tir, Tii = _dftc(64, 128, 8192, sign=+1.)
    Wi2r, Wi2i = _dftc(128, 64, 128, sign=+1., scale=1.0 / 8192.0)
    st = lambda a: np.ascontiguousarray(np.vstack([a, a]).astype(BF))
    c = {
        "w1r": np.ascontiguousarray(W1r.astype(BF)), "w1i": np.ascontiguousarray(W1i.astype(BF)),
        "w2r": st(W2r), "w2i": st(W2i), "w2in": st(-W2i),
        "wi1r": st(Wi1r), "wi1i": st(Wi1i), "wi1in": st(-Wi1i),
        "wi2r": np.ascontiguousarray(Wi2r.astype(BF)), "wi2in": np.ascontiguousarray((-Wi2i).astype(BF)),
        "twr": TWr, "twi": TWi, "tir": Tir, "tii": Tii,
        "idn": np.ascontiguousarray(np.eye(128, dtype=BF)),
        "idn2": np.ascontiguousarray(np.vstack([np.eye(64), np.eye(64)]).astype(BF)),
        "mdif": np.ascontiguousarray(np.vstack([np.eye(64), -np.eye(64)]).astype(BF)),
        "msum": np.ascontiguousarray(np.vstack([np.eye(64), np.eye(64)]).astype(BF)),
    }
    return c

FFTC = None

FFT_DBG = [0]

def build_fft(NB=8):
    p = Prog()
    S = 4096
    vxd = p.din("vxd", [512, S], BF16)
    ktd = p.din("ktd", [512, S], BF16)
    x0d = p.din("x0d", [512, S], BF16)
    fbr = p.din("fbr", [1, 512], F32)
    on2d = p.din("on2d", [1, 128], F32)
    yvo = p.dout("yvd", [512, S], BF16)
    cns = {}
    for nm, sh, dt in (("w1r", (64, 128), BF16), ("w1i", (64, 128), BF16),
                       ("w2r", (128, 64), BF16), ("w2i", (128, 64), BF16), ("w2in", (128, 64), BF16),
                       ("wi1r", (128, 64), BF16), ("wi1i", (128, 64), BF16), ("wi1in", (128, 64), BF16),
                       ("wi2r", (128, 64), BF16), ("wi2in", (128, 64), BF16),
                       ("twr", (128, 64), F32), ("twi", (128, 64), F32),
                       ("tir", (64, 128), F32), ("tii", (64, 128), F32),
                       ("idn", (128, 128), BF16), ("idn2", (128, 64), BF16),
                       ("mdif", (128, 64), BF16), ("msum", (128, 64), BF16)):
        d = p.din(nm, list(sh), dt)
        sb = p.sbuf(nm + "_sb", list(sh), dt)
        p.dma("sync", sb[:], d, "ldc")
        cns[nm] = sb
    fb_sb = p.sbuf("fb_sb", [1, 512], F32)
    on2 = p.sbuf("on2", [1, 128], F32)
    p.dma("sync", fb_sb[:], fbr, "ldc")
    p.dma("sync", on2[:], on2d, "ldc")
    xin = p.sbuf("xin", [64, S], BF16)
    x0b = p.sbuf("x0b", [64, S], BF16)
    Bre = p.sbuf("Bre", [128, S], BF16)
    Bim = p.sbuf("Bim", [128, S], BF16)
    Btr = p.sbuf("Btr", [128, S], BF16)
    Bti = p.sbuf("Bti", [128, S], BF16)
    Kst = p.sbuf("Kst", [128, 2 * S], BF16)
    Ksw = p.sbuf("Ksw", [128, 2 * S], BF16)
    Xst = p.sbuf("Xst", [128, 2 * S], BF16)
    pr1 = p.sbuf("pr1", [128, 2 * S], BF16)
    pr2 = p.sbuf("pr2", [128, 2 * S], BF16)
    yb = p.sbuf("yb", [64, S], F32)
    fbb = p.sbuf("fbb", [64, 64], F32)
    t1b = p.sbuf("t1b", [64, 2048], F32)
    PA = p.sbuf("PA", [64, 2 * S], BF16)
    PB = p.sbuf("PB", [64, 2 * S], BF16)
    ov = p.sbuf("ov", [64, 512], BF16)
    ps1 = p.psum("ps1", [128, 512], F32)
    ps2 = p.psum("ps2", [128, 512], F32)
    ps3 = p.psum("ps3", [128, 512], F32)
    ps4 = p.psum("ps4", [128, 512], F32)
    ps5 = p.psum("ps5", [128, 512], F32)
    ps6 = p.psum("ps6", [128, 512], F32)
    pb1 = p.psum("pb1", [128, 128], BF16)
    pb2 = p.psum("pb2", [128, 128], BF16)
    ldcv = p.cnt["ldc"]
    for eng in ("tensor", "vector", "scalar"):
        p.wait(eng, "ldc", ldcv)

    def stage12(src_sb, dst_st, dst_sw, t):
        npc = 4 if dst_sw is not None else 2
        Z = lambda nm: p.cnt.get(nm + t, 0)
        b_twm, b_tw, b_mm1 = Z("twm"), Z("tw"), Z("mm1")
        for n in range(8):
            sl = slice(n * 512, (n + 1) * 512)
            if b_twm + 4 * n > 0:
                p.wait("tensor", "twm" + t, b_twm + 4 * n)
            p.on("tensor", lambda e, sl=sl, src_sb=src_sb: e.matmul(
                ps1[:, 0:512], cns["w1r"][:], src_sb[0:64, sl], start=True, stop=True))
            p.on("tensor", lambda e, sl=sl, src_sb=src_sb: e.matmul(
                ps2[:, 0:512], cns["w1i"][:], src_sb[0:64, sl], start=True, stop=True),
                sem="mm1" + t)
            p.wait("vector", "mm1" + t, p.cnt["mm1" + t])
            if b_tw + 2 * n > 0:
                p.wait("vector", "tw" + t, b_tw + 2 * n)
            twr_b = cns["twr"][:].rearrange("p n -> p () n").broadcast_to((128, 8, 64))
            twi_b = cns["twi"][:].rearrange("p n -> p () n").broadcast_to((128, 8, 64))
            pa = ps1[:, 0:512].rearrange("p (c n) -> p c n", n=64)
            pb = ps2[:, 0:512].rearrange("p (c n) -> p c n", n=64)
            u1 = pr1[:, 0:512].rearrange("p (c n) -> p c n", n=64)
            u2 = pr1[:, 512:1024].rearrange("p (c n) -> p c n", n=64)
            u3 = pr2[:, 0:512].rearrange("p (c n) -> p c n", n=64)
            u4 = pr2[:, 512:1024].rearrange("p (c n) -> p c n", n=64)
            for (dst, a, wgt) in ((u1, pa, twr_b), (u2, pb, twi_b),
                                  (u3, pa, twi_b), (u4, pb, twr_b)):
                p.on("vector", lambda e, dst=dst, a=a, wgt=wgt: e.tensor_tensor(
                    dst, a, wgt, ALU.mult), sem="twm" + t)
            p.wait("vector", "twm" + t, p.cnt["twm" + t])
            p.on("vector", lambda e, sl=sl: e.tensor_tensor(
                Bre[:, sl], pr1[:, 0:512], pr1[:, 512:1024], ALU.subtract), sem="tw" + t)
            p.on("vector", lambda e, sl=sl: e.tensor_tensor(
                Bim[:, sl], pr2[:, 0:512], pr2[:, 512:1024], ALU.add), sem="tw" + t)
        p.wait("tensor", "tw" + t, p.cnt["tw" + t])
        b_mtE = Z("mtE")
        for cp in range(32):
            if b_mtE + 2 * cp > 0:
                p.wait("tensor", "mtE" + t, b_mtE + 2 * cp)
            for (src, dps) in ((Bre, pb1), (Bim, pb2)):
                ap = bass.AP(src, cp * 128, [[S, 128], [64, 2], [1, 64]])
                p.on("tensor", lambda e, ap=ap, dps=dps: e.transpose(
                    dps[:, 0:128], ap, cns["idn"][:]), sem="mtT" + t)
            p.wait("scalar", "mtT" + t, p.cnt["mtT" + t])
            p.on("scalar", lambda e, cp=cp: e.activation(
                Btr[:, cp * 128:(cp + 1) * 128], pb1[:, 0:128], AF.Copy), sem="mtE" + t)
            p.on("scalar", lambda e, cp=cp: e.activation(
                Bti[:, cp * 128:(cp + 1) * 128], pb2[:, 0:128], AF.Copy), sem="mtE" + t)
        p.wait("tensor", "mtE" + t, p.cnt["mtE" + t])
        b_s2e = Z("s2e")
        for c in range(64):
            par = c % 2
            b0 = par * 64
            cp = c // 2
            rre = Btr[b0:b0 + 64, cp * 128:(cp + 1) * 128]
            rim = Bti[b0:b0 + 64, cp * 128:(cp + 1) * 128]
            slot = c % 4
            pt = (ps3, ps4, ps5, ps6)[slot]
            po = 0
            if c >= 4 or b_s2e > 0:
                p.wait("tensor", "s2e" + t, b_s2e + npc * max(0, c - 3))
            p.on("tensor", lambda e, pt=pt, po=po, b0=b0, rre=rre: e.matmul(
                pt[0:64, po:po + 128], cns["w2r"][b0:b0 + 64, :], rre, start=True, stop=False))
            p.on("tensor", lambda e, pt=pt, po=po, b0=b0, rim=rim: e.matmul(
                pt[0:64, po:po + 128], cns["w2in"][b0:b0 + 64, :], rim, start=False, stop=True))
            p.on("tensor", lambda e, pt=pt, po=po, b0=b0, rre=rre: e.matmul(
                pt[64:128, po:po + 128], cns["w2i"][b0:b0 + 64, :], rre, start=True, stop=False))
            p.on("tensor", lambda e, pt=pt, po=po, b0=b0, rim=rim: e.matmul(
                pt[64:128, po:po + 128], cns["w2r"][b0:b0 + 64, :], rim, start=False, stop=True),
                sem="s2m" + t)
            p.wait("scalar", "s2m" + t, p.cnt["s2m" + t])
            csl = slice(c * 128, (c + 1) * 128)
            p.on("scalar", lambda e, pt=pt, po=po, csl=csl, dst_st=dst_st: e.activation(
                dst_st[0:64, csl], pt[0:64, po:po + 128], AF.Copy), sem="s2e" + t)
            p.on("scalar", lambda e, pt=pt, po=po, csl=csl, dst_st=dst_st: e.activation(
                dst_st[64:128, csl], pt[64:128, po:po + 128], AF.Copy), sem="s2e" + t)
            if dst_sw is not None:
                p.on("scalar", lambda e, pt=pt, po=po, csl=csl, dst_sw=dst_sw: e.activation(
                    dst_sw[64:128, csl], pt[0:64, po:po + 128], AF.Copy), sem="s2e" + t)
                p.on("scalar", lambda e, pt=pt, po=po, csl=csl, dst_sw=dst_sw: e.activation(
                    dst_sw[0:64, csl], pt[64:128, po:po + 128], AF.Copy), sem="s2e" + t)
        p.wait("tensor", "s2e" + t, p.cnt["s2e" + t])

    nsty = 0
    for blk in range(NB):
        t = ""
        Z = lambda nm: p.cnt.get(nm, 0)
        rs = slice(blk * 64, blk * 64 + 64)
        if blk > 0:
            p.wait("gpsimd", "ep1", 8 * blk)
            p.wait("gpsimd", "ep3", 8 * blk)
        p.dma("gpsimd", xin[:], ktd[rs, :], f"ldk{blk % 2}")
        p.wait("tensor", f"ldk{blk % 2}", 16 * (blk // 2 + 1))
        if blk > 0:
            p.wait("tensor", "fbc", blk)
        stage12(xin, Kst, Ksw, "k")
        if FFT_DBG[0] == 1:
            p.wait("sync", "s2ek", p.cnt["s2ek"])
            p.dma("sync", yvo[rs, :], Kst[0:64, 0:S], "sty")
            p.wait("sync", "sty", 16)
            return p.build()
        p.wait("gpsimd", "mm1k", p.cnt["mm1k"])
        p.dma("gpsimd", xin[:], vxd[rs, :], f"ldv{blk % 2}")
        p.dma("gpsimd", x0b[:], x0d[rs, :], f"ldv{blk % 2}")
        p.wait("tensor", f"ldv{blk % 2}", 32 * (blk // 2 + 1))
        p.wait("vector", f"ldv{blk % 2}", 32 * (blk // 2 + 1))
        stage12(xin, Xst, None, "v")
        if FFT_DBG[0] == 2:
            p.wait("sync", "s2ev", p.cnt["s2ev"])
            p.dma("sync", yvo[rs, :], Xst[0:64, 0:S], "sty")
            p.wait("sync", "sty", 16)
            return p.build()
        p.wait("vector", "s2ev", p.cnt["s2ev"])
        p.wait("vector", "s2ek", p.cnt["s2ek"])
        b_pw2 = Z("pw2")
        b_pwm = Z("pwm")
        for n in range(16):
            sl = slice(n * 512, (n + 1) * 512)
            if b_pwm + n > 0:
                p.wait("vector", "pwm", b_pwm + n)
            p.on("vector", lambda e, sl=sl: e.tensor_tensor(
                pr1[:, sl], Xst[:, sl], Kst[:, sl], ALU.mult), sem="pw1")
            p.on("vector", lambda e, sl=sl: e.tensor_tensor(
                pr2[:, sl], Xst[:, sl], Ksw[:, sl], ALU.mult), sem="pw1")
            p.wait("tensor", "pw1", p.cnt["pw1"])
            pt = (ps3, ps4)[n % 2]
            pu = (ps5, ps6)[n % 2]
            if b_pw2 + 2 * max(0, n - 1) > 0:
                p.wait("tensor", "pw2", b_pw2 + 2 * max(0, n - 1))
            p.on("tensor", lambda e, pt=pt, sl=sl: e.matmul(
                pt[0:64, 0:512], cns["mdif"][:], pr1[:, sl], start=True, stop=True))
            p.on("tensor", lambda e, pu=pu, sl=sl: e.matmul(
                pu[0:64, 0:512], cns["msum"][:], pr2[:, sl], start=True, stop=True),
                sem="pwm")
            p.wait("scalar", "pwm", p.cnt["pwm"])
            p.on("scalar", lambda e, pt=pt, sl=sl: e.activation(
                PA[:, sl], pt[0:64, 0:512], AF.Copy), sem="pw2")
            p.on("scalar", lambda e, pu=pu, sl=sl: e.activation(
                PB[:, sl], pu[0:64, 0:512], AF.Copy), sem="pw2")
        p.wait("tensor", "pw2", p.cnt["pw2"])
        p.wait("vector", "pw2", p.cnt["pw2"])
        if FFT_DBG[0] == 3:
            p.wait("sync", "pw2", p.cnt["pw2"])
            p.dma("sync", yvo[rs, :], Xst[0:64, 0:S], "sty")
            p.wait("sync", "sty", 16)
            return p.build()
        b_i1e = Z("i1e")
        for c in range(64):
            slot = c % 4
            pt = (ps3, ps4, ps5, ps6)[slot]
            po = 0
            rre = PA[:, c * 128:(c + 1) * 128]
            rim = PB[:, c * 128:(c + 1) * 128]
            if c >= 4 or b_i1e > 0:
                p.wait("tensor", "i1e", b_i1e + 2 * max(0, c - 3))
            p.on("tensor", lambda e, pt=pt, po=po, rre=rre: e.matmul(
                pt[0:64, po:po + 128], cns["wi1r"][0:64, :], rre, start=True, stop=False))
            p.on("tensor", lambda e, pt=pt, po=po, rim=rim: e.matmul(
                pt[0:64, po:po + 128], cns["wi1in"][0:64, :], rim, start=False, stop=True))
            p.on("tensor", lambda e, pt=pt, po=po, rim=rim: e.matmul(
                pt[64:128, po:po + 128], cns["wi1r"][0:64, :], rim, start=True, stop=False))
            p.on("tensor", lambda e, pt=pt, po=po, rre=rre: e.matmul(
                pt[64:128, po:po + 128], cns["wi1i"][0:64, :], rre, start=False, stop=True),
                sem="i1m")
            p.wait("scalar", "i1m", p.cnt["i1m"])
            csl = slice(c * 128, (c + 1) * 128)
            p.on("scalar", lambda e, pt=pt, po=po, csl=csl: e.activation(
                pr1[0:64, csl], pt[0:64, po:po + 128], AF.Copy), sem="i1e")
            p.on("scalar", lambda e, pt=pt, po=po, csl=csl: e.activation(
                pr2[0:64, csl], pt[64:128, po:po + 128], AF.Copy), sem="i1e")
        p.wait("vector", "i1e", p.cnt["i1e"])
        p.wait("vector", "i1m", p.cnt["i1m"])
        if FFT_DBG[0] == 4:
            p.wait("sync", "i1e", p.cnt["i1e"])
            p.dma("sync", yvo[rs, :], pr1[0:64, 0:S], "sty")
            p.wait("sync", "sty", 16)
            return p.build()
        b_tp = Z("tp")
        for n in range(16):
            sl = slice(n * 512, (n + 1) * 512)
            if b_tp + 2 * n > 0:
                p.wait("vector", "tp", b_tp + 2 * n)
            tir_b = cns["tir"][:].rearrange("p n -> p () n").broadcast_to((64, 4, 128))
            tii_b = cns["tii"][:].rearrange("p n -> p () n").broadcast_to((64, 4, 128))
            da = pr1[0:64, sl].rearrange("p (c n) -> p c n", n=128)
            db = pr2[0:64, sl].rearrange("p (c n) -> p c n", n=128)
            w1_ = t1b[0:64, 0:512].rearrange("p (c n) -> p c n", n=128)
            w2_ = t1b[0:64, 512:1024].rearrange("p (c n) -> p c n", n=128)
            w3_ = t1b[0:64, 1024:1536].rearrange("p (c n) -> p c n", n=128)
            w4_ = t1b[0:64, 1536:2048].rearrange("p (c n) -> p c n", n=128)
            for (dst, a, wgt) in ((w1_, da, tir_b), (w2_, db, tii_b),
                                  (w3_, db, tir_b), (w4_, da, tii_b)):
                p.on("vector", lambda e, dst=dst, a=a, wgt=wgt: e.tensor_tensor(
                    dst, a, wgt, ALU.mult), sem="tpm")
            p.wait("vector", "tpm", p.cnt["tpm"])
            p.on("vector", lambda e, sl=sl: e.tensor_tensor(
                PA[:, sl], t1b[0:64, 0:512], t1b[0:64, 512:1024], ALU.subtract),
                sem="tp")
            p.on("vector", lambda e, sl=sl: e.tensor_tensor(
                PB[:, sl], t1b[0:64, 1024:1536], t1b[0:64, 1536:2048], ALU.add),
                sem="tp")
        p.wait("tensor", "tp", p.cnt["tp"])
        if FFT_DBG[0] == 5:
            p.wait("sync", "tp", p.cnt["tp"])
            p.dma("sync", yvo[rs, :], PA[0:64, 0:S], "sty")
            p.wait("sync", "sty", 16)
            return p.build()
        b_itE = Z("itE")
        for c in range(64):
            if b_itE + 2 * c > 0:
                p.wait("tensor", "itE", b_itE + 2 * c)
            for (srow, dps) in ((PA[:, c * 128:(c + 1) * 128], pb1),
                                (PB[:, c * 128:(c + 1) * 128], pb2)):
                p.on("tensor", lambda e, srow=srow, dps=dps: e.transpose(
                    dps[0:128, 0:64], srow, cns["idn2"][0:64, 0:64]), sem="itT")
            p.wait("scalar", "itT", p.cnt["itT"])
            p.on("scalar", lambda e, c=c: e.activation(
                Btr[:, c * 64:(c + 1) * 64], pb1[:, 0:64], AF.Copy), sem="itE")
            p.on("scalar", lambda e, c=c: e.activation(
                Bti[:, c * 64:(c + 1) * 64], pb2[:, 0:64], AF.Copy), sem="itE")
        p.wait("tensor", "itE", p.cnt["itE"])
        if FFT_DBG[0] == 6:
            p.wait("sync", "itE", p.cnt["itE"])
            p.dma("sync", yvo[rs, :], Btr[0:64, 0:S], "sty")
            p.wait("sync", "sty", 16)
            return p.build()
        b_ye = Z("ye")
        for n in range(8):
            sl = slice(n * 512, (n + 1) * 512)
            if b_ye + max(0, n - 1) > 0:
                p.wait("tensor", "ye", b_ye + max(0, n - 1))
            pt = (ps3, ps4)[n % 2]
            p.on("tensor", lambda e, pt=pt, sl=sl: e.matmul(
                pt[0:64, 0:512], cns["wi2r"][:], Btr[:, sl], start=True, stop=False))
            p.on("tensor", lambda e, pt=pt, sl=sl: e.matmul(
                pt[0:64, 0:512], cns["wi2in"][:], Bti[:, sl], start=False, stop=True),
                sem="ym")
            p.wait("scalar", "ym", p.cnt["ym"])
            if b_ye + max(0, n - 1) > 0:
                p.wait("scalar", "ye", b_ye + max(0, n - 1))
            p.on("scalar", lambda e, pt=pt, sl=sl: e.activation(
                yb[:, sl], pt[0:64, 0:512], AF.Copy), sem="ye")
        p.on("tensor", lambda e, blk=blk: e.matmul(
            ps1[0:64, 0:64], on2[0:1, 0:64], fb_sb[0:1, blk * 64:(blk + 1) * 64],
            start=True, stop=True), sem="fbm")
        p.wait("vector", "fbm", blk + 1)
        p.on("vector", lambda e: e.tensor_copy(fbb[:], ps1[0:64, 0:64]), sem="fbc")
        p.wait("vector", "ye", p.cnt["ye"])
        p.wait("vector", "fbc", blk + 1)
        b_ep2 = Z("ep2")
        for n in range(8):
            sl = slice(n * 512, (n + 1) * 512)
            fb_b = fbb[:, n * 8:(n + 1) * 8].rearrange("p c -> p c ()").broadcast_to((64, 8, 64))
            xin3 = xin[0:64, sl].rearrange("p (c n) -> p c n", n=64)
            t13 = t1b[0:64, 0:512].rearrange("p (c n) -> p c n", n=64)
            if b_ep2 + n > 0:
                p.wait("vector", "ep2", b_ep2 + n)
            p.on("vector", lambda e, xin3=xin3, t13=t13, fb_b=fb_b: e.tensor_tensor(
                t13, xin3, fb_b, ALU.mult), sem="ep1")
            p.wait("vector", "ep1", p.cnt["ep1"])
            p.on("vector", lambda e, sl=sl: e.tensor_tensor(
                t1b[0:64, 512:1024], yb[:, sl], t1b[0:64, 0:512], ALU.add), sem="ep2")
            p.wait("vector", "ep2", p.cnt["ep2"])
            if nsty > 0:
                p.wait("vector", "sty", 16 * nsty)
            p.on("vector", lambda e, sl=sl: e.tensor_tensor(
                ov[:, 0:512], t1b[0:64, 512:1024], x0b[0:64, sl], ALU.mult), sem="ep3")
            p.wait("sync", "ep3", p.cnt["ep3"])
            p.dma("sync", yvo[rs, sl], ov[:, 0:512], "sty")
            nsty += 1
    p.wait("sync", "sty", 16 * nsty)
    return p.build()



def _l1_consts():
    S = 4096
    zf = np.zeros((3, S), np.float32)
    t_ = np.linspace(0.0, 1.0, S)
    tr = np.linspace(0.0, S - 1.0, S)
    w_ = 2 * np.pi * tr / S
    f_ = 1e-4
    zf[0] = t_
    zf[1] = np.cos(f_ * w_)
    zf[2] = -np.sin(f_ * w_)
    deltas = np.abs(np.linspace(np.log(0.01) / 1.5, np.log(0.01) / 0.3, 1024)).astype(np.float32)
    return zf, t_.astype(np.float32), deltas

def relay_fft(a):
    return np.ascontiguousarray(a.reshape(8, 64, 64, 64).transpose(0, 2, 1, 3).reshape(512, 4096))

def unrelay_fft(a):
    return np.ascontiguousarray(a.reshape(8, 64, 64, 64).transpose(0, 2, 1, 3).reshape(512, 4096))

def run_layer1(x_fm_shards, inp):
    xn = run_ln(x_fm_shards)
    nc = cached(("l1up",), build_projN, D, [{"out": "up", "m": 3072, "bias": True}])
    w_in = np.ascontiguousarray(inp["L1_attn_in_w"], dtype=np.float32)
    b_in = tile_vec(inp["L1_attn_in_b"])
    res = run(nc, [{"x": np.asarray(xn[c], dtype=BF), "w0": w_in, "b0": b_in}
                   for c in range(NC)])
    ups = [r["up"] for r in res]
    zf, trow, deltas = _l1_consts()
    up_sl = []
    for c in range(NC):
        j = c % 2
        pair = pair_cat(ups, c)
        rows = np.concatenate([pair[j * 512:(j + 1) * 512],
                               pair[1024 + j * 512:1024 + (j + 1) * 512],
                               pair[2048 + j * 512:2048 + (j + 1) * 512]], axis=0)
        up_sl.append(np.ascontiguousarray(rows))
    nc = cached(("l1conv",), build_l1conv)
    sw = np.asarray(inp["L1_attn_short_w"], dtype=np.float32)
    sb = np.asarray(inp["L1_attn_short_b"], dtype=np.float32)
    mlp = {k: np.ascontiguousarray(inp[f"L1_attn_mlp_{k}"], dtype=np.float32)
           for k in ("w1", "b1", "w2", "b2", "w3", "b3", "wo")}
    freq = np.asarray(inp["L1_attn_freq"], dtype=np.float32).reshape(64)
    ims = []
    for c in range(NC):
        j = c % 2
        chs = np.r_[j * 512:(j + 1) * 512]
        rows3 = np.concatenate([chs, 1024 + chs, 2048 + chs])
        twp = np.zeros((128, 36), np.float32)
        swr = sw[rows3]
        sbr = sb[rows3]
        for cc in range(12):
            for jj in range(3):
                twp[:, 3 * cc + jj] = swr[cc * 128:(cc + 1) * 128, jj]
        ims.append({"up": up_sl[c], "tw": twp, "tb": tile_vec(sbr), "zf": zf,
                    "w1": mlp["w1"], "w2": mlp["w2"], "w3": mlp["w3"],
                    "b123": np.ascontiguousarray(np.stack([mlp["b1"], mlp["b2"], mlp["b3"]], 1)),
                    "freqc": np.ascontiguousarray(np.stack([freq, np.full(64, -np.pi, np.float32)], 1)),
                    "wo": np.ascontiguousarray(mlp["wo"][:, chs]),
                    "negd": tile_vec(-deltas[chs]),
                    "trow": trow.reshape(1, 4096), "on2d": ONES_ROW})
    res = run(nc, ims)
    ncf = cached(("l1fft",), build_fft)
    global FFTC
    if FFTC is None:
        FFTC = fft_consts()
    fb = np.asarray(inp["L1_attn_filt_bias"], dtype=np.float32)
    ims2 = []
    for c in range(NC):
        j = c % 2
        chs = np.r_[j * 512:(j + 1) * 512]
        ims2.append({"vxd": relay_fft(np.asarray(res[c]["vx"])),
                     "ktd": relay_fft(np.asarray(res[c]["kt"])),
                     "x0d": relay_fft(np.asarray(res[c]["x0"])),
                     "fbr": np.ascontiguousarray(fb[chs].reshape(1, 512)),
                     "on2d": ONES_ROW, **FFTC})
    res2 = run(ncf, ims2)
    yv_sl = [unrelay_fft(np.asarray(r["yvd"])) for r in res2]
    yvs = []
    for c in range(NC):
        b = (c // 2) * 2
        tcols = np.s_[(c % 2) * T:(c % 2 + 1) * T]
        full = np.empty((1024, T), dtype=BF)
        full[0:512] = yv_sl[b][:, tcols]
        full[512:1024] = yv_sl[b + 1][:, tcols]
        yvs.append(full)
    ncg = cached(("gateup", D, 3072, "silu", True, "i32"), build_gateup, D, 3072,
                 AF.Silu, True, T, BF16, I32, I32)
    wg = np.ascontiguousarray(inp["L1_ffn_w_gate"].T)
    wu = np.ascontiguousarray(inp["L1_ffn_w_up"].T)
    s127 = tile_vec(np.full(3072, 1.0 / 127.0, np.float32))
    res3 = run(ncg, [{"x": np.asarray(xn[c], dtype=BF), "w0": wg, "w1": wu,
                      "s0": s127, "s1": s127} for c in range(NC)])
    hs = [r["y"] for r in res3]
    ncc = cached(("combine", D, 3072, True, True, "i32w2"), build_combine, D, 3072,
                 True, True, T, F32, I32)
    wo_l = np.ascontiguousarray(inp["L1_attn_out_w"], dtype=np.float32)
    wd = np.ascontiguousarray(inp["L1_ffn_w_down"].T)
    bo = tile_vec(inp["L1_attn_out_b"])
    s127d = tile_vec(np.full(1024, 1.0 / 127.0, np.float32))
    res4 = run(ncc, [{"x": np.ascontiguousarray(x_fm_shards[c], dtype=np.float32),
                      "a": yvs[c], "h": hs[c], "w1": wo_l, "w2": wd,
                      "b1": bo, "s2": s127d} for c in range(NC)])
    return [r["xo"] for r in res4]



def build_linattn(S=4096, Tn=T, H=16):
    p = Prog()
    SC = S // 128
    NT = Tn // 512
    qd = p.din("q", [D, Tn], BF16)
    ktm = p.din("ktm", [S, H * 64], BF16)
    vt = p.din("vt", [H, S * 65], BF16)
    on2d = p.din("on2d", [1, 128], F32)
    ctx = p.dout("ctx", [D, Tn], BF16)
    q_sb = p.sbuf("q_sb", [128, (D // 128) * Tn], BF16)
    k_sb = p.sbuf("k_sb", [128, SC * 64], BF16)
    v_sb = p.sbuf("v_sb", [128, SC * 65], BF16)
    kv_sb = p.sbuf("kv_sb", [128, 65], BF16)
    ctx_sb = p.sbuf("ctx_sb", [128, (D // 128) * Tn], BF16)
    on2 = p.sbuf("on2", [1, 128], F32)
    c1 = p.sbuf("c1", [64, 512], F32)
    rrow = p.sbuf("rrow", [1, 512], F32)
    ps_kv = p.psum("ps_kv", [64, 512], F32)
    ps_av = p.psum("ps_av", [65, 512], F32)
    ps_bc = p.psum("ps_bc", [64, 512], F32)
    load_fm(p, "sync", q_sb, qd, D // 128, Tn, "ld")
    p.dma("sync", on2[:], on2d, "ld")
    ldv = p.cnt["ld"]
    p.wait("tensor", "ld", ldv)
    kvm = ave = ctm = 0
    for h in range(H):
        if h > 0:
            p.wait("gpsimd", "kvm", h)
        p.dma("gpsimd", k_sb[:].rearrange("p (kc m) -> p kc m", m=64),
              ktm[:, h * 64:(h + 1) * 64].rearrange("(kc p) m -> p kc m", p=128), "lk")
        p.dma("gpsimd", v_sb[:].rearrange("p (kc m) -> p kc m", m=65),
              vt[h:h + 1, :].rearrange("one (kc pp m) -> (one pp) kc m", pp=128, m=65),
              "lk")
        p.wait("tensor", "lk", 32 * (h + 1))
        for kc in range(SC):
            p.on("tensor", lambda e, kc=kc, st=(kc == 0), sp=(kc == SC - 1): e.matmul(
                ps_kv[:, 0:65], chunk(k_sb, kc, 64), chunk(v_sb, kc, 65),
                start=st, stop=sp), sem="kvm" if kc == SC - 1 else None)
        kvm = p.cnt["kvm"]
        p.wait("scalar", "kvm", kvm)
        if h > 0:
            p.wait("scalar", "avm", ave)
        p.on("scalar", lambda e: e.activation(kv_sb[0:64, :], ps_kv[:, 0:65], AF.Copy),
             sem="kve")
        p.on("scalar", lambda e: e.activation(kv_sb[64:128, :], ps_kv[:, 0:65], AF.Copy),
             sem="kve")
        p.wait("tensor", "kve", p.cnt["kve"])
        kc_ = h // 2
        base = (h % 2) * 64
        for qt in range(NT):
            qsl = slice(qt * 512, (qt + 1) * 512)
            if ctm > 0:
                p.wait("tensor", "ctm", ctm)
            p.on("tensor", lambda e, kc_=kc_, base=base, qsl=qsl: e.matmul(
                ps_av[:, 0:512], kv_sb[base:base + 64, :],
                chunk(q_sb, kc_, Tn)[base:base + 64, qsl], start=True, stop=True),
                sem="avm")
            ave = p.cnt["avm"]
            p.wait("vector", "avm", ave)
            p.on("vector", lambda e: e.tensor_scalar(
                rrow[:], ps_av[64:65, 0:512], 1e-6, None, ALU.add), sem="rr0")
            p.wait("vector", "rr0", p.cnt["rr0"])
            p.on("vector", lambda e: e.reciprocal(rrow[:], rrow[:]), sem="rrd")
            p.wait("tensor", "rrd", p.cnt["rrd"])
            p.on("tensor", lambda e: e.matmul(ps_bc[:, 0:512], on2[0:1, 0:64], rrow[:],
                                              start=True, stop=True), sem="bcm")
            p.wait("scalar", "avm", ave)
            if ctm > 0:
                p.wait("scalar", "ctm", ctm)
            p.on("scalar", lambda e: e.activation(c1[:], ps_av[0:64, 0:512], AF.Copy),
                 sem="c1e")
            p.wait("vector", "bcm", p.cnt["bcm"])
            p.wait("vector", "c1e", p.cnt["c1e"])
            p.on("vector", lambda e, kc_=kc_, base=base, qsl=qsl: e.tensor_tensor(
                chunk(ctx_sb, kc_, Tn)[base:base + 64, qsl], c1[:], ps_bc[0:64, 0:512],
                ALU.mult), sem="ctm")
            ctm = p.cnt["ctm"]
    p.wait("sync", "ctm", ctm)
    for c in range(D // 128):
        p.dma("sync", ctx[c * 128:(c + 1) * 128, :], chunk(ctx_sb, c, Tn), "sto")
    p.wait("sync", "sto", p.cnt["sto"])
    return p.build()



def build_router():
    p = Prog()
    Tn = T
    x = p.din("x", [D, Tn], F32)
    wr = p.din("wr", [D, 8], F32)
    iot = p.din("iot", [128, 8], F32)
    rt = p.dout("route", [Tn, 18], F32)
    x_sb = p.sbuf("x_sb", [128, (D // 128) * Tn], F32)
    wr_sb = p.sbuf("wr_sb", [128, (D // 128) * 8], F32)
    iot_sb = p.sbuf("iot_sb", [128, 8], F32)
    o_sb = p.sbuf("o_sb", [128, 18], F32)
    sc_sb = p.sbuf("sc_sb", [128, 40], F32)
    ps = p.psum("ps", [128, 8], F32)
    load_fm(p, "sync", x_sb, x, D // 128, Tn, "ld")
    p.dma("sync", wr_sb[:].rearrange("p (kc m) -> p kc m", m=8),
          wr.rearrange("(kc p) m -> p kc m", p=128), "ld")
    p.dma("sync", iot_sb[:], iot, "ld")
    ldv = p.cnt["ld"]
    p.wait("tensor", "ld", ldv)
    p.wait("vector", "ld", ldv)
    V = "v"
    def vop(fn):
        p.on("vector", fn, sem=V)
        p.wait("vector", V, p.cnt[V])
    for tc in range(Tn // 128):
        if tc > 0:
            p.wait("tensor", "sto", 16 * tc)
            p.wait("tensor", V, p.cnt[V])
        for kc in range(D // 128):
            p.on("tensor", lambda e, tc=tc, kc=kc, st=(kc == 0), sp=(kc == D // 128 - 1):
                 e.matmul(ps[:], chunk(x_sb, kc, Tn)[:, tc * 128:(tc + 1) * 128],
                          chunk(wr_sb, kc, 8), start=st, stop=sp),
                 sem="mm" if kc == D // 128 - 1 else None)
        p.wait("vector", "mm", p.cnt["mm"])
        if tc > 0:
            p.wait("vector", "sto", 16 * tc)
        pr = sc_sb[:, 0:8]
        mx = sc_sb[:, 8:9]
        sm = sc_sb[:, 9:10]
        t1_ = sc_sb[:, 10:18]
        t2_ = sc_sb[:, 18:26]
        e1 = sc_sb[:, 26:27]
        e2 = sc_sb[:, 27:28]
        vop(lambda e: e.tensor_reduce(mx, ps[:], AX.X, ALU.max))
        vop(lambda e: e.tensor_tensor(pr, ps[:], mx.broadcast_to((128, 8)), ALU.subtract))
        p.wait("scalar", V, p.cnt[V])
        p.on("scalar", lambda e: e.activation(pr, pr, AF.Exp), sem="ex")
        p.wait("vector", "ex", p.cnt["ex"])
        vop(lambda e: e.tensor_reduce(sm, pr, AX.X, ALU.add))
        vop(lambda e: e.reciprocal(sm, sm))
        vop(lambda e: e.tensor_tensor(pr, pr, sm.broadcast_to((128, 8)), ALU.mult))
        vop(lambda e: e.tensor_reduce(mx, pr, AX.X, ALU.max))
        vop(lambda e: e.tensor_tensor(t1_, pr, mx.broadcast_to((128, 8)), ALU.is_ge))
        vop(lambda e: e.tensor_scalar(t2_, t1_, -100.0, 100.0, ALU.mult, op1=ALU.add))
        vop(lambda e: e.tensor_tensor(t2_, t2_, iot_sb[:], ALU.add))
        vop(lambda e: e.tensor_reduce(e1, t2_, AX.X, ALU.min))
        vop(lambda e: e.tensor_tensor(o_sb[:, 0:8], iot_sb[:], e1.broadcast_to((128, 8)),
                                      ALU.is_equal))
        vop(lambda e: e.tensor_copy(o_sb[:, 16:17], mx))
        vop(lambda e: e.tensor_scalar(t1_, o_sb[:, 0:8], -1.0, 1.0, ALU.mult, op1=ALU.add))
        vop(lambda e: e.tensor_tensor(pr, pr, t1_, ALU.mult))
        vop(lambda e: e.tensor_reduce(mx, pr, AX.X, ALU.max))
        vop(lambda e: e.tensor_tensor(t1_, pr, mx.broadcast_to((128, 8)), ALU.is_ge))
        vop(lambda e: e.tensor_scalar(t2_, t1_, -100.0, 100.0, ALU.mult, op1=ALU.add))
        vop(lambda e: e.tensor_tensor(t2_, t2_, iot_sb[:], ALU.add))
        vop(lambda e: e.tensor_reduce(e2, t2_, AX.X, ALU.min))
        vop(lambda e: e.tensor_tensor(o_sb[:, 8:16], iot_sb[:], e2.broadcast_to((128, 8)),
                                      ALU.is_equal))
        p.on("vector", lambda e: e.tensor_copy(o_sb[:, 17:18], mx), sem="fin")
        p.wait("sync", "fin", p.cnt["fin"])
        p.dma("sync", rt[tc * 128:(tc + 1) * 128, :], o_sb[:], "sto")
    p.wait("sync", "sto", p.cnt["sto"])
    return p.build()



def build_routing(NA=2 * 4 * 4096, CAP=4096):
    p = Prog()
    NCH = NA // 128
    A = p.din("A", [128, (NA // 128) * 8], F32)
    ltri = p.din("ltri", [128, 128], F32)
    onec = p.din("onec", [128, 1], F32)
    on2d = p.din("on2d", [1, 128], F32)
    ko = p.dout("keep", [128, NA // 128], F32)
    co = p.dout("cnt", [128, NA // 128], F32)
    pd = p.dout("pred", [128, (NA // 128) * 8], F32)
    a_sb = p.sbuf("a_sb", [128, NCH * 8], F32)
    lt_sb = p.sbuf("lt_sb", [128, 128], F32)
    on_sb = p.sbuf("on_sb", [128, 1], F32)
    on2 = p.sbuf("on2", [1, 128], F32)
    offs = p.sbuf("offs", [1, 16], F32)
    pre = p.sbuf("pre", [128, 8], F32)
    tt = p.sbuf("tt", [128, 24], F32)
    kout = p.sbuf("kout", [128, NCH], F32)
    cout = p.sbuf("cout", [128, NCH], F32)
    pall = p.sbuf("pall", [128, NCH * 8], F32)
    ps1 = p.psum("ps1", [128, 8], F32)
    ps2 = p.psum("ps2", [128, 8], F32)
    ps3 = p.psum("ps3", [1, 8], F32)
    p.dma("sync", a_sb[:], A, "ld")
    p.dma("sync", lt_sb[:], ltri, "ld")
    p.dma("sync", on_sb[:], onec, "ld")
    p.dma("sync", on2[:], on2d, "ld")
    ldv = p.cnt["ld"]
    p.wait("tensor", "ld", ldv)
    p.wait("vector", "ld", ldv)
    p.on("vector", lambda e: e.memset(offs[:], 0.0), sem="of")
    V = "v"
    def vop(fn):
        p.on("vector", fn, sem=V)
        p.wait("vector", V, p.cnt[V])
    for i in range(NCH):
        ac = chunk(a_sb, i, 8)
        if i > 0:
            p.wait("tensor", V, p.cnt[V])
        p.on("tensor", lambda e, ac=ac: e.matmul(ps1[:], lt_sb[:], ac, start=True, stop=True))
        p.on("tensor", lambda e, ac=ac: e.matmul(ps3[:], on_sb[:], ac, start=True, stop=True))
        p.wait("tensor", "of", p.cnt["of"])
        p.on("tensor", lambda e: e.matmul(ps2[:], on2[:], offs[0:1, 0:8], start=True, stop=True),
             sem="mm")
        p.wait("vector", "mm", p.cnt["mm"])
        vop(lambda e: e.tensor_copy(pre[:], ps1[:]))
        vop(lambda e: e.tensor_tensor(pre[:], pre[:], ps2[:], ALU.add))
        vop(lambda e, i=i: e.tensor_copy(chunk(pall, i, 8), pre[:]))
        vop(lambda e, CAP=CAP: e.tensor_scalar(tt[:, 0:8], pre[:], CAP + 0.5, None, ALU.is_lt))
        vop(lambda e, ac=ac: e.tensor_tensor(tt[:, 8:16], tt[:, 0:8], ac, ALU.mult))
        vop(lambda e, i=i: e.tensor_reduce(kout[:, i:i + 1], tt[:, 8:16], AX.X, ALU.add))
        vop(lambda e, ac=ac: e.tensor_tensor(tt[:, 16:24], pre[:], ac, ALU.mult))
        vop(lambda e, i=i: e.tensor_reduce(cout[:, i:i + 1], tt[:, 16:24], AX.X, ALU.add))
        vop(lambda e: e.tensor_copy(offs[0:1, 8:16], ps3[0:1, :]))
        p.on("vector", lambda e: e.tensor_tensor(offs[0:1, 0:8], offs[0:1, 0:8],
                                                 offs[0:1, 8:16], ALU.add), sem="of")
        p.wait("vector", "of", p.cnt["of"])
    p.wait("sync", V, p.cnt[V])
    p.dma("sync", ko, kout[:], "sto")
    p.dma("sync", co, cout[:], "sto")
    p.dma("sync", pd, pall[:], "sto")
    p.wait("sync", "sto", p.cnt["sto"])
    return p.build()



def build_moe(Tn=T):
    p = Prog()
    KC = D // 128
    FC = 2048 // 128
    NT = Tn // 512
    xn = p.din("xn", [D, Tn], BF16)
    oh1 = p.din("oh1", [8, Tn], F32)
    oh2 = p.din("oh2", [8, Tn], F32)
    mrow = p.din("mrow", [1, 4 * Tn], F32)
    wg = p.din("wg", [8 * D, 2048], F32)
    wu = p.din("wu", [8 * D, 2048], F32)
    wd = p.din("wd", [8 * 2048, D], F32)
    nw = p.din("nw", [128, (D // 128) * 8], F32)
    fnorm = p.din("fnorm", [128, KC], F32)
    fwg = p.din("fwg", [D, 512], F32)
    fwu = p.din("fwu", [D, 512], F32)
    fwd = p.din("fwd", [512, D], F32)
    onec = p.din("onec", [128, 1], F32)
    on2d = p.din("on2d", [1, 128], F32)
    fo = p.dout("fo", [D, Tn], F32)
    mskd = p.dtmp("mskd", [8, Tn], BF16)

    xn_sb = p.sbuf("xn_sb", [128, KC * Tn], BF16)
    mr_sb = p.sbuf("mr_sb", [1, 4 * Tn], BF16)
    on2b = p.sbuf("on2b", [1, 128], BF16)
    sc_sb = p.sbuf("sc_sb", [1, Tn], F32)
    scb2 = p.sbuf("scb2", [1, 2 * Tn], BF16)
    nw_sb = p.sbuf("nw_sb", [128, KC * 8], F32)
    fn_sb = p.sbuf("fn_sb", [128, KC], F32)
    ze = p.sbuf("ze", [128, KC * Tn], BF16)
    gt = p.sbuf("gt", [128, FC * Tn], BF16)
    acc = p.sbuf("acc", [128, KC * Tn], BF16)
    u1 = p.sbuf("u1", [128, Tn], F32)
    mske = mr_sb[0:1, 0:Tn]
    fh = ze
    fg = gt
    rows_sb = gt.bitcast(F32)[0:8, 2 * Tn:4 * Tn]
    onec_sb = p.sbuf("onec_sb", [128, 1], F32)
    on2 = p.sbuf("on2", [1, 128], F32)
    wbuf0 = p.sbuf("wbuf0", [128, 16 * 128], BF16)
    wbuf = [wbuf0, wbuf0]
    ps1 = p.psum("ps1", [128, Tn], F32)
    ps2 = p.psum("ps2", [128, Tn], F32)

    load_fm(p, "sync", xn_sb, xn, KC, Tn, "ld")
    p.dma("sync", rows_sb[:, 0:Tn], oh1, "ld")
    p.dma("sync", rows_sb[:, Tn:2 * Tn], oh2, "ld")
    p.dma("gpsimd", mr_sb[:], mrow, "ldg")
    p.dma("sync", nw_sb[:], nw, "ld")
    p.dma("sync", fn_sb[:], fnorm, "ld")
    p.dma("sync", onec_sb[:], onec, "ld")
    p.dma("sync", on2[:], on2d, "ld")
    p.dma("gpsimd", on2b[:], on2d, "ldg")
    ldv = p.cnt["ld"]
    for eng in ("tensor", "vector", "scalar"):
        p.wait(eng, "ld", ldv)
        p.wait(eng, "ldg", p.cnt["ldg"])

    rstd = sc_sb[0:1, 0:Tn]
    sfac = scb2[0:1, 0:Tn]
    ovf = scb2[0:1, Tn:2 * Tn]
    V = "v"
    def vop(fn):
        p.on("vector", fn, sem=V)
        p.wait("vector", V, p.cnt[V])
    for c in range(KC):
        p.on("scalar", lambda e, c=c: e.activation(u1[:], chunk(xn_sb, c, Tn), AF.Square),
             sem="sq")
        p.wait("tensor", "sq", p.cnt["sq"])
        for n in range(NT):
            sl = slice(n * 512, (n + 1) * 512)
            p.on("tensor", lambda e, sl=sl, st=(c == 0), sp=(c == KC - 1):
                 e.matmul(ps1[0:1, sl], onec_sb[:], u1[:, sl], start=st, stop=sp),
                 sem="smm" if c == KC - 1 else ("sqr" if n == NT - 1 else None))
        if c < KC - 1:
            p.wait("scalar", "sqr", p.cnt["sqr"])
    p.wait("vector", "smm", p.cnt["smm"])
    vop(lambda e: e.tensor_scalar(rstd, ps1[0:1, :], 1.0 / D, 1e-6, ALU.mult, op1=ALU.add))
    vop(lambda e: e.reciprocal(rstd, rstd))
    p.wait("scalar", V, p.cnt[V])
    p.on("scalar", lambda e: e.activation(rstd, rstd, AF.Sqrt), sem="rsq")
    p.wait("vector", "rsq", 1)
    p.wait("tensor", "rsq", 1)
    m1r = mr_sb[0:1, 0:Tn]
    m2r = mr_sb[0:1, Tn:2 * Tn]
    k1r = mr_sb[0:1, 2 * Tn:3 * Tn]
    k2r = mr_sb[0:1, 3 * Tn:4 * Tn]
    vop(lambda e: e.tensor_tensor(ovf, k1r, k2r, ALU.mult))
    vop(lambda e: e.tensor_tensor(sfac, m1r, ovf, ALU.mult))
    w2k = m2r
    vop(lambda e: e.tensor_tensor(w2k, m2r, ovf, ALU.mult))
    p.wait("tensor", V, p.cnt[V])
    for n in range(Tn // 512):
        sl = slice(n * 512, (n + 1) * 512)
        p.on("tensor", lambda e, sl=sl: e.matmul(ps1[0:8, sl], on2b[0:1, 0:8], sfac[0:1, sl], start=True, stop=True))
        p.on("tensor", lambda e, sl=sl: e.matmul(ps2[0:8, sl], on2b[0:1, 0:8], w2k[0:1, sl], start=True, stop=True),
             sem="bcm")
    p.wait("vector", "bcm", p.cnt["bcm"])
    vop(lambda e: e.tensor_tensor(u1[0:8, 0:Tn], rows_sb[:, 0:Tn], ps1[0:8, :], ALU.mult))
    oh2t = ze.bitcast(F32)[0:8, 0:Tn]
    vop(lambda e: e.tensor_tensor(oh2t, rows_sb[:, Tn:2 * Tn], ps2[0:8, :], ALU.mult))
    vop(lambda e: e.tensor_tensor(u1[0:8, 0:Tn], u1[0:8, 0:Tn], oh2t, ALU.add))
    p.wait("gpsimd", V, p.cnt[V])
    p.dma("gpsimd", mskd[:, :], u1[0:8, 0:Tn], "mskst")
    vop(lambda e: e.tensor_tensor(sfac, sfac, w2k, ALU.add))
    vop(lambda e: e.tensor_tensor(sfac, sfac, ovf, ALU.add))
    vop(lambda e: e.tensor_scalar(ovf, ovf, -1.0, 1.0, ALU.mult, op1=ALU.add))
    p.wait("tensor", V, p.cnt[V])
    p.wait("scalar", V, p.cnt[V])

    for n in range(NT):
        sl = slice(n * 512, (n + 1) * 512)
        p.on("tensor", lambda e, sl=sl: e.matmul(ps1[:, sl], on2[:], rstd[0:1, sl],
                                                 start=True, stop=True), sem="rbc")
    p.wait("vector", "rbc", p.cnt["rbc"])
    for c in range(KC):
        p.on("vector", lambda e, c=c: e.scalar_tensor_tensor(
            chunk(fh, c, Tn), chunk(xn_sb, c, Tn), fn_sb[:, c:c + 1], ps1[:],
            ALU.mult, ALU.mult), sem="fhd")
    p.wait("tensor", "fhd", p.cnt["fhd"])
    p.wait("scalar", "mskst", p.cnt["mskst"])
    p.wait("vector", "mskst", p.cnt["mskst"])
    fgw_sb = wbuf0
    for i, wsrc in enumerate((fwg, fwu)):
        for m in range(4):
            if p.cnt.get("fmm", 0) > 0:
                p.wait("gpsimd", "fmm", p.cnt["fmm"])
            if p.cnt.get("fgm", 0) > 0:
                p.wait("gpsimd", "fgm", p.cnt["fgm"])
            p.dma("gpsimd", fgw_sb[:, 0:KC * 128].rearrange("p (kc mm) -> p kc mm", mm=128),
                  wsrc.rearrange("(kc p) mm -> p kc mm", p=128)[:, :, m * 128:(m + 1) * 128],
                  "fwl")
            p.wait("tensor", "fwl", 16 * (4 * i + m + 1))
            if 4 * i + m >= 1:
                p.wait("tensor", "fev", p.cnt["fev"])
            for k in range(KC):
                for n in range(NT):
                    sl = slice(n * 512, (n + 1) * 512)
                    p.on("tensor", lambda e, k=k, sl=sl, st=(k == 0), sp=(k == KC - 1):
                         e.matmul(ps2[:, sl], fgw_sb[:, k * 128:(k + 1) * 128],
                                  chunk(fh, k, Tn)[:, sl], start=st, stop=sp),
                         sem="fmm" if (k == KC - 1 and n == NT - 1) else None)
            p.wait("scalar", "fmm", p.cnt["fmm"])
            if 4 * i + m >= 1:
                p.wait("scalar", "fev", p.cnt["fev"])
            mi = 4 * i + m
            if i == 0:
                p.on("scalar", lambda e, m=m: e.activation(
                    chunk(fg, m, Tn), ps2[:], AF.Gelu), sem="fev")
            else:
                if p.cnt.get("fgm", 0) > 0:
                    p.wait("scalar", "fgm", p.cnt["fgm"])
                p.on("scalar", lambda e: e.activation(u1[:], ps2[:], AF.Copy), sem="fev")
                p.wait("vector", "fev", p.cnt["fev"])
                p.on("vector", lambda e, m=m: e.tensor_tensor(
                    chunk(fg, m, Tn), chunk(fg, m, Tn), u1[:], ALU.mult), sem="fgm")
    p.wait("tensor", "fgm", p.cnt["fgm"])
    fdw_sb = wbuf0
    for m in range(KC):
        p.wait("gpsimd", "fmm", p.cnt["fmm"])
        if p.cnt.get("dmm", 0) > 0:
            p.wait("gpsimd", "dmm", p.cnt["dmm"])
        p.dma("gpsimd", fdw_sb[:, 0:4 * 128].rearrange("p (kc mm) -> p kc mm", mm=128),
              fwd.rearrange("(kc p) mm -> p kc mm", p=128)[:, :, m * 128:(m + 1) * 128],
              "fdl")
        p.wait("tensor", "fdl", 16 * (m + 1))
        if m >= 1:
            p.wait("tensor", "acc", p.cnt["acc"])
        for k in range(4):
            for n in range(NT):
                sl = slice(n * 512, (n + 1) * 512)
                p.on("tensor", lambda e, k=k, sl=sl, st=(k == 0), sp=(k == 3):
                     e.matmul(ps2[:, sl], fdw_sb[:, k * 128:(k + 1) * 128],
                              chunk(fg, k, Tn)[:, sl], start=st, stop=sp),
                     sem="dmm" if (k == 3 and n == NT - 1) else None)
        for n in range(Tn // 512):
            sl = slice(n * 512, (n + 1) * 512)
            p.on("tensor", lambda e, sl=sl: e.matmul(ps1[:, sl], on2b[:], sfac[0:1, sl], start=True, stop=True),
                 sem="dmm" if n == Tn // 512 - 1 else None)
        p.wait("vector", "dmm", p.cnt["dmm"])
        p.on("vector", lambda e, m=m: e.tensor_tensor(u1[:], ps2[:], chunk(xn_sb, m, Tn), ALU.add), sem="du")
        p.wait("tensor", "du", p.cnt["du"])
        for n in range(Tn // 512):
            sl = slice(n * 512, (n + 1) * 512)
            p.on("tensor", lambda e, sl=sl: e.matmul(ps2[:, sl], on2b[:], ovf[0:1, sl], start=True, stop=True),
                 sem="dmm2" if n == Tn // 512 - 1 else None)
        p.wait("vector", "dmm2", p.cnt["dmm2"])
        p.on("vector", lambda e: e.tensor_tensor(u1[:], u1[:], ps2[:], ALU.mult), sem="du2")
        p.wait("vector", "du2", p.cnt["du2"])
        p.on("vector", lambda e, m=m: e.tensor_tensor(chunk(acc, m, Tn), chunk(xn_sb, m, Tn), ps1[:], ALU.mult), sem="du3")
        p.wait("vector", "du3", p.cnt["du3"])
        p.on("vector", lambda e, m=m: e.tensor_tensor(chunk(acc, m, Tn), chunk(acc, m, Tn), u1[:], ALU.add), sem="acc")
    p.wait("tensor", "acc", p.cnt["acc"])

    for ex in range(8):
        for n in range(NT):
            sl = slice(n * 512, (n + 1) * 512)
            p.on("tensor", lambda e, sl=sl: e.matmul(ps1[:, sl], on2[:], rstd[0:1, sl],
                                                     start=True, stop=True), sem="zbc")
        p.wait("vector", "zbc", p.cnt["zbc"])
        if ex > 0:
            p.wait("vector", "gmm", p.cnt["gmm"])
        for c in range(KC):
            p.on("vector", lambda e, c=c, ex=ex: e.scalar_tensor_tensor(
                chunk(ze, c, Tn), chunk(xn_sb, c, Tn), nw_sb[:, c * 8 + ex:c * 8 + ex + 1],
                ps1[:], ALU.mult, ALU.mult), sem="zed")
        p.wait("tensor", "zed", p.cnt["zed"])
        for i, wsrc in enumerate((wg, wu)):
            for m in range(FC):
                wb = wbuf[0]
                gl = "gl"
                nloads = p.cnt.get(gl, 0) // 16
                for sother in ("gmm", "dmm", "dmmx", "fmm"):
                    if p.cnt.get(sother, 0) > 0:
                        p.wait("gpsimd", sother, p.cnt[sother])
                p.dma("gpsimd", wb[:, 0:KC * 128].rearrange("p (kc mm) -> p kc mm", mm=128),
                      wsrc.rearrange("(ec kc p) mm -> p (ec kc) mm", p=128, kc=KC)
                      [:, ex * KC:(ex + 1) * KC, m * 128:(m + 1) * 128], gl)
                p.wait("tensor", gl, 16 * (nloads + 1))
                if p.cnt.get("gev", 0) > 0:
                    p.wait("tensor", "gev", p.cnt["gev"])
                if p.cnt.get("gum", 0) > 0:
                    p.wait("tensor", "gum", p.cnt["gum"])
                for k in range(KC):
                    for n in range(NT):
                        sl = slice(n * 512, (n + 1) * 512)
                        p.on("tensor", lambda e, wb=wb, k=k, sl=sl, st=(k == 0), sp=(k == KC - 1):
                             e.matmul(ps2[:, sl], wb[:, k * 128:(k + 1) * 128],
                                      chunk(ze, k, Tn)[:, sl], start=st, stop=sp),
                             sem="gmm" if (k == KC - 1 and n == NT - 1) else None)
                p.wait("scalar", "gmm", p.cnt["gmm"])
                if i == 0:
                    p.on("scalar", lambda e, m=m: e.activation(
                        chunk(gt, m, Tn), ps2[:], AF.Gelu), sem="gev")
                else:
                    if p.cnt.get("gum", 0) > 0:
                        p.wait("scalar", "gum", p.cnt["gum"])
                    if p.cnt.get("acd", 0) > 0:
                        p.wait("scalar", "acd", p.cnt["acd"])
                    p.on("scalar", lambda e: e.activation(u1[:], ps2[:], AF.Copy), sem="gev")
                    p.wait("vector", "gev", p.cnt["gev"])
                    p.on("vector", lambda e, m=m: e.tensor_tensor(
                        chunk(gt, m, Tn), chunk(gt, m, Tn), u1[:], ALU.mult), sem="gum")
        p.wait("vector", "gum", p.cnt["gum"])
        if ex > 0:
            p.wait("sync", "gma", p.cnt["gma"])
        else:
            p.wait("sync", "mskst", 16)
        p.dma("sync", mske[:], mskd[ex:ex + 1, :], "mskld")
        p.wait("tensor", "mskst", 16)
        p.wait("tensor", "mskld", 16 * (ex + 1))
        for n in range(NT):
            sl = slice(n * 512, (n + 1) * 512)
            p.on("tensor", lambda e, sl=sl: e.matmul(
                ps1[:, sl], on2b[:], mske[0:1, sl], start=True, stop=True), sem="mbc")
        p.wait("vector", "mbc", p.cnt["mbc"])
        for m in range(FC):
            p.on("vector", lambda e, m=m: e.tensor_tensor(
                chunk(gt, m, Tn), chunk(gt, m, Tn), ps1[:], ALU.mult), sem="gma")
        p.wait("tensor", "gma", p.cnt["gma"])
        for m in range(KC):
            wb = wbuf[0]
            dl = "dl"
            ndl = p.cnt.get(dl, 0) // 16
            for sother in ("gmm", "dmmx"):
                if p.cnt.get(sother, 0) > 0:
                    p.wait("gpsimd", sother, p.cnt[sother])
            p.dma("gpsimd", wb[:].rearrange("p (kc mm) -> p kc mm", mm=128),
                  wd.rearrange("(ec kc p) mm -> p (ec kc) mm", p=128, kc=FC)
                  [:, ex * FC:(ex + 1) * FC, m * 128:(m + 1) * 128], dl)
            p.wait("tensor", dl, 16 * (ndl + 1))
            if p.cnt.get("acd", 0) > 0:
                p.wait("tensor", "acd", p.cnt["acd"])
            for k in range(FC):
                for n in range(NT):
                    sl = slice(n * 512, (n + 1) * 512)
                    p.on("tensor", lambda e, wb=wb, k=k, sl=sl, st=(k == 0), sp=(k == FC - 1):
                         e.matmul(ps2[:, sl], wb[:, k * 128:(k + 1) * 128],
                                  chunk(gt, k, Tn)[:, sl], start=st, stop=sp),
                         sem="dmmx" if (k == FC - 1 and n == NT - 1) else None)
            p.wait("vector", "dmmx", p.cnt["dmmx"])
            p.on("vector", lambda e, m=m: e.tensor_tensor(
                chunk(acc, m, Tn), chunk(acc, m, Tn), ps2[:], ALU.add), sem="acd")
    p.wait("scalar", "acd", p.cnt["acd"])
    for c in range(KC):
        p.on("scalar", lambda e, c=c: e.activation(u1[:], chunk(acc, c, Tn), AF.Copy),
             sem="fcp")
        p.wait("sync", "fcp", p.cnt["fcp"])
        p.dma("sync", fo[c * 128:(c + 1) * 128, :], u1[:], "sto")
        p.wait("scalar", "sto", p.cnt["sto"])
    p.wait("sync", "sto", p.cnt["sto"])
    return p.build()


def build_combine_direct(k1, bias1=True, Tn=T):
    p = Prog()
    K1, MC = k1 // 128, D // 128
    x = p.din("x", [D, Tn], F32)
    a = p.din("a", [k1, Tn], BF16)
    f = p.din("f", [D, Tn], F32)
    w1 = p.din("w1", [k1, D], F32)
    xo = p.dout("xo", [D, Tn], F32)
    a_sb = p.sbuf("a_sb", [128, K1 * Tn], BF16)
    x_sb = p.sbuf("x_sb", [128, Tn], F32)
    f_sb = p.sbuf("f_sb", [128, Tn], F32)
    e1 = p.sbuf("e1", [128, Tn], F32)
    o_sb = p.sbuf("o_sb", [128, Tn], F32)
    ps = [p.psum("pspool0", [128, Tn], F32), p.psum("pspool1", [128, Tn], F32)]
    load_fm(p, "sync", a_sb, a, K1, Tn, "ld")
    b_sb = None
    if bias1:
        b = p.din("b1", [128, MC], F32)
        b_sb = p.sbuf("b_sb", [128, MC], F32)
        p.dma("sync", b_sb[:], b, "ld")
    ldv = p.cnt["ld"]
    p.wait("tensor", "ld", ldv)
    p.wait("scalar", "ld", ldv)
    wsb = [p.sbuf(f"w1_{i}", [128, K1 * 128], BF16) for i in range(2)]
    NT = Tn // 512
    for m in range(MC):
        if m >= 2:
            p.wait("gpsimd", "mma", m - 1)
        p.dma("gpsimd", wsb[m % 2][:].rearrange("p (kc mm) -> p kc mm", mm=128),
              w1.rearrange("(kc p) mm -> p kc mm", p=128)[:, :, m * 128:(m + 1) * 128],
              f"wl{m % 2}")
        p.dma("sync", x_sb[:], x[m * 128:(m + 1) * 128, :], "ldx")
        p.dma("sync", f_sb[:], f[m * 128:(m + 1) * 128, :], "ldx")
        p.wait("tensor", f"wl{m % 2}", 16 * (m // 2 + 1))
        if m >= 2:
            p.wait("tensor", "cmb", m - 1)
        pt = ps[m % 2]
        for k in range(K1):
            for n in range(NT):
                sl = slice(n * 512, (n + 1) * 512)
                p.on("tensor", lambda e, pt=pt, m=m, k=k, sl=sl, st=(k == 0), sp=(k == K1 - 1):
                     e.matmul(pt[:, sl], wsb[m % 2][:, k * 128:(k + 1) * 128],
                              chunk(a_sb, k, Tn)[:, sl], start=st, stop=sp),
                     sem="mma" if (k == K1 - 1 and n == NT - 1) else None)
        p.wait("scalar", "mma", p.cnt["mma"])
        if m >= 1:
            p.wait("scalar", "cmb", p.cnt["cmb"])
        kw = {"bias": b_sb[:, m:m + 1]} if b_sb is not None else {}
        p.on("scalar", lambda e, pt=pt, kw=kw: e.activation(
            e1[:], pt[:], AF.Identity if kw else AF.Copy, **kw), sem="ev1")
        p.wait("vector", "ev1", p.cnt["ev1"])
        p.wait("vector", "ldx", 32 * (m + 1))
        if m >= 1:
            p.wait("vector", "sto", 16 * m)
        p.on("vector", lambda e: e.tensor_tensor(o_sb[:], e1[:], x_sb[:], ALU.add), sem="cmb0")
        p.wait("vector", "cmb0", p.cnt["cmb0"])
        p.on("vector", lambda e: e.tensor_tensor(o_sb[:], o_sb[:], f_sb[:], ALU.add), sem="cmb")
        p.wait("sync", "cmb", p.cnt["cmb"])
        p.dma("sync", xo[m * 128:(m + 1) * 128, :], o_sb[:], "sto")
    p.wait("sync", "sto", p.cnt["sto"])
    return p.build()



IOTA8 = np.ascontiguousarray(np.tile(np.arange(8, dtype=np.float32), (128, 1)))
LTRI = np.ascontiguousarray(np.triu(np.ones((128, 128), np.float32)))

def run_layer3(x_fm_shards, inp):
    xn, xn_f = run_ln(x_fm_shards, want_f32=True)
    nc = cached(("l3qkv",), build_projN, D,
                [{"out": "q", "m": D, "bias": True, "featmap": True},
                 {"out": "k", "m": D, "bias": True, "featmap": True},
                 {"out": "v", "m": D, "bias": True}])
    wq = fm(inp["L3_attn_Wq"]).astype(np.float32)
    wk = fm(inp["L3_attn_Wk"]).astype(np.float32)
    wv = fm(inp["L3_attn_Wv"]).astype(np.float32)
    res = run(nc, [{"x": np.asarray(xn[c], dtype=BF), "w0": wq, "w1": wk, "w2": wv,
                    "b0": tile_vec(inp["L3_attn_bq"]), "b1": tile_vec(inp["L3_attn_bk"]),
                    "b2": tile_vec(inp["L3_attn_bv"])} for c in range(NC)])
    qs, ks, vs = [r["q"] for r in res], [r["k"] for r in res], [r["v"] for r in res]
    nc = cached(("linattn",), build_linattn)
    ims = []
    for c in range(NC):
        ktm = np.ascontiguousarray(np.asarray(pair_cat(ks, c)).T)
        vt = make_vt(pair_cat(vs, c).astype(np.float32))
        ims.append({"q": qs[c], "ktm": ktm, "vt": vt, "on2d": ONES_ROW})
    ctxs = [r["ctx"] for r in run(nc, ims)]
    nc = cached(("router",), build_router)
    wr = np.ascontiguousarray(inp["L3_ffn_Wr"], dtype=np.float32)
    res = run(nc, [{"x": np.ascontiguousarray(xn_f[c], dtype=np.float32), "wr": wr,
                    "iot": IOTA8} for c in range(NC)])
    routes = [np.asarray(r["route"]) for r in res]
    oh_all = np.concatenate([r[:, 0:16].reshape(T, 2, 8) for r in routes], axis=0)
    A = oh_all.reshape(2 * NC * T, 8)
    A_t = np.ascontiguousarray(A.reshape(-1, 128, 8).transpose(1, 0, 2).reshape(128, -1))
    nc = cached(("routing",), build_routing)
    res = run(nc, [{"A": A_t, "ltri": LTRI, "onec": ONES_COL, "on2d": ONES_ROW}
                   for _ in range(NC)])
    keep = np.asarray(res[0]["keep"]).T.reshape(-1)
    k1_all = keep[0::2].reshape(NC, T)
    k2_all = keep[1::2].reshape(NC, T)
    nc = cached(("moe",), build_moe)
    wg = np.ascontiguousarray(inp["L3_ffn_Wg"].reshape(8 * D, 2048), dtype=np.float32)
    wu = np.ascontiguousarray(inp["L3_ffn_Wu"].reshape(8 * D, 2048), dtype=np.float32)
    wd = np.ascontiguousarray(inp["L3_ffn_Wd"].reshape(8 * 2048, D), dtype=np.float32)
    nwm = np.asarray(inp["L3_ffn_norm_w"], dtype=np.float32)
    nwt = np.zeros((128, (D // 128) * 8), np.float32)
    for kc in range(D // 128):
        for e in range(8):
            nwt[:, kc * 8 + e] = nwm[e, kc * 128:(kc + 1) * 128]
    nwt = np.ascontiguousarray(nwt)
    fwg = np.ascontiguousarray(inp["L3_ffn_fWg"], dtype=np.float32)
    fwu = np.ascontiguousarray(inp["L3_ffn_fWu"], dtype=np.float32)
    fwd = np.ascontiguousarray(inp["L3_ffn_fWd"], dtype=np.float32)
    fnorm = tile_vec(inp["L3_ffn_f_norm"])
    ims = []
    for c in range(NC):
        rt = routes[c]
        mrow = np.concatenate([rt[:, 16], rt[:, 17], k1_all[c], k2_all[c]]).reshape(1, -1)
        ims.append({"xn": np.asarray(xn[c], dtype=BF),
                    "oh1": np.ascontiguousarray(rt[:, 0:8].T),
                    "oh2": np.ascontiguousarray(rt[:, 8:16].T),
                    "mrow": np.ascontiguousarray(mrow.astype(np.float32)),
                    "wg": wg, "wu": wu, "wd": wd, "nw": nwt, "fnorm": fnorm,
                    "fwg": fwg, "fwu": fwu, "fwd": fwd,
                    "onec": ONES_COL, "on2d": ONES_ROW})
    fos = [r["fo"] for r in run(nc, ims)]
    nc = cached(("combd", D, True), build_combine_direct, D, True)
    wo = fm(inp["L3_attn_Wo"]).astype(np.float32)
    bo = tile_vec(inp["L3_attn_bo"])
    res = run(nc, [{"x": np.ascontiguousarray(x_fm_shards[c], dtype=np.float32),
                    "a": ctxs[c], "f": np.ascontiguousarray(fos[c]),
                    "w1": wo, "b1": bo} for c in range(NC)])
    return [r["xo"] for r in res]



def kernel(**inputs):
    x = np.asarray(inputs["L0_attn_x"], dtype=np.float32)
    shards = [np.ascontiguousarray(fm(s)) for s in shard_tokens(x)]
    shards = run_layer0(shards, inputs)
    shards = [np.ascontiguousarray(np.asarray(s, dtype=np.float32)) for s in shards]
    shards = run_layer1(shards, inputs)
    shards = [np.ascontiguousarray(np.asarray(s, dtype=np.float32)) for s in shards]
    shards = run_layer2(shards, inputs)
    shards = [np.ascontiguousarray(np.asarray(s, dtype=np.float32)) for s in shards]
    shards = run_layer3(shards, inputs)
    out = np.empty((4, 4096, D), dtype=np.float32)
    for c in range(NC):
        out[c // 2, (c % 2) * T:(c % 2 + 1) * T, :] = np.asarray(shards[c], dtype=np.float32).T
    return out
